# version 10
# baseline (speedup 1.0000x reference)
import os
import numpy as np
import ml_dtypes
from contextlib import ExitStack

import concourse.bass as bass
import concourse.mybir as mybir
from concourse.bass_utils import run_bass_kernel_spmd

F32 = mybir.dt.float32
BF16 = mybir.dt.bfloat16
AF = mybir.ActivationFunctionType
ALU = mybir.AluOpType
AX = mybir.AxisListType

B = 4
T = 4096
D = 2048
HK = 16
DK = 128
VE = 256
NV = HK * VE
DH = 8192
C = 128
NCH = T // C
OWN = T // 2
OCH0 = NCH // 2
QT0 = OWN - 128
QTN = T - QT0
NQT = QTN // 512
NEG = -1.0e30

NPHASES = int(os.environ.get("KM_NPH", "6"))
DEBUG_OUT = os.environ.get("KM_DEBUG", "0") == "1"

ENGS = ("tensor", "vector", "scalar", "gpsimd", "sync")


class EM:

    def __init__(self, nc, stack):
        self.nc = nc
        self.stack = stack
        self.ops = {e: [] for e in ENGS}
        self.sems = {}
        self.counts = {}
        self.hwm = {e: {} for e in ENGS}

    def sem(self, name):
        if name not in self.sems:
            self.sems[name] = self.stack.enter_context(self.nc.semaphore(name))
            self.counts[name] = 0
        return self.sems[name]

    def val(self, name):
        return self.counts.get(name, 0)

    def wait(self, eng, name, v=None):
        if v is None:
            v = self.counts.get(name, 0)
        if v <= 0 or self.hwm[eng].get(name, 0) >= v:
            return
        self.hwm[eng][name] = v
        s = self.sem(name)
        self.ops[eng].append(lambda h, s=s, v=v: h.wait_ge(s, v))

    def op(self, eng, builder, sem=None, n=1, waits=()):
        for w in waits:
            if isinstance(w, tuple):
                self.wait(eng, w[0], w[1])
            else:
                self.wait(eng, w)
        if sem is not None:
            s = self.sem(sem)
            self.counts[sem] += n
            def play(h, builder=builder, s=s, n=n):
                builder(h).then_inc(s, n)
        else:
            def play(h, builder=builder):
                builder(h)
        self.ops[eng].append(play)
        return self.counts.get(sem, 0) if sem else None

    def play(self, eng, h):
        for fn in self.ops[eng]:
            fn(h)

    def barrier(self, tag):
        s = self.sem("bar")
        target = self.counts["bar"] + len(ENGS)
        for e in ENGS:
            self.counts["bar"] += 1
            self.ops[e].append(lambda h, s=s: h.sem_inc(s, 1))
        for e in ENGS:
            self.hwm[e]["bar"] = target
            self.ops[e].append(lambda h, s=s, v=target: h.wait_ge(s, v))


class Arena:

    def __init__(self, nc, base, limit):
        self.nc = nc
        self.base = base
        self.limit = limit
        self.cur = base
        self.n = 0

    def alloc(self, name, shape, dtype):
        esz = {F32: 4, BF16: 2, mybir.dt.int32: 4}[dtype]
        nb = int(np.prod(shape[1:])) * esz
        off = (self.cur + 63) & ~63
        assert off + nb <= self.limit, f"arena overflow: {name} {off}+{nb}>{self.limit}"
        self.cur = off + nb
        self.n += 1
        return self.nc.alloc_sbuf_tensor_at(f"{name}_{self.n}", list(shape), dtype, offset=off)

    def reset(self):
        self.cur = self.base


class Bank:

    def __init__(self, h):
        self.h = h
        self.prod = None
        self.cons = None


def dget(dt_np):
    return {np.dtype(np.float32): F32}.get(np.dtype(dt_np), F32)


ARENA_BASE = 44 * 1024
ARENA_LIMIT = 224 * 1024 - 64


def build_program():
    nc = bass.Bass("TRN2", target_bir_lowering=False)
    stk = ExitStack()
    em = EM(nc, stk)

    def inp(name, shape, dt=F32):
        return nc.dram_tensor(name, list(shape), dt, kind="ExternalInput").ap()

    x_kv = inp("x_kv", (T, D))
    x_out = inp("x_out", (OWN, D))
    cm16 = inp("cm16", (HK, T))
    lncm16 = inp("lncm16", (HK, T))
    Wk = inp("Wk", (D, D)); Wv = inp("Wv", (D, NV)); Wq = inp("Wq", (D, D))
    Wab = inp("Wab", (D, 32))
    Wo = inp("Wo", (NV, D))
    qcw = inp("qcw", (D, 4)); qcb = inp("qcb", (D, 1))
    kcw = inp("kcw", (D, 4)); kcb = inp("kcb", (D, 1))
    vcw = inp("vcw", (NV, 4)); vcb = inp("vcb", (NV, 1))
    WgT = inp("WgT", (D, DH)); WuT = inp("WuT", (D, DH)); WdT = inp("WdT", (DH, D))
    cIbf = inp("cIbf", (128, 128), BF16)
    cIf = inp("cIf", (128, 128), F32)
    cMaskW = inp("cMaskW", (128, 128), F32)
    cMaskM = inp("cMaskM", (128, 128), F32)
    cOneRow = inp("cOneRow", (1, 128), F32)
    cOneRowB = inp("cOneRowB", (1, 512), BF16)
    cOneColB = inp("cOneColB", (128, 1), BF16)
    cTinyB = inp("cTinyB", (1, 1), BF16)

    kind_s = "ExternalOutput" if DEBUG_OUT else "Internal"
    def scratch(name, shape, dt):
        return nc.dram_tensor(name, list(shape), dt, kind=kind_s).ap()

    xn1 = scratch("xn1", (T, D), BF16)
    kr = scratch("kr", (D, T), BF16)
    vr = scratch("vr", (NV, T), BF16)
    qr = scratch("qr", (D, QTN), BF16)
    abz = scratch("abz", (32, T), F32)
    wrow_d = scratch("wrow_d", (HK, T), F32)
    carow_d = scratch("carow_d", (HK, T), F32)
    aclast_d = scratch("aclast_d", (HK, NCH), F32)
    tmsl_d = scratch("tmsl_d", (128, NCH * 64), F32)
    outs_d = scratch("outs_d", (128, 32 * OWN), BF16)
    out1_d = scratch("out1_d", (OWN, D), F32)
    xn2_d = scratch("xn2_d", (OWN, D), BF16)
    gated_d = scratch("gated_d", (DH, OWN), BF16)
    out = nc.dram_tensor("out", [OWN, D], F32, kind="ExternalOutput").ap()

    def perm(name, shape, dt):
        return stk.enter_context(nc.sbuf_tensor(name, list(shape), dt))

    sIbf = perm("sIbf", [128, 128], BF16)
    sIf = perm("sIf", [128, 128], F32)
    sMaskW = perm("sMaskW", [128, 128], F32)
    sMaskM = perm("sMaskM", [128, 128], F32)
    sOneRow = perm("sOneRow", [1, 128], F32)
    sOneRowB = perm("sOneRowB", [1, 512], BF16)
    sOneColB = perm("sOneColB", [128, 1], BF16)
    sTinyB = perm("sTinyB", [1, 1], BF16)
    sI2 = perm("sI2", [128, 256], BF16)
    zrow = perm("zrow", [16, 128], F32)
    tmsl = perm("tmsl", [128, NCH * 64], F32)
    colsA = perm("colsA", [128, 16], F32)
    colsB = perm("colsB", [128, 16], F32)
    epscol = perm("epscol", [128, 1], F32)

    arena = Arena(nc, ARENA_BASE, ARENA_LIMIT)

    pb = [stk.enter_context(nc.psum_tensor(f"pb{i}", [128, 512], F32)) for i in range(5)]
    pbE = stk.enter_context(nc.psum_tensor("pbE", [128, 1024], F32))
    pT = stk.enter_context(nc.psum_tensor("pT", [128, 1024], BF16))

    def DMAS(out_ap, in_ap, waits=()):
        return em.op("sync", lambda h, o=out_ap, i=in_ap: h.dma_start(out=o, in_=i),
                     sem="dls", n=16, waits=waits)

    def DMAT(out_ap, in_ap, waits=()):
        return em.op("sync", lambda h, o=out_ap, i=in_ap: h.dma_start_transpose(o, i),
                     sem="dls", n=16, waits=waits)

    def DMAG(out_ap, in_ap, waits=()):
        return em.op("gpsimd", lambda h, o=out_ap, i=in_ap: h.dma_start(out=o, in_=i),
                     sem="dlg", n=16, waits=waits)

    def DMAA(out_ap, in_ap, waits=()):
        return em.op("scalar", lambda h, o=out_ap, i=in_ap: h.dma_start(out=o, in_=i),
                     sem="dla", n=16, waits=waits)

    DMAS(sIbf[:], cIbf); DMAS(sIf[:], cIf)
    DMAS(sMaskW[:], cMaskW); DMAS(sMaskM[:], cMaskM)
    DMAS(sOneRow[:], cOneRow); DMAS(sOneRowB[:], cOneRowB)
    DMAS(sOneColB[:], cOneColB); DMAS(sTinyB[:], cTinyB)
    DMAS(sI2[:, 0:128], cIbf); DMAS(sI2[:, 128:256], cIbf)
    em.op("vector", lambda h: h.memset(zrow[:], 0.0), sem="dv")
    em.op("vector", lambda h: h.memset(epscol[:], 1e-5), sem="dv")
    c_init = em.val("dls")
    em.wait("sync", "dls", c_init)
    em.barrier("init")

    xt = [arena.alloc("xt", [128, D], F32) for _ in range(2)]
    sqs = [arena.alloc("sqs", [128, D], F32) for _ in range(2)]
    xnb = [arena.alloc("xnb", [128, D], BF16) for _ in range(2)]
    lastc = {}
    for i in range(T // 128):
        p = i % 2
        cols = colsA if p == 0 else colsB
        csum, cssq, cngm, cssd, cnvr, clnv, crsd = (cols[:, j:j+1] for j in range(7))
        w = lastc.get(("xt", p), [])
        ld = DMAS(xt[p][:], x_kv[i*128:(i+1)*128, :], waits=w)
        em.op("scalar", lambda h, o=sqs[p][:], i_=xt[p][:], a=csum:
              h.activation(o, i_, AF.Copy, accum_out=a), sem="ac",
              waits=[("dls", ld)] + lastc.get(("cols", p), []))
        em.op("scalar", lambda h, o=sqs[p][:], i_=xt[p][:], a=cssq:
              h.activation(o, i_, AF.Square, accum_out=a), sem="ac")
        a_sq = em.val("ac")
        em.op("vector", lambda h, o=cngm, i_=csum: h.tensor_scalar_mul(o, i_, -1.0 / D),
              sem="dv", waits=[("ac", a_sq)])
        em.op("vector", lambda h, o=cssd, i_=cssq: h.tensor_scalar_mul(o, i_, 1.0 / D), sem="dv")
        em.op("vector", lambda h, o=cnvr, i0=cngm, s=cngm, i1=cssd:
              h.scalar_tensor_tensor(o, i0, s, i1, ALU.mult, ALU.subtract), sem="dv")
        d_var = em.val("dv")
        em.op("scalar", lambda h, o=clnv, i_=cnvr: h.activation(o, i_, AF.Ln, scale=-1.0, bias=epscol[:]),
              sem="ac", waits=[("dv", d_var)])
        em.op("scalar", lambda h, o=crsd, i_=clnv: h.activation(o, i_, AF.Exp, scale=-0.5), sem="ac")
        a_rs = em.val("ac")
        em.op("vector", lambda h, o=xnb[p][:], i_=xt[p][:], s1=cngm, s2=crsd:
              h.tensor_scalar(o, i_, s1, s2, ALU.add, ALU.mult), sem="dv",
              waits=[("ac", a_rs)] + lastc.get(("xnb", p), []))
        d_app = em.val("dv")
        st = DMAS(xn1[i*128:(i+1)*128, :], xnb[p][:], waits=[("dv", d_app)])
        lastc[("xt", p)] = [("ac", a_sq), ("dv", d_app)]
        lastc[("cols", p)] = [("dv", d_app)]
        lastc[("xnb", p)] = [("dls", st)]
        if i == T // 128 // 2 - 1:
            ln1_half = st
    ln1_full = em.val("dls")
    arena.reset()

    if NPHASES >= 2:
        xng = arena.alloc("xng", [128, 16 * QTN], BF16)
        wbuf = [arena.alloc("wbuf", [128, 16 * 1024], BF16) for _ in range(2)]
        stg = [arena.alloc("stg", [128, QTN], BF16) for _ in range(2)]
        stgab = [arena.alloc("stgab", [32, 2048], F32) for _ in range(2)]
        wbuf_war = [[], []]
        stg_war = [[], []]
        stgab_war = [[], []]
        bank_war = [[] for _ in range(7)]
        bkctr = [0]

        xng_war = [[]]

        tl_mem = [0]

        def proj_pass(tg, t0, tw, segs, skip_load=False):
            if skip_load:
                tl = tl_mem[0]
            else:
                lnw = [("dls", ln1_half if (tg == 0) else ln1_full)]
                tl = 0
                for dc in range(16):
                    tl = DMAT(xng[:, dc * tw:(dc + 1) * tw],
                              xn1[t0:t0 + tw, dc * 128:(dc + 1) * 128],
                              waits=xng_war[0] + lnw)
                xng_war[0] = []
                tl_mem[0] = tl
            ntsub_full, rem = divmod(tw, 512)
            tsubs = [(i * 512, 512) for i in range(ntsub_full)]
            if rem:
                tsubs.append((ntsub_full * 512, rem))
            for (W, OFW, slab, slabw, f32out) in segs:
                for of0 in range(0, OFW, 1024):
                    sw = min(1024, OFW - of0)
                    wp = (of0 // 1024) % 2
                    wl = 0
                    for kc in range(16):
                        wl = DMAG(wbuf[wp][:, kc * sw:(kc + 1) * sw],
                                  W[kc * 128:(kc + 1) * 128, of0:of0 + sw],
                                  waits=wbuf_war[wp])
                    wbuf_war[wp] = []
                    for of8 in range(max(1, sw // 128)):
                        m = min(128, sw)
                        ofc = of0 + of8 * 128
                        sp = (ofc // 128) % 2
                        stgw = stgab_war[sp] if f32out else stg_war[sp]
                        sbuf_stg = stgab[sp] if f32out else stg[sp]
                        for (ts0, tsw) in tsubs:
                            bk = bkctr[0] % 4
                            bkctr[0] += 1
                            pbk = pb[bk]
                            for kc in range(16):
                                wts = []
                                if kc == 0:
                                    wts = [("dlg", wl), ("dls", tl)] + bank_war[bk]
                                    bank_war[bk] = []
                                em.op("tensor",
                                      lambda h, o=pbk[0:m, 0:tsw],
                                      l=wbuf[wp][:, kc * sw + of8 * 128: kc * sw + of8 * 128 + m],
                                      r=xng[:, kc * tw + ts0: kc * tw + ts0 + tsw],
                                      s=(kc == 0), e=(kc == 15):
                                      h.matmul(o, l, r, start=s, stop=e),
                                      sem="pe" if kc == 15 else None, waits=wts)
                            mmv = em.val("pe")
                            if bkctr[0] % 2 == 0:
                                ev = em.op("vector",
                                           lambda h, o=sbuf_stg[0:m, ts0:ts0 + tsw], i_=pbk[0:m, 0:tsw]:
                                           h.tensor_copy(o, i_), sem="dv",
                                           waits=[("pe", mmv)] + stgw)
                                bank_war[bk] = [("dv", ev)]
                            else:
                                ev = em.op("scalar",
                                           lambda h, o=sbuf_stg[0:m, ts0:ts0 + tsw], i_=pbk[0:m, 0:tsw]:
                                           h.activation(o, i_, AF.Copy), sem="ac",
                                           waits=[("pe", mmv)] + stgw)
                                bank_war[bk] = [("ac", ev)]
                            stgw.clear()
                        ofr = (ofc // 128) if not f32out else 0
                        dst = slab[ofc:ofc + m, slabw[0]:slabw[0] + tw] if f32out else \
                            slab[ofc:ofc + m, slabw[0]:slabw[0] + tw]
                        wv = [("dv", em.val("dv")), ("ac", em.val("ac"))]
                        st = DMAS(dst, sbuf_stg[0:m, 0:tw], waits=wv)
                        if f32out:
                            stgab_war[sp] = [("dls", st)]
                        else:
                            stg_war[sp] = [("dls", st)]
                    wbuf_war[wp] = [("pe", em.val("pe"))]
            xng_war[0] = [("pe", em.val("pe"))]

        for tg in range(2):
            t0 = tg * 2048
            proj_pass(tg, t0, 2048, [
                (Wab, 32, abz, (t0,), True),
                (Wk, 2048, kr, (t0,), False),
                (Wv, NV, vr, (t0,), False),
            ])
        proj_pass(2, QT0, QTN, [(Wq, 2048, qr, (0,), False)])
        em.wait("sync", "dls", em.val("dls"))
        em.barrier("proj")
        arena.reset()

    if NPHASES >= 3:
        Ab = arena.alloc("sc_a", [16, T], F32)
        Bb = arena.alloc("sc_b", [16, T], F32)
        Sb = arena.alloc("sc_s", [16, T], F32)
        Lb = arena.alloc("sc_l", [16, T], F32)
        CMb = arena.alloc("sc_cm", [16, T], F32)
        LCb = arena.alloc("sc_lc", [16, T], F32)
        sacl = arena.alloc("sc_acl", [16, NCH], F32)

        l1 = DMAS(Ab[:], abz[0:16, :])
        l2 = DMAS(Bb[:], abz[16:32, :])
        l3 = DMAS(CMb[:], cm16)
        l4 = DMAS(LCb[:], lncm16)

        def ACTOP(dst, src, fn, sc=1.0, bi=0.0, waits=(), accum=None):
            return em.op("scalar", lambda h, o=dst, i_=src, f=fn, s=sc, b=bi, a=accum:
                         h.activation(o, i_, f, scale=s, bias=b, accum_out=a),
                         sem="ac", waits=waits)

        def logsig(dst, z, wz):
            ACTOP(Sb[:], z[:], AF.Abs, waits=wz)
            ACTOP(Sb[:], Sb[:], AF.Exp, sc=-1.0)
            a1 = ACTOP(Sb[:], Sb[:], AF.Ln, bi=1.0)
            em.op("vector", lambda h, o=dst[:], i_=z[:]: h.tensor_scalar_min(o, i_, 0.0),
                  sem="dv", waits=[("ac", a1)])
            return em.op("vector", lambda h, o=dst[:], a=dst[:], b=Sb[:]:
                         h.tensor_sub(o, a, b), sem="dv")

        logsig(Lb, Ab, [("dls", l1)])
        d_la = em.val("dv")
        logsig(Ab, Bb, [("dls", l2), ("dv", d_la)])
        a_sg = ACTOP(Sb[:], Bb[:], AF.Sigmoid, waits=[("dv", em.val("dv"))])
        em.op("vector", lambda h: h.tensor_mul(Bb[:], Sb[:], CMb[:]), sem="dv",
              waits=[("ac", a_sg), ("dls", l3)])
        em.op("vector", lambda h: h.tensor_add(Ab[:], Ab[:], LCb[:]), sem="dv",
              waits=[("dls", l4)])
        for j in range(NCH):
            em.op("vector", lambda h, j=j: h.tensor_tensor_scan(
                CMb[:, j * C:(j + 1) * C], Lb[:, j * C:(j + 1) * C], zrow[:],
                0.0, ALU.add, ALU.add), sem="dv")
        em.op("vector", lambda h: h.tensor_scalar_mul(LCb[:], CMb[:], -1.0), sem="dv")
        d_ca = em.val("dv")
        em.op("vector", lambda h: h.tensor_add(Sb[:], CMb[:], Ab[:]), sem="dv")
        d_wr = em.val("dv")
        a_naw = ACTOP(Ab[:], Sb[:], AF.Exp, waits=[("dv", d_wr)])
        em.op("vector", lambda h: h.tensor_scalar_mul(Ab[:], Ab[:], -1.0), sem="dv",
              waits=[("ac", a_naw)])
        for j in range(NCH):
            em.op("vector", lambda h, j=j: h.tensor_single_scalar(
                Lb[:, j * C:(j + 1) * C], CMb[:, j * C:(j + 1) * C],
                CMb[:, j * C + C - 1:j * C + C], ALU.subtract),
                sem="dv", waits=[("dv", d_ca)])
        d_rp = em.val("dv")
        a_rac = ACTOP(Lb[:], Lb[:], AF.Exp, sc=-1.0, waits=[("dv", d_rp)])
        a_acl = ACTOP(sacl[:], CMb[:, C - 1::C], AF.Exp, waits=[("dv", d_ca)])
        st1 = DMAS(wrow_d, Sb[:], waits=[("dv", em.val("dv"))])
        st2 = DMAS(carow_d, CMb[:])
        st3 = DMAS(aclast_d, sacl[:], waits=[("ac", a_acl)])
        d_all = em.val("dv"); a_all = em.val("ac")
        tm_war = []
        for j in range(NCH):
            srcs = [LCb, Bb, Ab, Lb]
            for r in range(4):
                w = ([("dv", d_all), ("ac", a_all)] + tm_war) if r == 0 else ()
                em.op("tensor", lambda h, s=srcs[r], j=j, r=r:
                      h.transpose(pb[4][0:128, r * 16:(r + 1) * 16],
                                  s[:, j * C:(j + 1) * C], sIf[0:16, 0:16]),
                      sem="pe" if r == 3 else None, waits=w)
            pev = em.val("pe")
            ev = em.op("vector", lambda h, j=j: h.tensor_copy(
                tmsl[:, j * 64:(j + 1) * 64], pb[4][0:128, 0:64]),
                sem="dv", waits=[("pe", pev)])
            tm_war = [("dv", ev)]
        if DEBUG_OUT:
            DMAS(tmsl_d, tmsl[:], waits=[("dv", em.val("dv"))])
        em.wait("sync", "dls", em.val("dls"))
        em.barrier("scal")
        arena.reset()

    ctx = dict(locals())
    if NPHASES >= 4:
        emit_mix(ctx)
    if NPHASES >= 5:
        emit_wo(ctx)
    if NPHASES >= 6:
        emit_ffn(ctx)

    for s in ("dls", "dlg", "dla"):
        em.wait("sync", s, em.val(s))
    em.barrier("fin")

    blk = stk.enter_context(nc.Block())
    blk.tensor(lambda h: em.play("tensor", h))
    blk.vector(lambda h: em.play("vector", h))
    blk.scalar(lambda h: em.play("scalar", h))
    blk.gpsimd(lambda h: em.play("gpsimd", h))
    blk.sync(lambda h: em.play("sync", h))
    stk.close()
    return nc


def emit_mix(ctx):
    em = ctx["em"]; arena = ctx["arena"]
    pb = ctx["pb"]; pbE = ctx["pbE"]; pT = ctx["pT"]
    kr, vr, qr = ctx["kr"], ctx["vr"], ctx["qr"]
    wrow_d, carow_d, aclast_d = ctx["wrow_d"], ctx["carow_d"], ctx["aclast_d"]
    outs_d = ctx["outs_d"]; tmsl = ctx["tmsl"]
    sIbf, sIf = ctx["sIbf"], ctx["sIf"]
    sMaskW, sMaskM = ctx["sMaskW"], ctx["sMaskM"]
    sOneRow, sOneRowB, sTinyB = ctx["sOneRow"], ctx["sOneRowB"], ctx["sTinyB"]
    sOneColB = ctx["sOneColB"]; sI2 = ctx["sI2"]
    qcw, qcb, kcw, kcb, vcw, vcb = (ctx[k] for k in ("qcw", "qcb", "kcw", "kcb", "vcw", "vcb"))
    DMAS, DMAG = ctx["DMAS"], ctx["DMAG"]
    PAD = 4

    slabK = arena.alloc("mxK", [128, PAD + T], BF16)
    slabV0 = arena.alloc("mxV0", [128, PAD + T], BF16)
    slabV1 = arena.alloc("mxV1", [128, PAD + T], BF16)
    slabQ = arena.alloc("mxQ", [128, PAD + QTN], BF16)
    q2 = arena.alloc("mxQ2", [128, QTN], BF16)
    sqt = arena.alloc("mxSq", [128, 512], BF16)
    krnb = arena.alloc("mxKrn", [1, T], BF16)
    wrowh = arena.alloc("mxWrow", [1, T], F32)
    mrow = arena.alloc("mxMrow", [1, QTN], F32)
    lnscr = arena.alloc("mxLns", [1, 1024], F32)
    amq = arena.alloc("mxAmq", [1, QTN], BF16)
    aclr = arena.alloc("mxAcl", [1, NCH], F32)
    ACrep = arena.alloc("mxACr", [128, NCH], F32)
    cwq = arena.alloc("mxCwq", [128, 4], F32); cbq = arena.alloc("mxCbq", [128, 1], F32)
    cwk = arena.alloc("mxCwk", [128, 4], F32); cbk = arena.alloc("mxCbk", [128, 1], F32)
    cwv0 = arena.alloc("mxCv0", [128, 4], F32); cbv0 = arena.alloc("mxCb0", [128, 1], F32)
    cwv1 = arena.alloc("mxCv1", [128, 4], F32); cbv1 = arena.alloc("mxCb1", [128, 1], F32)
    dg = [arena.alloc(f"mxDg{i}", [128, 128], BF16) for i in range(16)]
    GLAG = 3
    WtMt2 = [arena.alloc(f"mxWtMt{i}", [128, 512], BF16) for i in range(GLAG)]
    Tnv2 = [arena.alloc(f"mxTnv{i}", [128, 256], BF16) for i in range(GLAG)]
    bV2 = [arena.alloc(f"mxbV{i}", [128, 512], BF16) for i in range(GLAG)]
    Ktm2 = [arena.alloc(f"mxKtm{i}", [128, 256], BF16) for i in range(GLAG)]
    W02g = [arena.alloc(f"mxW02{i}", [128, 256], BF16) for i in range(2)]
    PQ2g = [[arena.alloc(f"mxPQ{i}_{p}", [128, 512], BF16) for p in range(2)] for i in range(2)]
    P62g = [arena.alloc(f"mxP62{i}", [128, 256], BF16) for i in range(2)]
    PT2g = [[arena.alloc(f"mxPT{i}_{p}", [128, 256], BF16) for p in range(2)] for i in range(2)]
    LWM2 = arena.alloc("mxLWM", [128, 512], F32)
    AWK2 = [arena.alloc(f"mxAWK{i}", [128, 256], BF16) for i in range(GLAG)]
    X1b2 = [arena.alloc(f"mxX1{i}", [128, 256], BF16) for i in range(GLAG)]
    X1T2 = [arena.alloc(f"mxXT{i}", [128, 256], BF16) for i in range(GLAG)]
    Uloc2 = [arena.alloc(f"mxUL{i}", [128, 512], BF16) for i in range(GLAG)]
    B2 = [arena.alloc(f"mxB{i}", [128, 512], BF16) for i in range(GLAG)]
    MT2 = [arena.alloc(f"mxMT{i}", [128, 256], BF16) for i in range(GLAG)]
    Ut = arena.alloc("mxU", [128, 256], BF16)
    St = [arena.alloc(f"mxS{i}", [128, 256], BF16) for i in range(2)]
    outs_s = arena.alloc("mxOuts", [128, 2 * OWN], BF16)

    def MM(bank_ap, lhsT, rhs, st, sp, sem="pe", waits=()):
        return em.op("tensor", lambda h, o=bank_ap, l=lhsT, r=rhs, s=st, e=sp:
                     h.matmul(o, l, r, start=s, stop=e), sem=sem, waits=waits)

    def TRANS(bank_ap, in_ap, ident, waits=()):
        return em.op("tensor", lambda h, o=bank_ap, i_=in_ap, d=ident:
                     h.transpose(o, i_, d), sem="pe", waits=waits)

    def ACOPY(dst, src, waits=()):
        return em.op("scalar", lambda h, o=dst, i_=src:
                     h.activation(o, i_, AF.Copy), sem="ac", waits=waits)

    def AEXP(dst, src, bias, waits=(), scale=1.0):
        return em.op("scalar", lambda h, o=dst, i_=src, b=bias, s=scale:
                     h.activation(o, i_, AF.Exp, bias=b, scale=s), sem="ac", waits=waits)

    prev_stores = []
    slab_war = []
    for hd in range(HK):
        lw = list(slab_war)
        lK = DMAS(slabK[:, PAD:], kr[hd * 128:(hd + 1) * 128, :], waits=lw)
        lV0 = DMAS(slabV0[:, PAD:], vr[hd * 256:hd * 256 + 128, :], waits=lw)
        lV1 = DMAS(slabV1[:, PAD:], vr[hd * 256 + 128:hd * 256 + 256, :], waits=lw)
        lQ = DMAS(slabQ[:, PAD:], qr[hd * 128:(hd + 1) * 128, :], waits=lw)
        lrw = DMAS(wrowh[:], wrow_d[hd:hd + 1, :], waits=lw)
        lmr = DMAG(mrow[:], carow_d[hd:hd + 1, QT0:], waits=lw)
        lacl = DMAG(aclr[:], aclast_d[hd:hd + 1, :])
        lcw = DMAG(cwq[:], qcw[hd * 128:(hd + 1) * 128, :])
        DMAG(cbq[:], qcb[hd * 128:(hd + 1) * 128, :])
        DMAG(cwk[:], kcw[hd * 128:(hd + 1) * 128, :])
        DMAG(cbk[:], kcb[hd * 128:(hd + 1) * 128, :])
        DMAG(cwv0[:], vcw[hd * 256:hd * 256 + 128, :])
        DMAG(cbv0[:], vcb[hd * 256:hd * 256 + 128, :])
        DMAG(cwv1[:], vcw[hd * 256 + 128:hd * 256 + 256, :])
        lcb = DMAG(cbv1[:], vcb[hd * 256 + 128:hd * 256 + 256, :])
        z1 = em.op("vector", lambda h: h.memset(slabK[:, 0:PAD], 0.0), sem="dv",
                   waits=[("dls", lK)])
        em.op("vector", lambda h: h.memset(slabV0[:, 0:PAD], 0.0), sem="dv")
        em.op("vector", lambda h: h.memset(slabV1[:, 0:PAD], 0.0), sem="dv")
        em.op("vector", lambda h: h.memset(slabQ[:, 0:PAD], 0.0), sem="dv")

        slabs = [(slabK, T, cwk, cbk, 0), (slabV0, T, cwv0, cbv0, 1),
                 (slabV1, T, cwv1, cbv1, 2), (slabQ, QTN, cwq, cbq, 3)]
        for si, (slab, ncols, cw, cb, di) in enumerate(slabs):
            for j in range(4):
                em.op("vector", lambda h, j=j, c=cw, di=di: h.tensor_scalar_mul(
                    dg[di * 4 + j][:], sIbf[:], c[:, j:j + 1]), sem="dv",
                    waits=[("dlg", lcb), ("pe", em.val("pe"))])
        dtap = em.val("dv")
        a_prep0 = em.val("ac")
        steps = []
        for si, (slab, ncols, cw, cb, di) in enumerate(slabs):
            tiles = []
            t0v = 0
            while t0v < ncols:
                tw = min(512, ncols - t0v)
                tiles.append((si, t0v, tw))
                t0v += tw
            steps.append(list(reversed(tiles)))
        order = []
        mx = max(len(s) for s in steps)
        for i in range(mx):
            for s in steps:
                if i < len(s):
                    order.append(s[i])
        silu_g = {}
        for n_, (si, t0v, tw) in enumerate(order):
            slab, ncols, cw, cb, di = slabs[si]
            bki = n_ % 4
            bk = pb[bki]
            for j in range(4):
                MM(bk[0:128, 0:tw], dg[di * 4 + j][:],
                   slab[:, PAD + t0v - 3 + j:PAD + t0v - 3 + j + tw],
                   j == 0, j == 3,
                   sem="pe" if j == 3 else None,
                   waits=[("dv", dtap), ("ac", max(a_prep0, silu_g.get(bki, 0)))] if j == 0 else ())
            pev = em.val("pe")
            silu_g[bki] = em.op("scalar", lambda h, o=slab[:, PAD + t0v:PAD + t0v + tw],
                                i_=bk[0:128, 0:tw], b=cb:
                                h.activation(o, i_, AF.Silu, bias=b[:]), sem="ac",
                                waits=[("pe", pev), ("dlg", lcb)])
        a_conv_q = em.val("ac")

        for rnd in range(4):
            for half in range(2):
                ti = rnd * 2 + half
                a_sq = em.op("scalar", lambda h, o=sqt[:], i_=slabK[:, PAD + ti * 512:PAD + (ti + 1) * 512]:
                             h.activation(o, i_, AF.Square), sem="ac",
                             waits=[("dv", em.val("dv")), ("pe", em.val("pe"))])
                MM(pbE[0:1, half * 512:half * 512 + 512], sOneColB[:], sqt[:], True, False,
                   sem=None, waits=[("ac", a_sq)])
                MM(pbE[0:1, half * 512:half * 512 + 512], sTinyB[:], sOneRowB[:], False, True,
                   sem="pe")
            pev = em.val("pe")
            a_ln = em.op("scalar", lambda h, o=lnscr[:], i_=pbE[0:1, 0:1024]:
                         h.activation(o, i_, AF.Ln), sem="ac", waits=[("pe", pev)])
            em.op("scalar", lambda h, o=krnb[0:1, rnd * 1024:(rnd + 1) * 1024], i_=lnscr[:]:
                  h.activation(o, i_, AF.Exp, scale=-0.5), sem="ac")
        a_krn = em.val("ac")
        for ti in range(8):
            mmv = MM(pb[2][0:128, 0:512], sOneRowB[0:1, 0:128],
                     krnb[0:1, ti * 512:(ti + 1) * 512], True, True,
                     waits=[("ac", a_krn), ("dv", em.val("dv"))])
            em.op("vector", lambda h, ti=ti, o=slabK[:, PAD + ti * 512:PAD + (ti + 1) * 512],
                  a=slabK[:, PAD + ti * 512:PAD + (ti + 1) * 512], b=pb[2][0:128, 0:512]:
                  h.tensor_mul(o, a, b), sem="dv", waits=[("pe", mmv)])
        d_khat = em.val("dv")

        qtiles = [(0, 512), (512, 512), (1024, 512), (1536, 512), (2048, 128)]
        for rnd, grp in enumerate([qtiles[0:2], qtiles[2:4], qtiles[4:5]]):
            for gi, (t0v, tw) in enumerate(grp):
                a_sq = em.op("scalar", lambda h, o=sqt[0:128, 0:tw],
                             i_=slabQ[:, PAD + t0v:PAD + t0v + tw], tw=tw:
                             h.activation(o, i_, AF.Square), sem="ac",
                             waits=[("dv", em.val("dv")), ("pe", em.val("pe"))])
                MM(pbE[0:1, gi * 512:gi * 512 + tw], sOneColB[:], sqt[0:128, 0:tw], True, False,
                   sem=None, waits=[("ac", a_sq)])
                MM(pbE[0:1, gi * 512:gi * 512 + tw], sTinyB[:], sOneRowB[0:1, 0:tw], False, True,
                   sem="pe")
            gw = sum(tw for _, tw in grp)
            g0 = grp[0][0]
            pev = em.val("pe")
            a_ln = em.op("scalar", lambda h, o=lnscr[0:1, 0:gw], i_=pbE[0:1, 0:gw], gw=gw:
                         h.activation(o, i_, AF.Ln), sem="ac", waits=[("pe", pev)])
            em.op("vector", lambda h, o=mrow[0:1, g0:g0 + gw], i0=lnscr[0:1, 0:gw],
                  i1=mrow[0:1, g0:g0 + gw], gw=gw:
                  h.scalar_tensor_tensor(o, i0, -0.5, i1, ALU.mult, ALU.add),
                  sem="dv", waits=[("ac", a_ln), ("dlg", lmr)])
        d_mrow = em.val("dv")
        a_amq = em.op("scalar", lambda h: h.activation(amq[:], mrow[:], AF.Exp),
                      sem="ac", waits=[("dv", d_mrow)])
        for (t0v, tw) in qtiles:
            mmv = MM(pb[2][0:128, 0:tw], sOneRowB[0:1, 0:128], amq[0:1, t0v:t0v + tw],
                     True, True, waits=[("ac", a_amq), ("dv", em.val("dv"))])
            em.op("vector", lambda h, t0v=t0v, tw=tw, o=q2[:, t0v:t0v + tw],
                  a=slabQ[:, PAD + t0v:PAD + t0v + tw], b=pb[2][0:128, 0:tw]:
                  h.tensor_mul(o, a, b), sem="dv", waits=[("pe", mmv)])
        mmv = MM(pb[2][0:128, 0:NCH], sOneRow[:], aclr[:], True, True,
                 waits=[("dlg", lacl), ("dv", em.val("dv"))])
        d_acr = em.op("vector", lambda h: h.tensor_copy(ACrep[:], pb[2][0:128, 0:NCH]),
                      sem="dv", waits=[("pe", mmv)])
        em.op("vector", lambda h: h.memset(St[0][:], 0.0), sem="dv",
              waits=prev_stores)
        em.op("vector", lambda h: h.memset(outs_s[:], 0.0), sem="dv")
        d_prep = em.val("dv")

        a_prep = em.val("ac")
        hs = {
            "d_chain_prev": d_prep, "g_pT": [("ac", a_prep)],
            "g_p0": [("ac", a_prep)], "g_p1": [("ac", a_prep)],
            "g_S0": [("ac", a_prep)], "g_S1": [("ac", a_prep)],
            "g_U0": [("ac", a_prep)], "g_U1": [("ac", a_prep)],
            "d_lwm_war": d_prep,
            "sq_bank_g": {2: [("dv", d_prep)], 3: [("dv", d_prep)], 4: [("dv", d_prep)]},
        }
        mstate = {}

        def col(j, r):
            return tmsl[:, j * 64 + 16 * r + hd:j * 64 + 16 * r + hd + 1]

        def mach_s0(g):
            gi = g % GLAG
            jA, jB = 2 * g, 2 * g + 1
            wv = [("dv", d_khat), ("ac", a_conv_q), ("dv", em.val("dv"))] + hs["g_pT"]
            tkv = 0
            for (j, base) in ((jA, 0), (jB, 512)):
                jt = PAD + j * 128
                TRANS(pT[:, base + 128:base + 256], slabV0[:, jt:jt + 128], sIbf[:],
                      waits=wv); wv = ()
                TRANS(pT[:, base + 256:base + 384], slabV1[:, jt:jt + 128], sIbf[:])
                tkv = TRANS(pT[:, base + 384:base + 512], slabK[:, jt:jt + 128], sIbf[:])
            d_bv = d_ktm = 0
            for (j, base, cp) in ((jA, 0, 0), (jB, 512, 1)):
                d_bv = em.op("vector", lambda h, b=col(j, 1), base=base, cp=cp, gi=gi:
                             h.tensor_scalar_mul(bV2[gi][:, cp * 256:cp * 256 + 256],
                                                 pT[:, base + 128:base + 384], b),
                             sem="dv", waits=[("pe", tkv)])
                em.op("vector", lambda h, r=col(j, 3), base=base, cp=cp, gi=gi:
                      h.tensor_scalar_mul(Ktm2[gi][:, cp * 128:cp * 128 + 128],
                                          pT[:, base + 384:base + 512], r), sem="dv")
                d_ktm = em.op("vector", lambda h, n=col(j, 2), base=base, cp=cp, gi=gi:
                              h.tensor_scalar_mul(AWK2[gi][:, cp * 128:cp * 128 + 128],
                                                  pT[:, base + 384:base + 512], n), sem="dv")
            mstate[g] = dict(d_bv=d_bv, d_ktm=d_ktm)

        def mach_s1(g):
            gi = g % GLAG
            jA, jB = 2 * g, 2 * g + 1
            is_out = jA >= OCH0
            kk0 = 0
            for (j, base) in ((jA, 0), (jB, 256)):
                jt = PAD + j * 128
                w = ([("dv", em.val("dv"))] + hs["g_p0"]) if base == 0 else ()
                kk0 = MM(pb[0][0:128, base:base + 128], slabK[:, jt:jt + 128],
                         slabK[:, jt:jt + 128], True, True, waits=w)
                if is_out:
                    qloc = j * 128 - QT0
                    kk0 = MM(pb[0][0:128, base + 128:base + 256], slabK[:, jt:jt + 128],
                             slabQ[:, PAD + qloc:PAD + qloc + 128], True, True)
            ew = 0
            for (j, base) in ((jA, 0), (jB, 256)):
                w = hs["g_p1"] if base == 0 else ()
                MM(pb[1][0:128, base:base + 128], sMaskW[:], sIf[:], True, False,
                   sem=None, waits=w)
                ew = MM(pb[1][0:128, base:base + 128], sOneRow[:],
                        wrowh[0:1, j * 128:(j + 1) * 128], False, True,
                        waits=[("dls", lrw)])
                if is_out:
                    qloc = j * 128 - QT0
                    MM(pb[1][0:128, base + 128:base + 256], sMaskM[:], sIf[:], True, False,
                       sem=None)
                    ew = MM(pb[1][0:128, base + 128:base + 256], sOneRow[:],
                            mrow[0:1, qloc:qloc + 128], False, True)
            wid = 256 if is_out else 128
            a_lwm = 0
            for (j, base) in ((jA, 0), (jB, 256)):
                a_lwm = AEXP(LWM2[0:128, base:base + wid], pb[1][0:128, base:base + wid],
                             col(j, 0), waits=[("pe", ew), ("dv", hs["d_lwm_war"])])
            hs["g_p1"] = [("ac", a_lwm)]
            if is_out:
                d_wtmt = em.op("vector", lambda h, gi=gi: h.tensor_mul(
                    WtMt2[gi][:], LWM2[:], pb[0][0:128, 0:512]),
                    sem="dv", waits=[("ac", a_lwm), ("pe", kk0)])
            else:
                d_wtmt = em.op("vector", lambda h, gi=gi: h.tensor_mul(
                    WtMt2[gi].rearrange("p (c x) -> p c x", x=128)[:, 0::2, :],
                    LWM2.rearrange("p (c x) -> p c x", x=128)[:, 0::2, :],
                    pb[0].rearrange("p (c x) -> p c x", x=128)[:, 0::2, :]),
                    sem="dv", waits=[("ac", a_lwm), ("pe", kk0)])
            hs["d_lwm_war"] = d_wtmt
            hs["g_p0"] = [("dv", d_wtmt)]
            mstate[g]["d_wtmt"] = d_wtmt

        def mach_s2(g):
            gi = g % GLAG
            ms = mstate[g]
            W02 = W02g[g % 2]
            PT2 = PT2g[g % 2]
            d_wtmt, d_ktm = ms["d_wtmt"], ms["d_ktm"]
            tw0 = 0
            for (cp, base) in ((0, 0), (1, 512)):
                tw0 = TRANS(pT[:, base:base + 128], WtMt2[gi][:, cp * 256:cp * 256 + 128],
                            sIbf[:], waits=[("dv", d_wtmt), ("dv", d_ktm)] if cp == 0 else ())
            a_w0 = em.op("scalar", lambda h, W02=W02: h.activation(
                W02.rearrange("p (c x) -> p c x", x=128)[:, :, :],
                pT.rearrange("p (c x) -> p c x", x=128)[:, 0::4, :], AF.Copy),
                sem="ac", waits=[("pe", tw0)])
            hs["g_pT"] = [("ac", a_w0)]
            d_pt0 = em.op("vector", lambda h, gi=gi, PT2=PT2: h.tensor_sub(
                PT2[0].rearrange("p (c x) -> p c x", x=128)[:, :, :],
                sI2.rearrange("p (c x) -> p c x", x=128)[:, :, :],
                WtMt2[gi].rearrange("p (c x) -> p c x", x=128)[:, 0::2, :]),
                sem="dv", waits=[("dv", d_wtmt)])
            ms.update(PA=(W02[:, 0:128], WtMt2[gi][:, 0:128]),
                      PB=(W02[:, 128:256], WtMt2[gi][:, 256:384]),
                      sq_ev=[("ac", a_w0), ("dv", d_wtmt)],
                      PTcur=PT2[0][:], d_pt=d_pt0)

        def mach_lvl(g, k):
            gi = g % GLAG
            PQ2 = PQ2g[g % 2]
            P62 = P62g[g % 2]
            PT2 = PT2g[g % 2]
            ms = mstate[g]
            PA, PB = ms["PA"], ms["PB"]
            sq_ev = ms["sq_ev"]
            bank_g = hs["sq_bank_g"]
            PTcur = ms["PTcur"]
            d_pt = ms["d_pt"]
            bki = 2 + (k % 2)
            bk = pb[bki]
            if k < 6:
                MM(bk[0:128, 0:128], PA[1], PA[0], True, True, sem=None,
                   waits=sq_ev + bank_g[bki])
                MM(bk[0:128, 128:256], PA[0], PA[1], True, True, sem=None)
                MM(bk[0:128, 256:384], PB[1], PB[0], True, True, sem=None)
                mmq = MM(bk[0:128, 384:512], PB[0], PB[1], True, True)
                dst = PQ2[k % 2]
                a_ev = ACOPY(dst[:], bk[0:128, 0:512], waits=[("pe", mmq)])
                PA = (dst[:, 0:128], dst[:, 128:256])
                PB = (dst[:, 256:384], dst[:, 384:512])
                PkA, PkB = PA[0], PB[0]
            else:
                MM(bk[0:128, 0:128], PA[1], PA[0], True, True, sem=None,
                   waits=sq_ev + bank_g[bki])
                mmq = MM(bk[0:128, 128:256], PB[1], PB[0], True, True)
                a_ev = ACOPY(P62[:], bk[0:128, 0:256], waits=[("pe", mmq)])
                PkA, PkB = P62[:, 0:128], P62[:, 128:256]
            bank_g[bki] = [("ac", a_ev)]
            MM(pb[4][0:128, 0:128], PkA, PTcur[:, 0:128], True, True, sem=None,
               waits=[("ac", a_ev), ("dv", d_pt)] + bank_g[4])
            bank_g[4] = []
            mmp = MM(pb[4][0:128, 128:256], PkB, PTcur[:, 128:256], True, True)
            PTnew = Tnv2[gi][:] if k == 6 else PT2[k % 2][:]
            d_pt = em.op("vector", lambda h, o=PTnew, a=pb[4][0:128, 0:256], b=PTcur:
                         h.tensor_add(o, b, a), sem="dv", waits=[("pe", mmp)])
            bank_g[4] = [("dv", d_pt)]
            ms.update(PA=PA, PB=PB, sq_ev=[("ac", a_ev)], PTcur=PTnew, d_pt=d_pt)

        def tail_x1(g):
            gi = g % GLAG
            ms = mstate[g]
            is_out = 2 * g >= OCH0
            dz = em.val("dv")
            x1 = 0
            for cp in range(2):
                x1 = MM(pb[1][0:128, cp * 128:cp * 128 + 128],
                        Tnv2[gi][:, cp * 128:cp * 128 + 128],
                        AWK2[gi][:, cp * 128:cp * 128 + 128], True, True,
                        waits=([("dv", ms["d_pt"]), ("dv", dz)] + hs["g_p1"]) if cp == 0 else ())
                if is_out:
                    x1 = MM(pb[1][0:128, 256 + cp * 128:256 + cp * 128 + 128],
                            AWK2[gi][:, cp * 128:cp * 128 + 128],
                            Tnv2[gi][:, cp * 128:cp * 128 + 128], True, True)
            a_x1 = ACOPY(X1b2[gi][:], pb[1][0:128, 0:256], waits=[("pe", x1)])
            if is_out:
                a_x1 = ACOPY(X1T2[gi][:], pb[1][0:128, 256:512])
            ms["a_x1"] = a_x1

        def tail_ul(g):
            gi = g % GLAG
            ms = mstate[g]
            ul = 0
            for cp in range(2):
                ul = MM(pb[1][0:128, cp * 256:cp * 256 + 256],
                        Tnv2[gi][:, cp * 128:cp * 128 + 128],
                        bV2[gi][:, cp * 256:cp * 256 + 256], True, True,
                        waits=[("ac", ms["a_x1"])] if cp == 0 else ())
            ms["a_ul"] = ACOPY(Uloc2[gi][:], pb[1][0:128, 0:512], waits=[("pe", ul)])

        def tail_b(g):
            gi = g % GLAG
            ms = mstate[g]
            bb = 0
            for cp in range(2):
                bb = MM(pb[1][0:128, cp * 256:cp * 256 + 256],
                        Ktm2[gi][:, cp * 128:cp * 128 + 128],
                        Uloc2[gi][:, cp * 256:cp * 256 + 256], True, True,
                        waits=[("ac", ms["a_ul"])] if cp == 0 else ())
            ms["a_b"] = ACOPY(B2[gi][:], pb[1][0:128, 0:512], waits=[("pe", bb)])

        def tail_mt(g):
            gi = g % GLAG
            ms = mstate[g]
            mt = 0
            for cp in range(2):
                mt = MM(pb[1][0:128, cp * 128:cp * 128 + 128],
                        X1b2[gi][:, cp * 128:cp * 128 + 128],
                        Ktm2[gi][:, cp * 128:cp * 128 + 128], True, True,
                        waits=[("ac", ms["a_b"])] if cp == 0 else ())
            d_mt = 0
            for (jj, cp) in ((2 * g, 0), (2 * g + 1, 1)):
                d_mt = em.op("vector", lambda h, jj=jj, cp=cp, gi=gi:
                             h.scalar_tensor_tensor(
                                 MT2[gi][:, cp * 128:cp * 128 + 128], sIbf[:],
                                 ACrep[:, jj:jj + 1],
                                 pb[1][0:128, cp * 128:cp * 128 + 128],
                                 ALU.mult, ALU.add),
                             sem="dv", waits=[("pe", mt), ("dv", d_acr)])
            hs["g_p1"] = [("dv", d_mt)]
            ms["d_mt"] = d_mt

        def stages_of(g):
            st = [lambda g=g: mach_s0(g), lambda g=g: mach_s1(g), lambda g=g: mach_s2(g)]
            st += [lambda g=g, k=k: mach_lvl(g, k) for k in range(1, 7)]
            st += [lambda g=g: tail_x1(g), lambda g=g: tail_ul(g),
                   lambda g=g: tail_b(g), lambda g=g: tail_mt(g)]
            return st

        def chain(j):
            gi = (j // 2) % GLAG
            cp = j % 2
            cur = j % 2
            is_out = j >= OCH0
            qloc = j * 128 - QT0
            ms = mstate[j // 2]
            p = j % 2
            Sb = pbE[0:128, 0:256] if p == 0 else pbE[0:128, 512:768]
            Ub = pbE[0:128, 768:1024] if p == 0 else pbE[0:128, 256:512]
            gS = hs["g_S0"] if p == 0 else hs["g_S1"]
            gX = hs["g_U1"] if p == 0 else hs["g_U0"]
            s1 = MM(Sb, MT2[gi][:, cp * 128:cp * 128 + 128], St[cur][:], True, True,
                    waits=[("dv", hs["d_chain_prev"]), ("dv", ms["d_mt"])] + gS + gX)
            d_S = em.op("vector", lambda h, gi=gi, cp=cp, cur=cur, Sb=Sb:
                        h.tensor_add(St[1 - cur][:], B2[gi][:, cp * 256:cp * 256 + 256], Sb),
                        sem="dv", waits=[("pe", s1), ("ac", ms["a_b"])])
            hs["d_chain_prev"] = d_S
            if p == 0:
                hs["g_S0"] = [("dv", d_S)]
            else:
                hs["g_S1"] = [("dv", d_S)]

            if is_out:
                gU = hs["g_U0"] if p == 0 else hs["g_U1"]
                u2 = MM(Ub, X1T2[gi][:, cp * 128:cp * 128 + 128], St[cur][:], True, True,
                        waits=[("ac", ms["a_x1"])] + gU)
                d_U = em.op("vector", lambda h, gi=gi, cp=cp, Ub=Ub: h.tensor_add(
                    Ut[:], Uloc2[gi][:, cp * 256:cp * 256 + 256], Ub),
                    sem="dv", waits=[("pe", u2)])
                ob = j - OCH0
                mt = WtMt2[gi][:, cp * 256 + 128:cp * 256 + 256]
                MM(Ub[0:128, 0:128], St[cur][:, 0:128], q2[:, qloc:qloc + 128],
                   True, False, sem=None, waits=[("dv", d_U)])
                MM(Ub[0:128, 0:128], Ut[:, 0:128], mt, False, True, sem=None)
                MM(Ub[0:128, 128:256], St[cur][:, 128:256], q2[:, qloc:qloc + 128],
                   True, False, sem=None)
                oo = MM(Ub[0:128, 128:256], Ut[:, 128:256], mt, False, True)
                a_o2 = em.op("scalar", lambda h, ob=ob, Ub=Ub: h.activation(
                    outs_s.rearrange("p (v t) -> p v t", t=OWN)[:, :, ob * 128:(ob + 1) * 128],
                    Ub.rearrange("p (v t) -> p v t", t=128)[:, 0:2, :], AF.Copy),
                    sem="ac", waits=[("pe", oo)])
                if p == 0:
                    hs["g_U0"] = [("ac", a_o2)]
                else:
                    hs["g_U1"] = [("ac", a_o2)]

        NG = NCH // 2
        pend = {g: stages_of(g) for g in range(NG)}

        def pump(g, n):
            for _ in range(n):
                if pend.get(g):
                    pend[g].pop(0)()

        pump(0, 13)
        pump(1, 6)
        for g in range(NG):
            chain(2 * g)
            a = pend.get(g + 1, [])
            b = pend.get(g + 2, [])
            na, nb = len(a), min(6, len(b))
            for i in range(max(na, nb)):
                if i < na:
                    pump(g + 1, 1)
                if i < nb:
                    pump(g + 2, 1)
            chain(2 * g + 1)
        a_out_prev = em.val("ac")

        st1 = DMAS(outs_d[:, (2 * hd) * OWN:(2 * hd + 1) * OWN], outs_s[:, 0:OWN],
                   waits=[("ac", a_out_prev), ("dv", em.val("dv"))])
        st2 = DMAS(outs_d[:, (2 * hd + 1) * OWN:(2 * hd + 2) * OWN], outs_s[:, OWN:2 * OWN])
        prev_stores = [("dls", st2)]
        slab_war = [("pe", em.val("pe")), ("dv", em.val("dv")), ("ac", em.val("ac"))]

    em.wait("sync", "dls", em.val("dls"))
    em.barrier("mix")
    arena.reset()


def emit_wo(ctx):
    em = ctx["em"]; arena = ctx["arena"]
    pb = ctx["pb"]; pbE = ctx["pbE"]
    Wo = ctx["Wo"]; x_out = ctx["x_out"]
    outs_d, out1_d, xn2_d = ctx["outs_d"], ctx["out1_d"], ctx["xn2_d"]
    DMAS, DMAG, DMAA = ctx["DMAS"], ctx["DMAG"], ctx["DMAA"]
    colsA, colsB, epscol = ctx["colsA"], ctx["colsB"], ctx["epscol"]

    wo_s = arena.alloc("woW", [128, 32 * 1024], BF16)
    os_s = [arena.alloc("woO", [128, 32 * 512], BF16) for _ in range(2)]
    xo_s = [arena.alloc("woX", [128, 1024], F32) for _ in range(2)]
    o1 = [arena.alloc("woS", [128, 1024], F32) for _ in range(2)]
    outs3 = outs_d.rearrange("p (vc t) -> p vc t", t=OWN)

    os_war = [[], []]
    xo_war = [[], []]
    o1_war = [[], []]
    bank_g = [[], [], [], []]
    wo_war = []
    out1_st = {}
    for dh in range(2):
        wlv = []
        for vc in range(32):
            wlv.append(DMAG(wo_s[:, vc * 1024:(vc + 1) * 1024],
                            Wo[vc * 128:(vc + 1) * 128, dh * 1024:(dh + 1) * 1024],
                            waits=wo_war))
        wo_war = []
        for tq in range(4):
            qp = tq % 2
            lo = DMAS(os_s[qp].rearrange("p (vc t) -> p vc t", t=512),
                      outs3[:, :, tq * 512:(tq + 1) * 512], waits=os_war[qp])
            os_war[qp] = []
            for ti in range(4):
                tch = tq * 4 + ti
                tp = tch % 2
                lx = DMAA(xo_s[tp][:], x_out[tch * 128:(tch + 1) * 128,
                                             dh * 1024:(dh + 1) * 1024],
                          waits=xo_war[tp])
                for dc in range(2):
                    bki = 2 * tp + dc
                    bk = pb[bki]
                    for vc in range(32):
                        w = [("dlg", wlv[vc])]
                        if vc == 0:
                            w += [("dls", lo)] + bank_g[bki]
                            bank_g[bki] = []
                        mmv = MM_g(em, bk[0:128, 0:512],
                                   os_s[qp][:, vc * 512 + ti * 128:vc * 512 + ti * 128 + 128],
                                   wo_s[:, vc * 1024 + dc * 512:vc * 1024 + dc * 512 + 512],
                                   vc == 0, vc == 31, w)
                    ev = em.op("vector", lambda h, o=o1[tp][:, dc * 512:(dc + 1) * 512],
                               a=bk[0:128, 0:512],
                               b=xo_s[tp][:, dc * 512:(dc + 1) * 512]:
                               h.tensor_add(o, a, b), sem="dv",
                               waits=[("pe", mmv), ("dla", lx)] + o1_war[tp])
                    o1_war[tp] = []
                    bank_g[bki] = [("dv", ev)]
                st = DMAS(out1_d[tch * 128:(tch + 1) * 128, dh * 1024:(dh + 1) * 1024],
                          o1[tp][:], waits=[("dv", em.val("dv"))])
                out1_st[(tch, dh)] = st
                o1_war[tp] = [("dls", st)]
                xo_war[tp] = [("dv", em.val("dv"))]
            os_war[qp] = [("pe", em.val("pe"))]
        wo_war = [("pe", em.val("pe"))]

    xt = [arena.alloc("l2x", [128, D], F32)]
    sqs = [arena.alloc("l2q", [128, D], F32)]
    xnb = [arena.alloc("l2n", [128, D], BF16)]
    lastc = {}
    for i in range(OWN // 128):
        p = 0
        cols = colsA if p == 0 else colsB
        csum, cssq, cngm, cssd, cnvr, clnv, crsd = (cols[:, jj:jj+1] for jj in range(7))
        w = lastc.get(("xt", p), []) + [("dls", out1_st[(i, 1)])]
        ld = DMAS(xt[p][:], out1_d[i*128:(i+1)*128, :], waits=w)
        em.op("scalar", lambda h, o=sqs[p][:], i_=xt[p][:], a=csum:
              h.activation(o, i_, AF.Copy, accum_out=a), sem="ac",
              waits=[("dls", ld)] + lastc.get(("cols", p), []))
        em.op("scalar", lambda h, o=sqs[p][:], i_=xt[p][:], a=cssq:
              h.activation(o, i_, AF.Square, accum_out=a), sem="ac")
        a_sq = em.val("ac")
        em.op("vector", lambda h, o=cngm, i_=csum: h.tensor_scalar_mul(o, i_, -1.0 / D),
              sem="dv", waits=[("ac", a_sq)])
        em.op("vector", lambda h, o=cssd, i_=cssq: h.tensor_scalar_mul(o, i_, 1.0 / D), sem="dv")
        em.op("vector", lambda h, o=cnvr, i0=cngm, s=cngm, i1=cssd:
              h.scalar_tensor_tensor(o, i0, s, i1, ALU.mult, ALU.subtract), sem="dv")
        d_var = em.val("dv")
        em.op("scalar", lambda h, o=clnv, i_=cnvr:
              h.activation(o, i_, AF.Ln, scale=-1.0, bias=epscol[:]),
              sem="ac", waits=[("dv", d_var)])
        em.op("scalar", lambda h, o=crsd, i_=clnv: h.activation(o, i_, AF.Exp, scale=-0.5),
              sem="ac")
        a_rs = em.val("ac")
        em.op("vector", lambda h, o=xnb[p][:], i_=xt[p][:], s1=cngm, s2=crsd:
              h.tensor_scalar(o, i_, s1, s2, ALU.add, ALU.mult), sem="dv",
              waits=[("ac", a_rs)] + lastc.get(("xnb", p), []))
        d_app = em.val("dv")
        st = DMAS(xn2_d[i*128:(i+1)*128, :], xnb[p][:], waits=[("dv", d_app)])
        lastc[("xt", p)] = [("ac", a_sq), ("dv", d_app)]
        lastc[("cols", p)] = [("dv", d_app)]
        lastc[("xnb", p)] = [("dls", st)]
    em.wait("sync", "dls", em.val("dls"))
    em.barrier("ln2")
    arena.reset()


def MM_g(em, out_ap, lhsT, rhs, st, sp, waits=()):
    return em.op("tensor", lambda h, o=out_ap, l=lhsT, r=rhs, s=st, e=sp:
                 h.matmul(o, l, r, start=s, stop=e),
                 sem="pe" if sp else None, waits=waits)


def emit_ffn(ctx):
    em = ctx["em"]; arena = ctx["arena"]
    pb = ctx["pb"]
    WgT, WuT, WdT = ctx["WgT"], ctx["WuT"], ctx["WdT"]
    xn2_d, gated_d, out1_d = ctx["xn2_d"], ctx["gated_d"], ctx["out1_d"]
    out = ctx["out"]
    DMAS, DMAG, DMAA = ctx["DMAS"], ctx["DMAG"], ctx["DMAA"]

    xf = arena.alloc("ffX", [128, 16 * OWN], BF16)
    wstr2 = [arena.alloc(f"ffW{i}", [128, 16 * 1024], BF16) for i in range(2)]
    gs = arena.alloc("ffG", [128, 8 * OWN], BF16)
    gmu = [arena.alloc("ffM", [128, OWN], BF16) for _ in range(2)]
    tl = 0
    for dc in range(16):
        tl = ctx["DMAT"](xf[:, dc * OWN:(dc + 1) * OWN],
                         xn2_d[:, dc * 128:(dc + 1) * 128])
    bank_g = [[] for _ in range(4)]
    gmu_war = [[], []]
    gs_war = []
    wldv = {}

    def wload(idx, guard):
        he, phase = idx // 2, idx % 2
        wstr = wstr2[idx % 2]
        h0 = he * 1024
        wl = 0
        for kc in range(16):
            wl = DMAG(wstr[:, kc * 1024:(kc + 1) * 1024],
                      (WgT if phase == 0 else WuT)[kc * 128:(kc + 1) * 128, h0:h0 + 1024],
                      waits=guard)
        wldv[idx] = wl

    wload(0, [])
    wload(1, [])
    for he in range(8):
        h0 = he * 1024
        for phase in range(2):
            idx = he * 2 + phase
            wstr = wstr2[idx % 2]
            wl = wldv[idx]
            for hof in range(8):
                for ts in range(4):
                    bki = (hof * 4 + ts) % 4
                    bk = pb[bki]
                    for kc in range(16):
                        w = []
                        if kc == 0:
                            w = [("dlg", wl), ("dls", tl)] + bank_g[bki]
                            bank_g[bki] = []
                        mmv = MM_g(em, bk[0:128, 0:512],
                                   wstr[:, kc * 1024 + hof * 128:kc * 1024 + hof * 128 + 128],
                                   xf[:, kc * OWN + ts * 512:kc * OWN + ts * 512 + 512],
                                   kc == 0, kc == 15, w)
                    if phase == 0:
                        a_ev = em.op("scalar", lambda h, o=gs[:, hof * OWN + ts * 512:hof * OWN + ts * 512 + 512],
                                     i_=bk[0:128, 0:512]:
                                     h.activation(o, i_, AF.Silu), sem="ac",
                                     waits=[("pe", mmv)] + gs_war)
                        gs_war = []
                        bank_g[bki] = [("ac", a_ev)]
                    else:
                        hp = hof % 2
                        ev = em.op("vector", lambda h, o=gmu[hp][:, ts * 512:ts * 512 + 512],
                                   a=bk[0:128, 0:512],
                                   b=gs[:, hof * OWN + ts * 512:hof * OWN + ts * 512 + 512]:
                                   h.tensor_mul(o, a, b), sem="dv",
                                   waits=[("pe", mmv), ("ac", a_gate)] + gmu_war[hp])
                        gmu_war[hp] = []
                        bank_g[bki] = [("dv", ev)]
                if phase == 1:
                    hp = hof % 2
                    st = DMAS(gated_d[h0 + hof * 128:h0 + hof * 128 + 128, :],
                              gmu[hp][:], waits=[("dv", em.val("dv"))])
                    gmu_war[hp] = [("dls", st)]
            if idx + 2 < 16:
                wload(idx + 2, [("pe", em.val("pe"))])
            if phase == 0:
                a_gate = em.val("ac")
            else:
                gs_war = [("dv", em.val("dv"))]
    em.wait("sync", "dls", em.val("dls"))
    em.barrier("gup")
    arena.reset()

    wd_s2 = [arena.alloc("fdW0", [128, 64 * 512], BF16)]
    gld = [arena.alloc("fdG", [128, 64 * 256], BF16) for _ in range(2)]
    o1t = [arena.alloc("fdO", [128, 512], F32) for _ in range(2)]
    ost = [arena.alloc("fdS", [128, 512], F32) for _ in range(2)]
    g3 = gated_d.rearrange("(hc p) t -> p hc t", p=128)
    bank_g = [[] for _ in range(4)]
    gld_war = [[], []]
    o1t_war = [[], []]
    ost_war = [[], []]
    wdldv = {}

    def wdload(dq, guard):
        wl = 0
        for hc in range(64):
            wl = DMAG(wd_s2[0][:, hc * 512:(hc + 1) * 512],
                      WdT[hc * 128:(hc + 1) * 128, dq * 512:dq * 512 + 512], waits=guard)
        wdldv[dq] = wl

    wdload(0, [])
    for dq in range(4):
        d0 = dq * 512
        wd_s = wd_s2[0]
        wl = wdldv[dq]
        for tp2 in range(8):
            pp = tp2 % 2
            lg = DMAS(gld[pp].rearrange("p (hc t) -> p hc t", t=256),
                      g3[:, :, tp2 * 256:(tp2 + 1) * 256], waits=gld_war[pp])
            gld_war[pp] = []
            for ti in range(2):
                tch = tp2 * 2 + ti
                bki = tch % 2 + 2 * (dq % 2)
                bk = pb[bki]
                lx = DMAA(o1t[pp][:], out1_d[tch * 128:(tch + 1) * 128, d0:d0 + 512],
                          waits=o1t_war[pp])
                for hc in range(64):
                    w = []
                    if hc == 0:
                        w = [("dlg", wl), ("dls", lg)] + bank_g[bki]
                        bank_g[bki] = []
                    mmv = MM_g(em, bk[0:128, 0:512],
                               gld[pp][:, hc * 256 + ti * 128:hc * 256 + ti * 128 + 128],
                               wd_s[:, hc * 512:(hc + 1) * 512],
                               hc == 0, hc == 63, w)
                ev = em.op("vector", lambda h, o=ost[pp][:], a=bk[0:128, 0:512],
                           b=o1t[pp][:]:
                           h.tensor_add(o, a, b), sem="dv",
                           waits=[("pe", mmv), ("dla", lx)] + ost_war[pp])
                bank_g[bki] = [("dv", ev)]
                o1t_war[pp] = [("dv", ev)]
                st = DMAS(out[tch * 128:(tch + 1) * 128, d0:d0 + 512], ost[pp][:],
                          waits=[("dv", ev)])
                ost_war[pp] = [("dls", st)]
            gld_war[pp] = [("pe", em.val("pe"))]
        if dq + 1 < 4:
            wdload(dq + 1, [("pe", em.val("pe"))])



def _consts():
    r = np.arange(128)
    cIbf = np.eye(128, dtype=ml_dtypes.bfloat16)
    cIf = np.eye(128, dtype=np.float32)
    cMaskW = np.where(r[None, :] >= r[:, None], np.float32(NEG), np.float32(0))
    cMaskM = np.where(r[None, :] > r[:, None], np.float32(NEG), np.float32(0))
    return {
        "cIbf": cIbf, "cIf": cIf,
        "cMaskW": cMaskW.astype(np.float32), "cMaskM": cMaskM.astype(np.float32),
        "cOneRow": np.ones((1, 128), np.float32),
        "cOneRowB": np.ones((1, 512), ml_dtypes.bfloat16),
        "cOneColB": np.ones((128, 1), ml_dtypes.bfloat16),
        "cTinyB": np.full((1, 1), 1e-24, ml_dtypes.bfloat16),
    }


def make_in_maps(inputs):
    x = np.ascontiguousarray(inputs["L0_attn_x"], np.float32)
    Wab = np.ascontiguousarray(
        np.concatenate([inputs["L0_attn_Wa"], inputs["L0_attn_Wb"]], axis=1), np.float32)
    shared = dict(
        Wk=np.ascontiguousarray(inputs["L0_attn_Wk"], np.float32),
        Wv=np.ascontiguousarray(inputs["L0_attn_Wv"], np.float32),
        Wq=np.ascontiguousarray(inputs["L0_attn_Wq"], np.float32),
        Wab=Wab,
        Wo=np.ascontiguousarray(inputs["L0_attn_Wo"], np.float32),
        qcw=np.ascontiguousarray(inputs["L0_attn_qcw"], np.float32),
        qcb=np.ascontiguousarray(inputs["L0_attn_qcb"], np.float32).reshape(D, 1),
        kcw=np.ascontiguousarray(inputs["L0_attn_kcw"], np.float32),
        kcb=np.ascontiguousarray(inputs["L0_attn_kcb"], np.float32).reshape(D, 1),
        vcw=np.ascontiguousarray(inputs["L0_attn_vcw"], np.float32),
        vcb=np.ascontiguousarray(inputs["L0_attn_vcb"], np.float32).reshape(NV, 1),
        WgT=np.ascontiguousarray(inputs["L0_ffn_w_gate"].T, np.float32),
        WuT=np.ascontiguousarray(inputs["L0_ffn_w_up"].T, np.float32),
        WdT=np.ascontiguousarray(inputs["L0_ffn_w_down"].T, np.float32),
        **_consts(),
    )
    ones_half = np.ones((HK, OWN), np.float32)
    zeros_half = np.zeros((HK, OWN), np.float32)
    cm_h0 = np.concatenate([zeros_half, ones_half], axis=1)
    cm_h1 = np.concatenate([ones_half, ones_half], axis=1)
    lncm_h0 = np.concatenate([np.full((HK, OWN), NEG, np.float32), zeros_half], axis=1)
    lncm_h1 = np.zeros((HK, T), np.float32)
    in_maps = []
    for c in range(8):
        b, hf = c // 2, c % 2
        if hf == 1:
            x_kvc = x[b]
            x_outc = x[b, OWN:]
            cm, lncm = cm_h1, lncm_h0 * 0
        else:
            x_kvc = np.concatenate([np.zeros((OWN, D), np.float32), x[b, :OWN]], axis=0)
            x_outc = x[b, :OWN]
            cm, lncm = cm_h0, lncm_h0
        in_maps.append(dict(shared,
                            x_kv=np.ascontiguousarray(x_kvc),
                            x_out=np.ascontiguousarray(x_outc),
                            cm16=cm, lncm16=lncm))
    return in_maps


_NC_CACHE = {}


def kernel(**inputs):
    key = (NPHASES, DEBUG_OUT)
    if key not in _NC_CACHE:
        _NC_CACHE[key] = build_program()
    nc = _NC_CACHE[key]
    in_maps = make_in_maps(inputs)
    res = run_bass_kernel_spmd(nc, in_maps, core_ids=list(range(8)))
    out = np.empty((B, T, D), np.float32)
    for c in range(8):
        b, hf = c // 2, c % 2
        out[b, hf * OWN:(hf + 1) * OWN] = res.results[c]["out"]
    kernel.last_results = res
    return out


# version 11
# speedup vs baseline: 1.1752x; 1.1752x over previous
import os
import numpy as np
import ml_dtypes
from contextlib import ExitStack

import concourse.bass as bass
import concourse.mybir as mybir
from concourse.bass_utils import run_bass_kernel_spmd

F32 = mybir.dt.float32
BF16 = mybir.dt.bfloat16
AF = mybir.ActivationFunctionType
ALU = mybir.AluOpType
AX = mybir.AxisListType

B = 4
T = 4096
D = 2048
HK = 16
DK = 128
VE = 256
NV = HK * VE
DH = 8192
C = 128
NCH = T // C
OWN = T // 2
OCH0 = NCH // 2
QT0 = OWN - 128
QTN = T - QT0
NQT = QTN // 512
NEG = -1.0e30

NPHASES = int(os.environ.get("KM_NPH", "6"))
DEBUG_OUT = os.environ.get("KM_DEBUG", "0") == "1"

ENGS = ("tensor", "vector", "scalar", "gpsimd", "sync")


class EM:

    def __init__(self, nc, stack):
        self.nc = nc
        self.stack = stack
        self.ops = {e: [] for e in ENGS}
        self.sems = {}
        self.counts = {}
        self.hwm = {e: {} for e in ENGS}

    def sem(self, name):
        if name not in self.sems:
            self.sems[name] = self.stack.enter_context(self.nc.semaphore(name))
            self.counts[name] = 0
        return self.sems[name]

    def val(self, name):
        return self.counts.get(name, 0)

    def wait(self, eng, name, v=None):
        if v is None:
            v = self.counts.get(name, 0)
        if v <= 0 or self.hwm[eng].get(name, 0) >= v:
            return
        self.hwm[eng][name] = v
        s = self.sem(name)
        self.ops[eng].append(lambda h, s=s, v=v: h.wait_ge(s, v))

    def op(self, eng, builder, sem=None, n=1, waits=()):
        for w in waits:
            if isinstance(w, tuple):
                self.wait(eng, w[0], w[1])
            else:
                self.wait(eng, w)
        if sem is not None:
            s = self.sem(sem)
            self.counts[sem] += n
            def play(h, builder=builder, s=s, n=n):
                builder(h).then_inc(s, n)
        else:
            def play(h, builder=builder):
                builder(h)
        self.ops[eng].append(play)
        return self.counts.get(sem, 0) if sem else None

    def play(self, eng, h):
        for fn in self.ops[eng]:
            fn(h)

    def barrier(self, tag):
        s = self.sem("bar")
        target = self.counts["bar"] + len(ENGS)
        for e in ENGS:
            self.counts["bar"] += 1
            self.ops[e].append(lambda h, s=s: h.sem_inc(s, 1))
        for e in ENGS:
            self.hwm[e]["bar"] = target
            self.ops[e].append(lambda h, s=s, v=target: h.wait_ge(s, v))


class Arena:

    def __init__(self, nc, base, limit):
        self.nc = nc
        self.base = base
        self.limit = limit
        self.cur = base
        self.n = 0

    def alloc(self, name, shape, dtype):
        esz = {F32: 4, BF16: 2, mybir.dt.int32: 4}[dtype]
        nb = int(np.prod(shape[1:])) * esz
        off = (self.cur + 63) & ~63
        assert off + nb <= self.limit, f"arena overflow: {name} {off}+{nb}>{self.limit}"
        self.cur = off + nb
        self.n += 1
        return self.nc.alloc_sbuf_tensor_at(f"{name}_{self.n}", list(shape), dtype, offset=off)

    def reset(self):
        self.cur = self.base


class Bank:

    def __init__(self, h):
        self.h = h
        self.prod = None
        self.cons = None


def dget(dt_np):
    return {np.dtype(np.float32): F32}.get(np.dtype(dt_np), F32)


ARENA_BASE = 44 * 1024
ARENA_LIMIT = 224 * 1024 - 64


def build_program():
    nc = bass.Bass("TRN2", target_bir_lowering=False)
    stk = ExitStack()
    em = EM(nc, stk)

    def inp(name, shape, dt=F32):
        return nc.dram_tensor(name, list(shape), dt, kind="ExternalInput").ap()

    x_kv = inp("x_kv", (T, D))
    x_out = inp("x_out", (OWN, D))
    cm16 = inp("cm16", (HK, T))
    lncm16 = inp("lncm16", (HK, T))
    Wk = inp("Wk", (D, D)); Wv = inp("Wv", (D, NV)); Wq = inp("Wq", (D, D))
    Wab = inp("Wab", (D, 32))
    Wo = inp("Wo", (NV, D))
    qcw = inp("qcw", (D, 4)); qcb = inp("qcb", (D, 1))
    kcw = inp("kcw", (D, 4)); kcb = inp("kcb", (D, 1))
    vcw = inp("vcw", (NV, 4)); vcb = inp("vcb", (NV, 1))
    WgT = inp("WgT", (D, DH)); WuT = inp("WuT", (D, DH)); WdT = inp("WdT", (DH, D))
    cIbf = inp("cIbf", (128, 128), BF16)
    cIf = inp("cIf", (128, 128), F32)
    cMaskW = inp("cMaskW", (128, 128), F32)
    cMaskM = inp("cMaskM", (128, 128), F32)
    cOneRow = inp("cOneRow", (1, 128), F32)
    cOneRowB = inp("cOneRowB", (1, 512), BF16)
    cOneColB = inp("cOneColB", (128, 1), BF16)
    cTinyB = inp("cTinyB", (1, 1), BF16)

    kind_s = "ExternalOutput" if DEBUG_OUT else "Internal"
    def scratch(name, shape, dt):
        return nc.dram_tensor(name, list(shape), dt, kind=kind_s).ap()

    xn1 = scratch("xn1", (T, D), BF16)
    kr = scratch("kr", (D, T), BF16)
    vr = scratch("vr", (NV, T), BF16)
    qr = scratch("qr", (D, QTN), BF16)
    abz = scratch("abz", (32, T), F32)
    wrow_d = scratch("wrow_d", (HK, T), F32)
    carow_d = scratch("carow_d", (HK, T), F32)
    aclast_d = scratch("aclast_d", (HK, NCH), F32)
    tmsl_d = scratch("tmsl_d", (128, NCH * 64), F32)
    outs_d = scratch("outs_d", (128, 32 * OWN), BF16)
    out1_d = scratch("out1_d", (OWN, D), F32)
    xn2_d = scratch("xn2_d", (OWN, D), BF16)
    gated_d = scratch("gated_d", (DH, OWN), BF16)
    out = nc.dram_tensor("out", [OWN, D], F32, kind="ExternalOutput").ap()

    def perm(name, shape, dt):
        return stk.enter_context(nc.sbuf_tensor(name, list(shape), dt))

    sIbf = perm("sIbf", [128, 128], BF16)
    sIf = perm("sIf", [128, 128], F32)
    sMaskW = perm("sMaskW", [128, 128], F32)
    sMaskM = perm("sMaskM", [128, 128], F32)
    sOneRow = perm("sOneRow", [1, 128], F32)
    sOneRowB = perm("sOneRowB", [1, 512], BF16)
    sOneColB = perm("sOneColB", [128, 1], BF16)
    sTinyB = perm("sTinyB", [1, 1], BF16)
    sI2 = perm("sI2", [128, 256], BF16)
    zrow = perm("zrow", [16, 128], F32)
    tmsl = perm("tmsl", [128, NCH * 64], F32)
    colsA = perm("colsA", [128, 16], F32)
    colsB = perm("colsB", [128, 16], F32)
    epscol = perm("epscol", [128, 1], F32)

    arena = Arena(nc, ARENA_BASE, ARENA_LIMIT)

    pb = [stk.enter_context(nc.psum_tensor(f"pb{i}", [128, 512], F32)) for i in range(5)]
    pbE = stk.enter_context(nc.psum_tensor("pbE", [128, 1024], F32))
    pT = stk.enter_context(nc.psum_tensor("pT", [128, 1024], BF16))

    def DMAS(out_ap, in_ap, waits=()):
        return em.op("sync", lambda h, o=out_ap, i=in_ap: h.dma_start(out=o, in_=i),
                     sem="dls", n=16, waits=waits)

    def DMAT(out_ap, in_ap, waits=()):
        return em.op("sync", lambda h, o=out_ap, i=in_ap: h.dma_start_transpose(o, i),
                     sem="dls", n=16, waits=waits)

    def DMAG(out_ap, in_ap, waits=()):
        return em.op("gpsimd", lambda h, o=out_ap, i=in_ap: h.dma_start(out=o, in_=i),
                     sem="dlg", n=16, waits=waits)

    def DMAA(out_ap, in_ap, waits=()):
        return em.op("scalar", lambda h, o=out_ap, i=in_ap: h.dma_start(out=o, in_=i),
                     sem="dla", n=16, waits=waits)

    DMAS(sIbf[:], cIbf); DMAS(sIf[:], cIf)
    DMAS(sMaskW[:], cMaskW); DMAS(sMaskM[:], cMaskM)
    DMAS(sOneRow[:], cOneRow); DMAS(sOneRowB[:], cOneRowB)
    DMAS(sOneColB[:], cOneColB); DMAS(sTinyB[:], cTinyB)
    DMAS(sI2[:, 0:128], cIbf); DMAS(sI2[:, 128:256], cIbf)
    em.op("vector", lambda h: h.memset(zrow[:], 0.0), sem="dv")
    em.op("vector", lambda h: h.memset(epscol[:], 1e-5), sem="dv")
    c_init = em.val("dls")
    em.wait("sync", "dls", c_init)
    em.barrier("init")

    xt = [arena.alloc("xt", [128, D], F32) for _ in range(2)]
    sqs = [arena.alloc("sqs", [128, D], F32) for _ in range(2)]
    xnb = [arena.alloc("xnb", [128, D], BF16) for _ in range(2)]
    lastc = {}
    for i in range(T // 128):
        p = i % 2
        cols = colsA if p == 0 else colsB
        csum, cssq, cngm, cssd, cnvr, clnv, crsd = (cols[:, j:j+1] for j in range(7))
        w = lastc.get(("xt", p), [])
        ld = DMAS(xt[p][:], x_kv[i*128:(i+1)*128, :], waits=w)
        em.op("scalar", lambda h, o=sqs[p][:], i_=xt[p][:], a=csum:
              h.activation(o, i_, AF.Copy, accum_out=a), sem="ac",
              waits=[("dls", ld)] + lastc.get(("cols", p), []))
        em.op("scalar", lambda h, o=sqs[p][:], i_=xt[p][:], a=cssq:
              h.activation(o, i_, AF.Square, accum_out=a), sem="ac")
        a_sq = em.val("ac")
        em.op("vector", lambda h, o=cngm, i_=csum: h.tensor_scalar_mul(o, i_, -1.0 / D),
              sem="dv", waits=[("ac", a_sq)])
        em.op("vector", lambda h, o=cssd, i_=cssq: h.tensor_scalar_mul(o, i_, 1.0 / D), sem="dv")
        em.op("vector", lambda h, o=cnvr, i0=cngm, s=cngm, i1=cssd:
              h.scalar_tensor_tensor(o, i0, s, i1, ALU.mult, ALU.subtract), sem="dv")
        d_var = em.val("dv")
        em.op("scalar", lambda h, o=clnv, i_=cnvr: h.activation(o, i_, AF.Ln, scale=-1.0, bias=epscol[:]),
              sem="ac", waits=[("dv", d_var)])
        em.op("scalar", lambda h, o=crsd, i_=clnv: h.activation(o, i_, AF.Exp, scale=-0.5), sem="ac")
        a_rs = em.val("ac")
        em.op("vector", lambda h, o=xnb[p][:], i_=xt[p][:], s1=cngm, s2=crsd:
              h.tensor_scalar(o, i_, s1, s2, ALU.add, ALU.mult), sem="dv",
              waits=[("ac", a_rs)] + lastc.get(("xnb", p), []))
        d_app = em.val("dv")
        st = DMAS(xn1[i*128:(i+1)*128, :], xnb[p][:], waits=[("dv", d_app)])
        lastc[("xt", p)] = [("ac", a_sq), ("dv", d_app)]
        lastc[("cols", p)] = [("dv", d_app)]
        lastc[("xnb", p)] = [("dls", st)]
        if i == T // 128 // 2 - 1:
            ln1_half = st
    ln1_full = em.val("dls")
    arena.reset()

    if NPHASES >= 2:
        xng = arena.alloc("xng", [128, 16 * QTN], BF16)
        wbuf = [arena.alloc("wbuf", [128, 16 * 1024], BF16) for _ in range(2)]
        stg = [arena.alloc("stg", [128, QTN], BF16) for _ in range(2)]
        stgab = [arena.alloc("stgab", [32, 2048], F32) for _ in range(2)]
        wbuf_war = [[], []]
        stg_war = [[], []]
        stgab_war = [[], []]
        bank_war = [[] for _ in range(7)]
        bkctr = [0]

        xng_war = [[]]

        def proj_pass(tg, t0, tw, segs):
            lnw = [("dls", ln1_half if tg == 0 else ln1_full)]
            tl = 0
            for dc in range(16):
                tl = DMAT(xng[:, dc * tw:(dc + 1) * tw],
                          xn1[t0:t0 + tw, dc * 128:(dc + 1) * 128],
                          waits=xng_war[0] + lnw)
            xng_war[0] = []
            ntsub_full, rem = divmod(tw, 512)
            tsubs = [(i * 512, 512) for i in range(ntsub_full)]
            if rem:
                tsubs.append((ntsub_full * 512, rem))
            for (W, OFW, slab, slabw, f32out) in segs:
                for of0 in range(0, OFW, 1024):
                    sw = min(1024, OFW - of0)
                    wp = (of0 // 1024) % 2
                    wl = 0
                    for kc in range(16):
                        wl = DMAG(wbuf[wp][:, kc * sw:(kc + 1) * sw],
                                  W[kc * 128:(kc + 1) * 128, of0:of0 + sw],
                                  waits=wbuf_war[wp])
                    wbuf_war[wp] = []
                    for of8 in range(max(1, sw // 128)):
                        m = min(128, sw)
                        ofc = of0 + of8 * 128
                        sp = (ofc // 128) % 2
                        stgw = stgab_war[sp] if f32out else stg_war[sp]
                        sbuf_stg = stgab[sp] if f32out else stg[sp]
                        for (ts0, tsw) in tsubs:
                            bk = bkctr[0] % 4
                            bkctr[0] += 1
                            pbk = pb[bk]
                            for kc in range(16):
                                wts = []
                                if kc == 0:
                                    wts = [("dlg", wl), ("dls", tl)] + bank_war[bk]
                                    bank_war[bk] = []
                                em.op("tensor",
                                      lambda h, o=pbk[0:m, 0:tsw],
                                      l=wbuf[wp][:, kc * sw + of8 * 128: kc * sw + of8 * 128 + m],
                                      r=xng[:, kc * tw + ts0: kc * tw + ts0 + tsw],
                                      s=(kc == 0), e=(kc == 15):
                                      h.matmul(o, l, r, start=s, stop=e),
                                      sem="pe" if kc == 15 else None, waits=wts)
                            mmv = em.val("pe")
                            if bkctr[0] % 2 == 0:
                                ev = em.op("vector",
                                           lambda h, o=sbuf_stg[0:m, ts0:ts0 + tsw], i_=pbk[0:m, 0:tsw]:
                                           h.tensor_copy(o, i_), sem="dv",
                                           waits=[("pe", mmv)] + stgw)
                                bank_war[bk] = [("dv", ev)]
                            else:
                                ev = em.op("scalar",
                                           lambda h, o=sbuf_stg[0:m, ts0:ts0 + tsw], i_=pbk[0:m, 0:tsw]:
                                           h.activation(o, i_, AF.Copy), sem="ac",
                                           waits=[("pe", mmv)] + stgw)
                                bank_war[bk] = [("ac", ev)]
                            stgw.clear()
                        ofr = (ofc // 128) if not f32out else 0
                        dst = slab[ofc:ofc + m, slabw[0]:slabw[0] + tw] if f32out else \
                            slab[ofc:ofc + m, slabw[0]:slabw[0] + tw]
                        wv = [("dv", em.val("dv")), ("ac", em.val("ac"))]
                        st = DMAS(dst, sbuf_stg[0:m, 0:tw], waits=wv)
                        if f32out:
                            stgab_war[sp] = [("dls", st)]
                        else:
                            stg_war[sp] = [("dls", st)]
                    wbuf_war[wp] = [("pe", em.val("pe"))]
            xng_war[0] = [("pe", em.val("pe"))]

        for tg in range(2):
            t0 = tg * 2048
            proj_pass(tg, t0, 2048, [
                (Wab, 32, abz, (t0,), True),
                (Wk, 2048, kr, (t0,), False),
                (Wv, NV, vr, (t0,), False),
            ])
        proj_pass(2, QT0, QTN, [(Wq, 2048, qr, (0,), False)])
        em.wait("sync", "dls", em.val("dls"))
        em.barrier("proj")
        arena.reset()

    if NPHASES >= 3:
        Ab = arena.alloc("sc_a", [16, T], F32)
        Bb = arena.alloc("sc_b", [16, T], F32)
        Sb = arena.alloc("sc_s", [16, T], F32)
        Lb = arena.alloc("sc_l", [16, T], F32)
        CMb = arena.alloc("sc_cm", [16, T], F32)
        LCb = arena.alloc("sc_lc", [16, T], F32)
        sacl = arena.alloc("sc_acl", [16, NCH], F32)

        l1 = DMAS(Ab[:], abz[0:16, :])
        l2 = DMAS(Bb[:], abz[16:32, :])
        l3 = DMAS(CMb[:], cm16)
        l4 = DMAS(LCb[:], lncm16)

        def ACTOP(dst, src, fn, sc=1.0, bi=0.0, waits=(), accum=None):
            return em.op("scalar", lambda h, o=dst, i_=src, f=fn, s=sc, b=bi, a=accum:
                         h.activation(o, i_, f, scale=s, bias=b, accum_out=a),
                         sem="ac", waits=waits)

        def logsig(dst, z, wz):
            ACTOP(Sb[:], z[:], AF.Abs, waits=wz)
            ACTOP(Sb[:], Sb[:], AF.Exp, sc=-1.0)
            a1 = ACTOP(Sb[:], Sb[:], AF.Ln, bi=1.0)
            em.op("vector", lambda h, o=dst[:], i_=z[:]: h.tensor_scalar_min(o, i_, 0.0),
                  sem="dv", waits=[("ac", a1)])
            return em.op("vector", lambda h, o=dst[:], a=dst[:], b=Sb[:]:
                         h.tensor_sub(o, a, b), sem="dv")

        logsig(Lb, Ab, [("dls", l1)])
        d_la = em.val("dv")
        logsig(Ab, Bb, [("dls", l2), ("dv", d_la)])
        a_sg = ACTOP(Sb[:], Bb[:], AF.Sigmoid, waits=[("dv", em.val("dv"))])
        em.op("vector", lambda h: h.tensor_mul(Bb[:], Sb[:], CMb[:]), sem="dv",
              waits=[("ac", a_sg), ("dls", l3)])
        em.op("vector", lambda h: h.tensor_add(Ab[:], Ab[:], LCb[:]), sem="dv",
              waits=[("dls", l4)])
        for j in range(NCH):
            em.op("vector", lambda h, j=j: h.tensor_tensor_scan(
                CMb[:, j * C:(j + 1) * C], Lb[:, j * C:(j + 1) * C], zrow[:],
                0.0, ALU.add, ALU.add), sem="dv")
        em.op("vector", lambda h: h.tensor_scalar_mul(LCb[:], CMb[:], -1.0), sem="dv")
        d_ca = em.val("dv")
        em.op("vector", lambda h: h.tensor_add(Sb[:], CMb[:], Ab[:]), sem="dv")
        d_wr = em.val("dv")
        a_naw = ACTOP(Ab[:], Sb[:], AF.Exp, waits=[("dv", d_wr)])
        em.op("vector", lambda h: h.tensor_scalar_mul(Ab[:], Ab[:], -1.0), sem="dv",
              waits=[("ac", a_naw)])
        for j in range(NCH):
            em.op("vector", lambda h, j=j: h.tensor_single_scalar(
                Lb[:, j * C:(j + 1) * C], CMb[:, j * C:(j + 1) * C],
                CMb[:, j * C + C - 1:j * C + C], ALU.subtract),
                sem="dv", waits=[("dv", d_ca)])
        d_rp = em.val("dv")
        a_rac = ACTOP(Lb[:], Lb[:], AF.Exp, sc=-1.0, waits=[("dv", d_rp)])
        a_acl = ACTOP(sacl[:], CMb[:, C - 1::C], AF.Exp, waits=[("dv", d_ca)])
        st1 = DMAS(wrow_d, Sb[:], waits=[("dv", em.val("dv"))])
        st2 = DMAS(carow_d, CMb[:])
        st3 = DMAS(aclast_d, sacl[:], waits=[("ac", a_acl)])
        d_all = em.val("dv"); a_all = em.val("ac")
        tm_war = []
        for j in range(NCH):
            srcs = [LCb, Bb, Ab, Lb]
            for r in range(4):
                w = ([("dv", d_all), ("ac", a_all)] + tm_war) if r == 0 else ()
                em.op("tensor", lambda h, s=srcs[r], j=j, r=r:
                      h.transpose(pb[4][0:128, r * 16:(r + 1) * 16],
                                  s[:, j * C:(j + 1) * C], sIf[0:16, 0:16]),
                      sem="pe" if r == 3 else None, waits=w)
            pev = em.val("pe")
            ev = em.op("vector", lambda h, j=j: h.tensor_copy(
                tmsl[:, j * 64:(j + 1) * 64], pb[4][0:128, 0:64]),
                sem="dv", waits=[("pe", pev)])
            tm_war = [("dv", ev)]
        if DEBUG_OUT:
            DMAS(tmsl_d, tmsl[:], waits=[("dv", em.val("dv"))])
        em.wait("sync", "dls", em.val("dls"))
        em.barrier("scal")
        arena.reset()

    ctx = dict(locals())
    if NPHASES >= 4:
        emit_mix(ctx)
    if NPHASES >= 5:
        emit_wo(ctx)
    if NPHASES >= 6:
        emit_ffn(ctx)

    for s in ("dls", "dlg", "dla"):
        em.wait("sync", s, em.val(s))
    em.barrier("fin")

    blk = stk.enter_context(nc.Block())
    blk.tensor(lambda h: em.play("tensor", h))
    blk.vector(lambda h: em.play("vector", h))
    blk.scalar(lambda h: em.play("scalar", h))
    blk.gpsimd(lambda h: em.play("gpsimd", h))
    blk.sync(lambda h: em.play("sync", h))
    stk.close()
    return nc


def emit_mix(ctx):
    em = ctx["em"]; arena = ctx["arena"]
    pb = ctx["pb"]; pbE = ctx["pbE"]; pT = ctx["pT"]
    kr, vr, qr = ctx["kr"], ctx["vr"], ctx["qr"]
    wrow_d, carow_d, aclast_d = ctx["wrow_d"], ctx["carow_d"], ctx["aclast_d"]
    outs_d = ctx["outs_d"]; tmsl = ctx["tmsl"]
    sIbf, sIf = ctx["sIbf"], ctx["sIf"]
    sMaskW, sMaskM = ctx["sMaskW"], ctx["sMaskM"]
    sOneRow, sOneRowB, sTinyB = ctx["sOneRow"], ctx["sOneRowB"], ctx["sTinyB"]
    sOneColB = ctx["sOneColB"]; sI2 = ctx["sI2"]
    qcw, qcb, kcw, kcb, vcw, vcb = (ctx[k] for k in ("qcw", "qcb", "kcw", "kcb", "vcw", "vcb"))
    DMAS, DMAG = ctx["DMAS"], ctx["DMAG"]
    PAD = 4

    slabK = arena.alloc("mxK", [128, PAD + T], BF16)
    slabV0 = arena.alloc("mxV0", [128, PAD + T], BF16)
    slabV1 = arena.alloc("mxV1", [128, PAD + T], BF16)
    slabQ = arena.alloc("mxQ", [128, PAD + QTN], BF16)
    q2 = arena.alloc("mxQ2", [128, QTN], BF16)
    sqt = arena.alloc("mxSq", [128, 512], BF16)
    krnb = arena.alloc("mxKrn", [1, T], BF16)
    wrowh = arena.alloc("mxWrow", [1, T], F32)
    mrow = arena.alloc("mxMrow", [1, QTN], F32)
    lnscr = arena.alloc("mxLns", [1, 1024], F32)
    amq = arena.alloc("mxAmq", [1, QTN], BF16)
    aclr = arena.alloc("mxAcl", [1, NCH], F32)
    ACrep = arena.alloc("mxACr", [128, NCH], F32)
    cwq = arena.alloc("mxCwq", [128, 4], F32); cbq = arena.alloc("mxCbq", [128, 1], F32)
    cwk = arena.alloc("mxCwk", [128, 4], F32); cbk = arena.alloc("mxCbk", [128, 1], F32)
    cwv0 = arena.alloc("mxCv0", [128, 4], F32); cbv0 = arena.alloc("mxCb0", [128, 1], F32)
    cwv1 = arena.alloc("mxCv1", [128, 4], F32); cbv1 = arena.alloc("mxCb1", [128, 1], F32)
    dg = [arena.alloc(f"mxDg{i}", [128, 128], BF16) for i in range(16)]
    GLAG = 3
    WtMt2 = [arena.alloc(f"mxWtMt{i}", [128, 512], BF16) for i in range(GLAG)]
    Tnv2 = [arena.alloc(f"mxTnv{i}", [128, 256], BF16) for i in range(GLAG)]
    bV2 = [arena.alloc(f"mxbV{i}", [128, 512], BF16) for i in range(GLAG)]
    Ktm2 = [arena.alloc(f"mxKtm{i}", [128, 256], BF16) for i in range(GLAG)]
    W02g = [arena.alloc(f"mxW02{i}", [128, 256], BF16) for i in range(2)]
    PQ2g = [[arena.alloc(f"mxPQ{i}_{p}", [128, 512], BF16) for p in range(2)] for i in range(2)]
    P62g = [arena.alloc(f"mxP62{i}", [128, 256], BF16) for i in range(2)]
    PT2g = [[arena.alloc(f"mxPT{i}_{p}", [128, 256], BF16) for p in range(2)] for i in range(2)]
    LWM2 = arena.alloc("mxLWM", [128, 512], F32)
    AWK2 = [arena.alloc(f"mxAWK{i}", [128, 256], BF16) for i in range(GLAG)]
    X1b2 = [arena.alloc(f"mxX1{i}", [128, 256], BF16) for i in range(GLAG)]
    X1T2 = [arena.alloc(f"mxXT{i}", [128, 256], BF16) for i in range(GLAG)]
    Uloc2 = [arena.alloc(f"mxUL{i}", [128, 512], BF16) for i in range(GLAG)]
    B2 = [arena.alloc(f"mxB{i}", [128, 512], BF16) for i in range(GLAG)]
    MT2 = [arena.alloc(f"mxMT{i}", [128, 256], BF16) for i in range(GLAG)]
    Ut = arena.alloc("mxU", [128, 256], BF16)
    St = [arena.alloc(f"mxS{i}", [128, 256], BF16) for i in range(2)]
    outs_s = arena.alloc("mxOuts", [128, 2 * OWN], BF16)

    def MM(bank_ap, lhsT, rhs, st, sp, sem="pe", waits=()):
        return em.op("tensor", lambda h, o=bank_ap, l=lhsT, r=rhs, s=st, e=sp:
                     h.matmul(o, l, r, start=s, stop=e), sem=sem, waits=waits)

    def TRANS(bank_ap, in_ap, ident, waits=()):
        return em.op("tensor", lambda h, o=bank_ap, i_=in_ap, d=ident:
                     h.transpose(o, i_, d), sem="pe", waits=waits)

    def ACOPY(dst, src, waits=()):
        return em.op("scalar", lambda h, o=dst, i_=src:
                     h.activation(o, i_, AF.Copy), sem="ac", waits=waits)

    def AEXP(dst, src, bias, waits=(), scale=1.0):
        return em.op("scalar", lambda h, o=dst, i_=src, b=bias, s=scale:
                     h.activation(o, i_, AF.Exp, bias=b, scale=s), sem="ac", waits=waits)

    prev_stores = []
    slab_war = []
    for hd in range(HK):
        lw = list(slab_war)
        lK = DMAS(slabK[:, PAD:], kr[hd * 128:(hd + 1) * 128, :], waits=lw)
        lV0 = DMAS(slabV0[:, PAD:], vr[hd * 256:hd * 256 + 128, :], waits=lw)
        lV1 = DMAS(slabV1[:, PAD:], vr[hd * 256 + 128:hd * 256 + 256, :], waits=lw)
        lQ = DMAS(slabQ[:, PAD:], qr[hd * 128:(hd + 1) * 128, :], waits=lw)
        lrw = DMAS(wrowh[:], wrow_d[hd:hd + 1, :], waits=lw)
        lmr = DMAG(mrow[:], carow_d[hd:hd + 1, QT0:], waits=lw)
        lacl = DMAG(aclr[:], aclast_d[hd:hd + 1, :])
        lcw = DMAG(cwq[:], qcw[hd * 128:(hd + 1) * 128, :])
        DMAG(cbq[:], qcb[hd * 128:(hd + 1) * 128, :])
        DMAG(cwk[:], kcw[hd * 128:(hd + 1) * 128, :])
        DMAG(cbk[:], kcb[hd * 128:(hd + 1) * 128, :])
        DMAG(cwv0[:], vcw[hd * 256:hd * 256 + 128, :])
        DMAG(cbv0[:], vcb[hd * 256:hd * 256 + 128, :])
        DMAG(cwv1[:], vcw[hd * 256 + 128:hd * 256 + 256, :])
        lcb = DMAG(cbv1[:], vcb[hd * 256 + 128:hd * 256 + 256, :])
        z1 = em.op("vector", lambda h: h.memset(slabK[:, 0:PAD], 0.0), sem="dv",
                   waits=[("dls", lK)])
        em.op("vector", lambda h: h.memset(slabV0[:, 0:PAD], 0.0), sem="dv")
        em.op("vector", lambda h: h.memset(slabV1[:, 0:PAD], 0.0), sem="dv")
        em.op("vector", lambda h: h.memset(slabQ[:, 0:PAD], 0.0), sem="dv")

        slabs = [(slabK, T, cwk, cbk, 0), (slabV0, T, cwv0, cbv0, 1),
                 (slabV1, T, cwv1, cbv1, 2), (slabQ, QTN, cwq, cbq, 3)]
        for si, (slab, ncols, cw, cb, di) in enumerate(slabs):
            for j in range(4):
                em.op("vector", lambda h, j=j, c=cw, di=di: h.tensor_scalar_mul(
                    dg[di * 4 + j][:], sIbf[:], c[:, j:j + 1]), sem="dv",
                    waits=[("dlg", lcb), ("pe", em.val("pe"))])
        dtap = em.val("dv")
        a_prep0 = em.val("ac")
        steps = []
        for si, (slab, ncols, cw, cb, di) in enumerate(slabs):
            tiles = []
            t0v = 0
            while t0v < ncols:
                tw = min(512, ncols - t0v)
                tiles.append((si, t0v, tw))
                t0v += tw
            steps.append(list(reversed(tiles)))
        order = []
        mx = max(len(s) for s in steps)
        for i in range(mx):
            for s in steps:
                if i < len(s):
                    order.append(s[i])
        silu_g = {}
        for n_, (si, t0v, tw) in enumerate(order):
            slab, ncols, cw, cb, di = slabs[si]
            bki = n_ % 4
            bk = pb[bki]
            for j in range(4):
                MM(bk[0:128, 0:tw], dg[di * 4 + j][:],
                   slab[:, PAD + t0v - 3 + j:PAD + t0v - 3 + j + tw],
                   j == 0, j == 3,
                   sem="pe" if j == 3 else None,
                   waits=[("dv", dtap), ("ac", max(a_prep0, silu_g.get(bki, 0)))] if j == 0 else ())
            pev = em.val("pe")
            silu_g[bki] = em.op("scalar", lambda h, o=slab[:, PAD + t0v:PAD + t0v + tw],
                                i_=bk[0:128, 0:tw], b=cb:
                                h.activation(o, i_, AF.Silu, bias=b[:]), sem="ac",
                                waits=[("pe", pev), ("dlg", lcb)])
        a_conv_q = em.val("ac")

        for rnd in range(4):
            for half in range(2):
                ti = rnd * 2 + half
                a_sq = em.op("scalar", lambda h, o=sqt[:], i_=slabK[:, PAD + ti * 512:PAD + (ti + 1) * 512]:
                             h.activation(o, i_, AF.Square), sem="ac",
                             waits=[("dv", em.val("dv")), ("pe", em.val("pe"))])
                MM(pbE[0:1, half * 512:half * 512 + 512], sOneColB[:], sqt[:], True, False,
                   sem=None, waits=[("ac", a_sq)])
                MM(pbE[0:1, half * 512:half * 512 + 512], sTinyB[:], sOneRowB[:], False, True,
                   sem="pe")
            pev = em.val("pe")
            a_ln = em.op("scalar", lambda h, o=lnscr[:], i_=pbE[0:1, 0:1024]:
                         h.activation(o, i_, AF.Ln), sem="ac", waits=[("pe", pev)])
            em.op("scalar", lambda h, o=krnb[0:1, rnd * 1024:(rnd + 1) * 1024], i_=lnscr[:]:
                  h.activation(o, i_, AF.Exp, scale=-0.5), sem="ac")
        a_krn = em.val("ac")
        for ti in range(8):
            mmv = MM(pb[2][0:128, 0:512], sOneRowB[0:1, 0:128],
                     krnb[0:1, ti * 512:(ti + 1) * 512], True, True,
                     waits=[("ac", a_krn), ("dv", em.val("dv"))])
            em.op("vector", lambda h, ti=ti, o=slabK[:, PAD + ti * 512:PAD + (ti + 1) * 512],
                  a=slabK[:, PAD + ti * 512:PAD + (ti + 1) * 512], b=pb[2][0:128, 0:512]:
                  h.tensor_mul(o, a, b), sem="dv", waits=[("pe", mmv)])
        d_khat = em.val("dv")

        qtiles = [(0, 512), (512, 512), (1024, 512), (1536, 512), (2048, 128)]
        for rnd, grp in enumerate([qtiles[0:2], qtiles[2:4], qtiles[4:5]]):
            for gi, (t0v, tw) in enumerate(grp):
                a_sq = em.op("scalar", lambda h, o=sqt[0:128, 0:tw],
                             i_=slabQ[:, PAD + t0v:PAD + t0v + tw], tw=tw:
                             h.activation(o, i_, AF.Square), sem="ac",
                             waits=[("dv", em.val("dv")), ("pe", em.val("pe"))])
                MM(pbE[0:1, gi * 512:gi * 512 + tw], sOneColB[:], sqt[0:128, 0:tw], True, False,
                   sem=None, waits=[("ac", a_sq)])
                MM(pbE[0:1, gi * 512:gi * 512 + tw], sTinyB[:], sOneRowB[0:1, 0:tw], False, True,
                   sem="pe")
            gw = sum(tw for _, tw in grp)
            g0 = grp[0][0]
            pev = em.val("pe")
            a_ln = em.op("scalar", lambda h, o=lnscr[0:1, 0:gw], i_=pbE[0:1, 0:gw], gw=gw:
                         h.activation(o, i_, AF.Ln), sem="ac", waits=[("pe", pev)])
            em.op("vector", lambda h, o=mrow[0:1, g0:g0 + gw], i0=lnscr[0:1, 0:gw],
                  i1=mrow[0:1, g0:g0 + gw], gw=gw:
                  h.scalar_tensor_tensor(o, i0, -0.5, i1, ALU.mult, ALU.add),
                  sem="dv", waits=[("ac", a_ln), ("dlg", lmr)])
        d_mrow = em.val("dv")
        a_amq = em.op("scalar", lambda h: h.activation(amq[:], mrow[:], AF.Exp),
                      sem="ac", waits=[("dv", d_mrow)])
        for (t0v, tw) in qtiles:
            mmv = MM(pb[2][0:128, 0:tw], sOneRowB[0:1, 0:128], amq[0:1, t0v:t0v + tw],
                     True, True, waits=[("ac", a_amq), ("dv", em.val("dv"))])
            em.op("vector", lambda h, t0v=t0v, tw=tw, o=q2[:, t0v:t0v + tw],
                  a=slabQ[:, PAD + t0v:PAD + t0v + tw], b=pb[2][0:128, 0:tw]:
                  h.tensor_mul(o, a, b), sem="dv", waits=[("pe", mmv)])
        mmv = MM(pb[2][0:128, 0:NCH], sOneRow[:], aclr[:], True, True,
                 waits=[("dlg", lacl), ("dv", em.val("dv"))])
        d_acr = em.op("vector", lambda h: h.tensor_copy(ACrep[:], pb[2][0:128, 0:NCH]),
                      sem="dv", waits=[("pe", mmv)])
        em.op("vector", lambda h: h.memset(St[0][:], 0.0), sem="dv",
              waits=prev_stores)
        em.op("vector", lambda h: h.memset(outs_s[:], 0.0), sem="dv")
        d_prep = em.val("dv")

        a_prep = em.val("ac")
        hs = {
            "d_chain_prev": d_prep, "g_pT": [("ac", a_prep)],
            "g_p0": [("ac", a_prep)], "g_p1": [("ac", a_prep)],
            "g_S0": [("ac", a_prep)], "g_S1": [("ac", a_prep)],
            "g_U0": [("ac", a_prep)], "g_U1": [("ac", a_prep)],
            "d_lwm_war": d_prep,
            "sq_bank_g": {2: [("dv", d_prep)], 3: [("dv", d_prep)], 4: [("dv", d_prep)]},
        }
        mstate = {}

        def col(j, r):
            return tmsl[:, j * 64 + 16 * r + hd:j * 64 + 16 * r + hd + 1]

        def mach_s0(g):
            gi = g % GLAG
            jA, jB = 2 * g, 2 * g + 1
            wv = [("dv", d_khat), ("ac", a_conv_q), ("dv", em.val("dv"))] + hs["g_pT"]
            tkv = 0
            for (j, base) in ((jA, 0), (jB, 512)):
                jt = PAD + j * 128
                TRANS(pT[:, base + 128:base + 256], slabV0[:, jt:jt + 128], sIbf[:],
                      waits=wv); wv = ()
                TRANS(pT[:, base + 256:base + 384], slabV1[:, jt:jt + 128], sIbf[:])
                tkv = TRANS(pT[:, base + 384:base + 512], slabK[:, jt:jt + 128], sIbf[:])
            d_bv = d_ktm = 0
            for (j, base, cp) in ((jA, 0, 0), (jB, 512, 1)):
                d_bv = em.op("vector", lambda h, b=col(j, 1), base=base, cp=cp, gi=gi:
                             h.tensor_scalar_mul(bV2[gi][:, cp * 256:cp * 256 + 256],
                                                 pT[:, base + 128:base + 384], b),
                             sem="dv", waits=[("pe", tkv)])
                em.op("vector", lambda h, r=col(j, 3), base=base, cp=cp, gi=gi:
                      h.tensor_scalar_mul(Ktm2[gi][:, cp * 128:cp * 128 + 128],
                                          pT[:, base + 384:base + 512], r), sem="dv")
                d_ktm = em.op("vector", lambda h, n=col(j, 2), base=base, cp=cp, gi=gi:
                              h.tensor_scalar_mul(AWK2[gi][:, cp * 128:cp * 128 + 128],
                                                  pT[:, base + 384:base + 512], n), sem="dv")
            mstate[g] = dict(d_bv=d_bv, d_ktm=d_ktm)

        def mach_s1(g):
            gi = g % GLAG
            jA, jB = 2 * g, 2 * g + 1
            is_out = jA >= OCH0
            kk0 = 0
            for (j, base) in ((jA, 0), (jB, 256)):
                jt = PAD + j * 128
                w = ([("dv", em.val("dv"))] + hs["g_p0"]) if base == 0 else ()
                kk0 = MM(pb[0][0:128, base:base + 128], slabK[:, jt:jt + 128],
                         slabK[:, jt:jt + 128], True, True, waits=w)
                if is_out:
                    qloc = j * 128 - QT0
                    kk0 = MM(pb[0][0:128, base + 128:base + 256], slabK[:, jt:jt + 128],
                             slabQ[:, PAD + qloc:PAD + qloc + 128], True, True)
            ew = 0
            for (j, base) in ((jA, 0), (jB, 256)):
                w = hs["g_p1"] if base == 0 else ()
                MM(pb[1][0:128, base:base + 128], sMaskW[:], sIf[:], True, False,
                   sem=None, waits=w)
                ew = MM(pb[1][0:128, base:base + 128], sOneRow[:],
                        wrowh[0:1, j * 128:(j + 1) * 128], False, True,
                        waits=[("dls", lrw)])
                if is_out:
                    qloc = j * 128 - QT0
                    MM(pb[1][0:128, base + 128:base + 256], sMaskM[:], sIf[:], True, False,
                       sem=None)
                    ew = MM(pb[1][0:128, base + 128:base + 256], sOneRow[:],
                            mrow[0:1, qloc:qloc + 128], False, True)
            wid = 256 if is_out else 128
            a_lwm = 0
            for (j, base) in ((jA, 0), (jB, 256)):
                a_lwm = AEXP(LWM2[0:128, base:base + wid], pb[1][0:128, base:base + wid],
                             col(j, 0), waits=[("pe", ew), ("dv", hs["d_lwm_war"])])
            hs["g_p1"] = [("ac", a_lwm)]
            if is_out:
                d_wtmt = em.op("vector", lambda h, gi=gi: h.tensor_mul(
                    WtMt2[gi][:], LWM2[:], pb[0][0:128, 0:512]),
                    sem="dv", waits=[("ac", a_lwm), ("pe", kk0)])
            else:
                d_wtmt = em.op("vector", lambda h, gi=gi: h.tensor_mul(
                    WtMt2[gi].rearrange("p (c x) -> p c x", x=128)[:, 0::2, :],
                    LWM2.rearrange("p (c x) -> p c x", x=128)[:, 0::2, :],
                    pb[0].rearrange("p (c x) -> p c x", x=128)[:, 0::2, :]),
                    sem="dv", waits=[("ac", a_lwm), ("pe", kk0)])
            hs["d_lwm_war"] = d_wtmt
            hs["g_p0"] = [("dv", d_wtmt)]
            mstate[g]["d_wtmt"] = d_wtmt

        def mach_s2(g):
            gi = g % GLAG
            ms = mstate[g]
            W02 = W02g[g % 2]
            PT2 = PT2g[g % 2]
            d_wtmt, d_ktm = ms["d_wtmt"], ms["d_ktm"]
            tw0 = 0
            for (cp, base) in ((0, 0), (1, 512)):
                tw0 = TRANS(pT[:, base:base + 128], WtMt2[gi][:, cp * 256:cp * 256 + 128],
                            sIbf[:], waits=[("dv", d_wtmt), ("dv", d_ktm)] if cp == 0 else ())
            a_w0 = em.op("scalar", lambda h, W02=W02: h.activation(
                W02.rearrange("p (c x) -> p c x", x=128)[:, :, :],
                pT.rearrange("p (c x) -> p c x", x=128)[:, 0::4, :], AF.Copy),
                sem="ac", waits=[("pe", tw0)])
            hs["g_pT"] = [("ac", a_w0)]
            d_pt0 = em.op("vector", lambda h, gi=gi, PT2=PT2: h.tensor_sub(
                PT2[0].rearrange("p (c x) -> p c x", x=128)[:, :, :],
                sI2.rearrange("p (c x) -> p c x", x=128)[:, :, :],
                WtMt2[gi].rearrange("p (c x) -> p c x", x=128)[:, 0::2, :]),
                sem="dv", waits=[("dv", d_wtmt)])
            ms.update(PA=(W02[:, 0:128], WtMt2[gi][:, 0:128]),
                      PB=(W02[:, 128:256], WtMt2[gi][:, 256:384]),
                      sq_ev=[("ac", a_w0), ("dv", d_wtmt)],
                      PTcur=PT2[0][:], d_pt=d_pt0)

        def mach_lvl(g, k):
            gi = g % GLAG
            PQ2 = PQ2g[g % 2]
            P62 = P62g[g % 2]
            PT2 = PT2g[g % 2]
            ms = mstate[g]
            PA, PB = ms["PA"], ms["PB"]
            sq_ev = ms["sq_ev"]
            bank_g = hs["sq_bank_g"]
            PTcur = ms["PTcur"]
            d_pt = ms["d_pt"]
            bki = 2 + (k % 2)
            bk = pb[bki]
            if k < 6:
                MM(bk[0:128, 0:128], PA[1], PA[0], True, True, sem=None,
                   waits=sq_ev + bank_g[bki])
                MM(bk[0:128, 128:256], PA[0], PA[1], True, True, sem=None)
                MM(bk[0:128, 256:384], PB[1], PB[0], True, True, sem=None)
                mmq = MM(bk[0:128, 384:512], PB[0], PB[1], True, True)
                dst = PQ2[k % 2]
                a_ev = ACOPY(dst[:], bk[0:128, 0:512], waits=[("pe", mmq)])
                PA = (dst[:, 0:128], dst[:, 128:256])
                PB = (dst[:, 256:384], dst[:, 384:512])
                PkA, PkB = PA[0], PB[0]
            else:
                MM(bk[0:128, 0:128], PA[1], PA[0], True, True, sem=None,
                   waits=sq_ev + bank_g[bki])
                mmq = MM(bk[0:128, 128:256], PB[1], PB[0], True, True)
                a_ev = ACOPY(P62[:], bk[0:128, 0:256], waits=[("pe", mmq)])
                PkA, PkB = P62[:, 0:128], P62[:, 128:256]
            bank_g[bki] = [("ac", a_ev)]
            MM(pb[4][0:128, 0:128], PkA, PTcur[:, 0:128], True, True, sem=None,
               waits=[("ac", a_ev), ("dv", d_pt)] + bank_g[4])
            bank_g[4] = []
            mmp = MM(pb[4][0:128, 128:256], PkB, PTcur[:, 128:256], True, True)
            PTnew = Tnv2[gi][:] if k == 6 else PT2[k % 2][:]
            d_pt = em.op("vector", lambda h, o=PTnew, a=pb[4][0:128, 0:256], b=PTcur:
                         h.tensor_add(o, b, a), sem="dv", waits=[("pe", mmp)])
            bank_g[4] = [("dv", d_pt)]
            ms.update(PA=PA, PB=PB, sq_ev=[("ac", a_ev)], PTcur=PTnew, d_pt=d_pt)

        def tail_x1(g):
            gi = g % GLAG
            ms = mstate[g]
            is_out = 2 * g >= OCH0
            dz = em.val("dv")
            x1 = 0
            for cp in range(2):
                x1 = MM(pb[1][0:128, cp * 128:cp * 128 + 128],
                        Tnv2[gi][:, cp * 128:cp * 128 + 128],
                        AWK2[gi][:, cp * 128:cp * 128 + 128], True, True,
                        waits=([("dv", ms["d_pt"]), ("dv", dz)] + hs["g_p1"]) if cp == 0 else ())
                if is_out:
                    x1 = MM(pb[1][0:128, 256 + cp * 128:256 + cp * 128 + 128],
                            AWK2[gi][:, cp * 128:cp * 128 + 128],
                            Tnv2[gi][:, cp * 128:cp * 128 + 128], True, True)
            a_x1 = ACOPY(X1b2[gi][:], pb[1][0:128, 0:256], waits=[("pe", x1)])
            if is_out:
                a_x1 = ACOPY(X1T2[gi][:], pb[1][0:128, 256:512])
            ms["a_x1"] = a_x1

        def tail_ul(g):
            gi = g % GLAG
            ms = mstate[g]
            ul = 0
            for cp in range(2):
                ul = MM(pb[1][0:128, cp * 256:cp * 256 + 256],
                        Tnv2[gi][:, cp * 128:cp * 128 + 128],
                        bV2[gi][:, cp * 256:cp * 256 + 256], True, True,
                        waits=[("ac", ms["a_x1"])] if cp == 0 else ())
            ms["a_ul"] = ACOPY(Uloc2[gi][:], pb[1][0:128, 0:512], waits=[("pe", ul)])

        def tail_b(g):
            gi = g % GLAG
            ms = mstate[g]
            bb = 0
            for cp in range(2):
                bb = MM(pb[1][0:128, cp * 256:cp * 256 + 256],
                        Ktm2[gi][:, cp * 128:cp * 128 + 128],
                        Uloc2[gi][:, cp * 256:cp * 256 + 256], True, True,
                        waits=[("ac", ms["a_ul"])] if cp == 0 else ())
            ms["a_b"] = ACOPY(B2[gi][:], pb[1][0:128, 0:512], waits=[("pe", bb)])

        def tail_mt(g):
            gi = g % GLAG
            ms = mstate[g]
            mt = 0
            for cp in range(2):
                mt = MM(pb[1][0:128, cp * 128:cp * 128 + 128],
                        X1b2[gi][:, cp * 128:cp * 128 + 128],
                        Ktm2[gi][:, cp * 128:cp * 128 + 128], True, True,
                        waits=[("ac", ms["a_b"])] if cp == 0 else ())
            d_mt = 0
            for (jj, cp) in ((2 * g, 0), (2 * g + 1, 1)):
                d_mt = em.op("vector", lambda h, jj=jj, cp=cp, gi=gi:
                             h.scalar_tensor_tensor(
                                 MT2[gi][:, cp * 128:cp * 128 + 128], sIbf[:],
                                 ACrep[:, jj:jj + 1],
                                 pb[1][0:128, cp * 128:cp * 128 + 128],
                                 ALU.mult, ALU.add),
                             sem="dv", waits=[("pe", mt), ("dv", d_acr)])
            hs["g_p1"] = [("dv", d_mt)]
            ms["d_mt"] = d_mt

        def stages_of(g):
            st = [lambda g=g: mach_s0(g), lambda g=g: mach_s1(g), lambda g=g: mach_s2(g)]
            st += [lambda g=g, k=k: mach_lvl(g, k) for k in range(1, 7)]
            st += [lambda g=g: tail_x1(g), lambda g=g: tail_ul(g),
                   lambda g=g: tail_b(g), lambda g=g: tail_mt(g)]
            return st

        def chain(j):
            gi = (j // 2) % GLAG
            cp = j % 2
            cur = j % 2
            is_out = j >= OCH0
            qloc = j * 128 - QT0
            ms = mstate[j // 2]
            p = j % 2
            Sb = pbE[0:128, 0:256] if p == 0 else pbE[0:128, 512:768]
            Ub = pbE[0:128, 768:1024] if p == 0 else pbE[0:128, 256:512]
            gS = hs["g_S0"] if p == 0 else hs["g_S1"]
            gX = hs["g_U1"] if p == 0 else hs["g_U0"]
            s1 = MM(Sb, MT2[gi][:, cp * 128:cp * 128 + 128], St[cur][:], True, True,
                    waits=[("dv", hs["d_chain_prev"]), ("dv", ms["d_mt"])] + gS + gX)
            d_S = em.op("vector", lambda h, gi=gi, cp=cp, cur=cur, Sb=Sb:
                        h.tensor_add(St[1 - cur][:], B2[gi][:, cp * 256:cp * 256 + 256], Sb),
                        sem="dv", waits=[("pe", s1), ("ac", ms["a_b"])])
            hs["d_chain_prev"] = d_S
            if p == 0:
                hs["g_S0"] = [("dv", d_S)]
            else:
                hs["g_S1"] = [("dv", d_S)]

            if is_out:
                gU = hs["g_U0"] if p == 0 else hs["g_U1"]
                u2 = MM(Ub, X1T2[gi][:, cp * 128:cp * 128 + 128], St[cur][:], True, True,
                        waits=[("ac", ms["a_x1"])] + gU)
                d_U = em.op("vector", lambda h, gi=gi, cp=cp, Ub=Ub: h.tensor_add(
                    Ut[:], Uloc2[gi][:, cp * 256:cp * 256 + 256], Ub),
                    sem="dv", waits=[("pe", u2)])
                ob = j - OCH0
                mt = WtMt2[gi][:, cp * 256 + 128:cp * 256 + 256]
                MM(Ub[0:128, 0:128], St[cur][:, 0:128], q2[:, qloc:qloc + 128],
                   True, False, sem=None, waits=[("dv", d_U)])
                MM(Ub[0:128, 0:128], Ut[:, 0:128], mt, False, True, sem=None)
                MM(Ub[0:128, 128:256], St[cur][:, 128:256], q2[:, qloc:qloc + 128],
                   True, False, sem=None)
                oo = MM(Ub[0:128, 128:256], Ut[:, 128:256], mt, False, True)
                a_o2 = em.op("scalar", lambda h, ob=ob, Ub=Ub: h.activation(
                    outs_s.rearrange("p (v t) -> p v t", t=OWN)[:, :, ob * 128:(ob + 1) * 128],
                    Ub.rearrange("p (v t) -> p v t", t=128)[:, 0:2, :], AF.Copy),
                    sem="ac", waits=[("pe", oo)])
                if p == 0:
                    hs["g_U0"] = [("ac", a_o2)]
                else:
                    hs["g_U1"] = [("ac", a_o2)]

        NG = NCH // 2
        pend = {g: stages_of(g) for g in range(NG)}

        def pump(g, n):
            for _ in range(n):
                if pend.get(g):
                    pend[g].pop(0)()

        pump(0, 13)
        pump(1, 6)
        for g in range(NG):
            chain(2 * g)
            a = pend.get(g + 1, [])
            b = pend.get(g + 2, [])
            na, nb = len(a), min(6, len(b))
            for i in range(max(na, nb)):
                if i < na:
                    pump(g + 1, 1)
                if i < nb:
                    pump(g + 2, 1)
            chain(2 * g + 1)
        a_out_prev = em.val("ac")

        st1 = DMAS(outs_d[:, (2 * hd) * OWN:(2 * hd + 1) * OWN], outs_s[:, 0:OWN],
                   waits=[("ac", a_out_prev), ("dv", em.val("dv"))])
        st2 = DMAS(outs_d[:, (2 * hd + 1) * OWN:(2 * hd + 2) * OWN], outs_s[:, OWN:2 * OWN])
        prev_stores = [("dls", st2)]
        slab_war = [("pe", em.val("pe")), ("dv", em.val("dv")), ("ac", em.val("ac"))]

    em.wait("sync", "dls", em.val("dls"))
    em.barrier("mix")
    arena.reset()


def emit_wo(ctx):
    em = ctx["em"]; arena = ctx["arena"]
    pb = ctx["pb"]; pbE = ctx["pbE"]
    Wo = ctx["Wo"]; x_out = ctx["x_out"]
    outs_d, out1_d, xn2_d = ctx["outs_d"], ctx["out1_d"], ctx["xn2_d"]
    DMAS, DMAG, DMAA = ctx["DMAS"], ctx["DMAG"], ctx["DMAA"]
    colsA, colsB, epscol = ctx["colsA"], ctx["colsB"], ctx["epscol"]

    wo_s = arena.alloc("woW", [128, 32 * 1024], BF16)
    os_s = [arena.alloc("woO", [128, 32 * 512], BF16) for _ in range(2)]
    xo_s = [arena.alloc("woX", [128, 1024], F32) for _ in range(2)]
    o1 = [arena.alloc("woS", [128, 1024], F32) for _ in range(2)]
    outs3 = outs_d.rearrange("p (vc t) -> p vc t", t=OWN)

    os_war = [[], []]
    xo_war = [[], []]
    o1_war = [[], []]
    bank_g = [[], [], [], []]
    wo_war = []
    for dh in range(2):
        wlv = []
        for vc in range(32):
            wlv.append(DMAG(wo_s[:, vc * 1024:(vc + 1) * 1024],
                            Wo[vc * 128:(vc + 1) * 128, dh * 1024:(dh + 1) * 1024],
                            waits=wo_war))
        wo_war = []
        for tq in range(4):
            qp = tq % 2
            lo = DMAS(os_s[qp].rearrange("p (vc t) -> p vc t", t=512),
                      outs3[:, :, tq * 512:(tq + 1) * 512], waits=os_war[qp])
            os_war[qp] = []
            for ti in range(4):
                tch = tq * 4 + ti
                tp = tch % 2
                lx = DMAA(xo_s[tp][:], x_out[tch * 128:(tch + 1) * 128,
                                             dh * 1024:(dh + 1) * 1024],
                          waits=xo_war[tp])
                for dc in range(2):
                    bki = 2 * tp + dc
                    bk = pb[bki]
                    for vc in range(32):
                        w = [("dlg", wlv[vc])]
                        if vc == 0:
                            w += [("dls", lo)] + bank_g[bki]
                            bank_g[bki] = []
                        mmv = MM_g(em, bk[0:128, 0:512],
                                   os_s[qp][:, vc * 512 + ti * 128:vc * 512 + ti * 128 + 128],
                                   wo_s[:, vc * 1024 + dc * 512:vc * 1024 + dc * 512 + 512],
                                   vc == 0, vc == 31, w)
                    ev = em.op("vector", lambda h, o=o1[tp][:, dc * 512:(dc + 1) * 512],
                               a=bk[0:128, 0:512],
                               b=xo_s[tp][:, dc * 512:(dc + 1) * 512]:
                               h.tensor_add(o, a, b), sem="dv",
                               waits=[("pe", mmv), ("dla", lx)] + o1_war[tp])
                    o1_war[tp] = []
                    bank_g[bki] = [("dv", ev)]
                st = DMAS(out1_d[tch * 128:(tch + 1) * 128, dh * 1024:(dh + 1) * 1024],
                          o1[tp][:], waits=[("dv", em.val("dv"))])
                o1_war[tp] = [("dls", st)]
                xo_war[tp] = [("dv", em.val("dv"))]
            os_war[qp] = [("pe", em.val("pe"))]
        wo_war = [("pe", em.val("pe"))]
    em.wait("sync", "dls", em.val("dls"))
    em.barrier("wo_mm")
    arena.reset()

    xt = [arena.alloc("l2x", [128, D], F32) for _ in range(2)]
    sqs = [arena.alloc("l2q", [128, D], F32) for _ in range(2)]
    xnb = [arena.alloc("l2n", [128, D], BF16) for _ in range(2)]
    lastc = {}
    for i in range(OWN // 128):
        p = i % 2
        cols = colsA if p == 0 else colsB
        csum, cssq, cngm, cssd, cnvr, clnv, crsd = (cols[:, jj:jj+1] for jj in range(7))
        w = lastc.get(("xt", p), [])
        ld = DMAS(xt[p][:], out1_d[i*128:(i+1)*128, :], waits=w)
        em.op("scalar", lambda h, o=sqs[p][:], i_=xt[p][:], a=csum:
              h.activation(o, i_, AF.Copy, accum_out=a), sem="ac",
              waits=[("dls", ld)] + lastc.get(("cols", p), []))
        em.op("scalar", lambda h, o=sqs[p][:], i_=xt[p][:], a=cssq:
              h.activation(o, i_, AF.Square, accum_out=a), sem="ac")
        a_sq = em.val("ac")
        em.op("vector", lambda h, o=cngm, i_=csum: h.tensor_scalar_mul(o, i_, -1.0 / D),
              sem="dv", waits=[("ac", a_sq)])
        em.op("vector", lambda h, o=cssd, i_=cssq: h.tensor_scalar_mul(o, i_, 1.0 / D), sem="dv")
        em.op("vector", lambda h, o=cnvr, i0=cngm, s=cngm, i1=cssd:
              h.scalar_tensor_tensor(o, i0, s, i1, ALU.mult, ALU.subtract), sem="dv")
        d_var = em.val("dv")
        em.op("scalar", lambda h, o=clnv, i_=cnvr:
              h.activation(o, i_, AF.Ln, scale=-1.0, bias=epscol[:]),
              sem="ac", waits=[("dv", d_var)])
        em.op("scalar", lambda h, o=crsd, i_=clnv: h.activation(o, i_, AF.Exp, scale=-0.5),
              sem="ac")
        a_rs = em.val("ac")
        em.op("vector", lambda h, o=xnb[p][:], i_=xt[p][:], s1=cngm, s2=crsd:
              h.tensor_scalar(o, i_, s1, s2, ALU.add, ALU.mult), sem="dv",
              waits=[("ac", a_rs)] + lastc.get(("xnb", p), []))
        d_app = em.val("dv")
        st = DMAS(xn2_d[i*128:(i+1)*128, :], xnb[p][:], waits=[("dv", d_app)])
        lastc[("xt", p)] = [("ac", a_sq), ("dv", d_app)]
        lastc[("cols", p)] = [("dv", d_app)]
        lastc[("xnb", p)] = [("dls", st)]
    em.wait("sync", "dls", em.val("dls"))
    em.barrier("ln2")
    arena.reset()


def MM_g(em, out_ap, lhsT, rhs, st, sp, waits=()):
    return em.op("tensor", lambda h, o=out_ap, l=lhsT, r=rhs, s=st, e=sp:
                 h.matmul(o, l, r, start=s, stop=e),
                 sem="pe" if sp else None, waits=waits)


def emit_ffn(ctx):
    em = ctx["em"]; arena = ctx["arena"]
    pb = ctx["pb"]
    WgT, WuT, WdT = ctx["WgT"], ctx["WuT"], ctx["WdT"]
    xn2_d, gated_d, out1_d = ctx["xn2_d"], ctx["gated_d"], ctx["out1_d"]
    out = ctx["out"]
    DMAS, DMAG, DMAA = ctx["DMAS"], ctx["DMAG"], ctx["DMAA"]

    xf = arena.alloc("ffX", [128, 16 * OWN], BF16)
    wstr2 = [arena.alloc(f"ffW{i}", [128, 16 * 1024], BF16) for i in range(2)]
    gs = arena.alloc("ffG", [128, 8 * OWN], BF16)
    gmu = [arena.alloc("ffM", [128, OWN], BF16) for _ in range(2)]
    tl = 0
    for dc in range(16):
        tl = ctx["DMAT"](xf[:, dc * OWN:(dc + 1) * OWN],
                         xn2_d[:, dc * 128:(dc + 1) * 128])
    bank_g = [[] for _ in range(4)]
    gmu_war = [[], []]
    gs_war = []
    wldv = {}

    def wload(idx, guard):
        he, phase = idx // 2, idx % 2
        wstr = wstr2[idx % 2]
        h0 = he * 1024
        wl = 0
        for kc in range(16):
            wl = DMAG(wstr[:, kc * 1024:(kc + 1) * 1024],
                      (WgT if phase == 0 else WuT)[kc * 128:(kc + 1) * 128, h0:h0 + 1024],
                      waits=guard)
        wldv[idx] = wl

    wload(0, [])
    wload(1, [])
    for he in range(8):
        h0 = he * 1024
        for phase in range(2):
            idx = he * 2 + phase
            wstr = wstr2[idx % 2]
            wl = wldv[idx]
            for hof in range(8):
                for ts in range(4):
                    bki = (hof * 4 + ts) % 4
                    bk = pb[bki]
                    for kc in range(16):
                        w = []
                        if kc == 0:
                            w = [("dlg", wl), ("dls", tl)] + bank_g[bki]
                            bank_g[bki] = []
                        mmv = MM_g(em, bk[0:128, 0:512],
                                   wstr[:, kc * 1024 + hof * 128:kc * 1024 + hof * 128 + 128],
                                   xf[:, kc * OWN + ts * 512:kc * OWN + ts * 512 + 512],
                                   kc == 0, kc == 15, w)
                    if phase == 0:
                        a_ev = em.op("scalar", lambda h, o=gs[:, hof * OWN + ts * 512:hof * OWN + ts * 512 + 512],
                                     i_=bk[0:128, 0:512]:
                                     h.activation(o, i_, AF.Silu), sem="ac",
                                     waits=[("pe", mmv)] + gs_war)
                        gs_war = []
                        bank_g[bki] = [("ac", a_ev)]
                    else:
                        hp = hof % 2
                        ev = em.op("vector", lambda h, o=gmu[hp][:, ts * 512:ts * 512 + 512],
                                   a=bk[0:128, 0:512],
                                   b=gs[:, hof * OWN + ts * 512:hof * OWN + ts * 512 + 512]:
                                   h.tensor_mul(o, a, b), sem="dv",
                                   waits=[("pe", mmv), ("ac", a_gate)] + gmu_war[hp])
                        gmu_war[hp] = []
                        bank_g[bki] = [("dv", ev)]
                if phase == 1:
                    hp = hof % 2
                    st = DMAS(gated_d[h0 + hof * 128:h0 + hof * 128 + 128, :],
                              gmu[hp][:], waits=[("dv", em.val("dv"))])
                    gmu_war[hp] = [("dls", st)]
            if idx + 2 < 16:
                wload(idx + 2, [("pe", em.val("pe"))])
            if phase == 0:
                a_gate = em.val("ac")
            else:
                gs_war = [("dv", em.val("dv"))]
    em.wait("sync", "dls", em.val("dls"))
    em.barrier("gup")
    arena.reset()

    wd_s2 = [arena.alloc("fdW0", [128, 64 * 512], BF16)]
    gld = [arena.alloc("fdG", [128, 64 * 256], BF16) for _ in range(2)]
    o1t = [arena.alloc("fdO", [128, 512], F32) for _ in range(2)]
    ost = [arena.alloc("fdS", [128, 512], F32) for _ in range(2)]
    g3 = gated_d.rearrange("(hc p) t -> p hc t", p=128)
    bank_g = [[] for _ in range(4)]
    gld_war = [[], []]
    o1t_war = [[], []]
    ost_war = [[], []]
    wdldv = {}

    def wdload(dq, guard):
        wl = 0
        for hc in range(64):
            wl = DMAG(wd_s2[0][:, hc * 512:(hc + 1) * 512],
                      WdT[hc * 128:(hc + 1) * 128, dq * 512:dq * 512 + 512], waits=guard)
        wdldv[dq] = wl

    wdload(0, [])
    for dq in range(4):
        d0 = dq * 512
        wd_s = wd_s2[0]
        wl = wdldv[dq]
        for tp2 in range(8):
            pp = tp2 % 2
            lg = DMAS(gld[pp].rearrange("p (hc t) -> p hc t", t=256),
                      g3[:, :, tp2 * 256:(tp2 + 1) * 256], waits=gld_war[pp])
            gld_war[pp] = []
            for ti in range(2):
                tch = tp2 * 2 + ti
                bki = tch % 2 + 2 * (dq % 2)
                bk = pb[bki]
                lx = DMAA(o1t[pp][:], out1_d[tch * 128:(tch + 1) * 128, d0:d0 + 512],
                          waits=o1t_war[pp])
                for hc in range(64):
                    w = []
                    if hc == 0:
                        w = [("dlg", wl), ("dls", lg)] + bank_g[bki]
                        bank_g[bki] = []
                    mmv = MM_g(em, bk[0:128, 0:512],
                               gld[pp][:, hc * 256 + ti * 128:hc * 256 + ti * 128 + 128],
                               wd_s[:, hc * 512:(hc + 1) * 512],
                               hc == 0, hc == 63, w)
                ev = em.op("vector", lambda h, o=ost[pp][:], a=bk[0:128, 0:512],
                           b=o1t[pp][:]:
                           h.tensor_add(o, a, b), sem="dv",
                           waits=[("pe", mmv), ("dla", lx)] + ost_war[pp])
                bank_g[bki] = [("dv", ev)]
                o1t_war[pp] = [("dv", ev)]
                st = DMAS(out[tch * 128:(tch + 1) * 128, d0:d0 + 512], ost[pp][:],
                          waits=[("dv", ev)])
                ost_war[pp] = [("dls", st)]
            gld_war[pp] = [("pe", em.val("pe"))]
        if dq + 1 < 4:
            wdload(dq + 1, [("pe", em.val("pe"))])



def _consts():
    r = np.arange(128)
    cIbf = np.eye(128, dtype=ml_dtypes.bfloat16)
    cIf = np.eye(128, dtype=np.float32)
    cMaskW = np.where(r[None, :] >= r[:, None], np.float32(NEG), np.float32(0))
    cMaskM = np.where(r[None, :] > r[:, None], np.float32(NEG), np.float32(0))
    return {
        "cIbf": cIbf, "cIf": cIf,
        "cMaskW": cMaskW.astype(np.float32), "cMaskM": cMaskM.astype(np.float32),
        "cOneRow": np.ones((1, 128), np.float32),
        "cOneRowB": np.ones((1, 512), ml_dtypes.bfloat16),
        "cOneColB": np.ones((128, 1), ml_dtypes.bfloat16),
        "cTinyB": np.full((1, 1), 1e-24, ml_dtypes.bfloat16),
    }


def make_in_maps(inputs):
    x = np.ascontiguousarray(inputs["L0_attn_x"], np.float32)
    Wab = np.ascontiguousarray(
        np.concatenate([inputs["L0_attn_Wa"], inputs["L0_attn_Wb"]], axis=1), np.float32)
    shared = dict(
        Wk=np.ascontiguousarray(inputs["L0_attn_Wk"], np.float32),
        Wv=np.ascontiguousarray(inputs["L0_attn_Wv"], np.float32),
        Wq=np.ascontiguousarray(inputs["L0_attn_Wq"], np.float32),
        Wab=Wab,
        Wo=np.ascontiguousarray(inputs["L0_attn_Wo"], np.float32),
        qcw=np.ascontiguousarray(inputs["L0_attn_qcw"], np.float32),
        qcb=np.ascontiguousarray(inputs["L0_attn_qcb"], np.float32).reshape(D, 1),
        kcw=np.ascontiguousarray(inputs["L0_attn_kcw"], np.float32),
        kcb=np.ascontiguousarray(inputs["L0_attn_kcb"], np.float32).reshape(D, 1),
        vcw=np.ascontiguousarray(inputs["L0_attn_vcw"], np.float32),
        vcb=np.ascontiguousarray(inputs["L0_attn_vcb"], np.float32).reshape(NV, 1),
        WgT=np.ascontiguousarray(inputs["L0_ffn_w_gate"].T, np.float32),
        WuT=np.ascontiguousarray(inputs["L0_ffn_w_up"].T, np.float32),
        WdT=np.ascontiguousarray(inputs["L0_ffn_w_down"].T, np.float32),
        **_consts(),
    )
    ones_half = np.ones((HK, OWN), np.float32)
    zeros_half = np.zeros((HK, OWN), np.float32)
    cm_h0 = np.concatenate([zeros_half, ones_half], axis=1)
    cm_h1 = np.concatenate([ones_half, ones_half], axis=1)
    lncm_h0 = np.concatenate([np.full((HK, OWN), NEG, np.float32), zeros_half], axis=1)
    lncm_h1 = np.zeros((HK, T), np.float32)
    in_maps = []
    for c in range(8):
        b, hf = c // 2, c % 2
        if hf == 1:
            x_kvc = x[b]
            x_outc = x[b, OWN:]
            cm, lncm = cm_h1, lncm_h0 * 0
        else:
            x_kvc = np.concatenate([np.zeros((OWN, D), np.float32), x[b, :OWN]], axis=0)
            x_outc = x[b, :OWN]
            cm, lncm = cm_h0, lncm_h0
        in_maps.append(dict(shared,
                            x_kv=np.ascontiguousarray(x_kvc),
                            x_out=np.ascontiguousarray(x_outc),
                            cm16=cm, lncm16=lncm))
    return in_maps


_NC_CACHE = {}


def kernel(**inputs):
    key = (NPHASES, DEBUG_OUT)
    if key not in _NC_CACHE:
        _NC_CACHE[key] = build_program()
    nc = _NC_CACHE[key]
    in_maps = make_in_maps(inputs)
    res = run_bass_kernel_spmd(nc, in_maps, core_ids=list(range(8)))
    out = np.empty((B, T, D), np.float32)
    for c in range(8):
        b, hf = c // 2, c % 2
        out[b, hf * OWN:(hf + 1) * OWN] = res.results[c]["out"]
    kernel.last_results = res
    return out


# version 12
# speedup vs baseline: 1.1808x; 1.0048x over previous
import os
import numpy as np
import ml_dtypes
from contextlib import ExitStack

import concourse.bass as bass
import concourse.mybir as mybir
from concourse.bass_utils import run_bass_kernel_spmd

F32 = mybir.dt.float32
BF16 = mybir.dt.bfloat16
AF = mybir.ActivationFunctionType
ALU = mybir.AluOpType
AX = mybir.AxisListType

B = 4
T = 4096
D = 2048
HK = 16
DK = 128
VE = 256
NV = HK * VE
DH = 8192
C = 128
NCH = T // C
OWN = T // 2
OCH0 = NCH // 2
QT0 = OWN - 128
QTN = T - QT0
NQT = QTN // 512
NEG = -1.0e30

NPHASES = int(os.environ.get("KM_NPH", "6"))
DEBUG_OUT = os.environ.get("KM_DEBUG", "0") == "1"

ENGS = ("tensor", "vector", "scalar", "gpsimd", "sync")


class EM:

    def __init__(self, nc, stack):
        self.nc = nc
        self.stack = stack
        self.ops = {e: [] for e in ENGS}
        self.sems = {}
        self.counts = {}
        self.hwm = {e: {} for e in ENGS}

    def sem(self, name):
        if name not in self.sems:
            self.sems[name] = self.stack.enter_context(self.nc.semaphore(name))
            self.counts[name] = 0
        return self.sems[name]

    def val(self, name):
        return self.counts.get(name, 0)

    def wait(self, eng, name, v=None):
        if v is None:
            v = self.counts.get(name, 0)
        if v <= 0 or self.hwm[eng].get(name, 0) >= v:
            return
        self.hwm[eng][name] = v
        s = self.sem(name)
        self.ops[eng].append(lambda h, s=s, v=v: h.wait_ge(s, v))

    def op(self, eng, builder, sem=None, n=1, waits=()):
        for w in waits:
            if isinstance(w, tuple):
                self.wait(eng, w[0], w[1])
            else:
                self.wait(eng, w)
        if sem is not None:
            s = self.sem(sem)
            self.counts[sem] += n
            def play(h, builder=builder, s=s, n=n):
                builder(h).then_inc(s, n)
        else:
            def play(h, builder=builder):
                builder(h)
        self.ops[eng].append(play)
        return self.counts.get(sem, 0) if sem else None

    def play(self, eng, h):
        for fn in self.ops[eng]:
            fn(h)

    def barrier(self, tag):
        s = self.sem("bar")
        target = self.counts["bar"] + len(ENGS)
        for e in ENGS:
            self.counts["bar"] += 1
            self.ops[e].append(lambda h, s=s: h.sem_inc(s, 1))
        for e in ENGS:
            self.hwm[e]["bar"] = target
            self.ops[e].append(lambda h, s=s, v=target: h.wait_ge(s, v))


class Arena:

    def __init__(self, nc, base, limit):
        self.nc = nc
        self.base = base
        self.limit = limit
        self.cur = base
        self.n = 0

    def alloc(self, name, shape, dtype):
        esz = {F32: 4, BF16: 2, mybir.dt.int32: 4}[dtype]
        nb = int(np.prod(shape[1:])) * esz
        off = (self.cur + 63) & ~63
        assert off + nb <= self.limit, f"arena overflow: {name} {off}+{nb}>{self.limit}"
        self.cur = off + nb
        self.n += 1
        return self.nc.alloc_sbuf_tensor_at(f"{name}_{self.n}", list(shape), dtype, offset=off)

    def reset(self):
        self.cur = self.base


class Bank:

    def __init__(self, h):
        self.h = h
        self.prod = None
        self.cons = None


def dget(dt_np):
    return {np.dtype(np.float32): F32}.get(np.dtype(dt_np), F32)


ARENA_BASE = 44 * 1024
ARENA_LIMIT = 224 * 1024 - 64


def build_program():
    nc = bass.Bass("TRN2", target_bir_lowering=False)
    stk = ExitStack()
    em = EM(nc, stk)

    def inp(name, shape, dt=F32):
        return nc.dram_tensor(name, list(shape), dt, kind="ExternalInput").ap()

    x_kv = inp("x_kv", (T, D))
    x_out = inp("x_out", (OWN, D))
    cm16 = inp("cm16", (HK, T))
    lncm16 = inp("lncm16", (HK, T))
    Wk = inp("Wk", (D, D)); Wv = inp("Wv", (D, NV)); Wq = inp("Wq", (D, D))
    Wab = inp("Wab", (D, 32))
    Wo = inp("Wo", (NV, D))
    qcw = inp("qcw", (D, 4)); qcb = inp("qcb", (D, 1))
    kcw = inp("kcw", (D, 4)); kcb = inp("kcb", (D, 1))
    vcw = inp("vcw", (NV, 4)); vcb = inp("vcb", (NV, 1))
    WgT = inp("WgT", (D, DH)); WuT = inp("WuT", (D, DH)); WdT = inp("WdT", (DH, D))
    cIbf = inp("cIbf", (128, 128), BF16)
    cIf = inp("cIf", (128, 128), F32)
    cMaskW = inp("cMaskW", (128, 128), F32)
    cMaskM = inp("cMaskM", (128, 128), F32)
    cOneRow = inp("cOneRow", (1, 128), F32)
    cOneRowB = inp("cOneRowB", (1, 512), BF16)
    cOneColB = inp("cOneColB", (128, 1), BF16)
    cTinyB = inp("cTinyB", (1, 1), BF16)

    kind_s = "ExternalOutput" if DEBUG_OUT else "Internal"
    def scratch(name, shape, dt):
        return nc.dram_tensor(name, list(shape), dt, kind=kind_s).ap()

    xn1 = scratch("xn1", (T, D), BF16)
    kr = scratch("kr", (D, T), BF16)
    vr = scratch("vr", (NV, T), BF16)
    qr = scratch("qr", (D, QTN), BF16)
    abz = scratch("abz", (32, T), F32)
    wrow_d = scratch("wrow_d", (HK, T), F32)
    carow_d = scratch("carow_d", (HK, T), F32)
    aclast_d = scratch("aclast_d", (HK, NCH), F32)
    tmsl_d = scratch("tmsl_d", (128, NCH * 64), F32)
    outs_d = scratch("outs_d", (128, 32 * OWN), BF16)
    out1_d = scratch("out1_d", (OWN, D), F32)
    xn2_d = scratch("xn2_d", (OWN, D), BF16)
    gated_d = scratch("gated_d", (DH, OWN), BF16)
    out = nc.dram_tensor("out", [OWN, D], F32, kind="ExternalOutput").ap()

    def perm(name, shape, dt):
        return stk.enter_context(nc.sbuf_tensor(name, list(shape), dt))

    sIbf = perm("sIbf", [128, 128], BF16)
    sIf = perm("sIf", [128, 128], F32)
    sMaskW = perm("sMaskW", [128, 128], F32)
    sMaskM = perm("sMaskM", [128, 128], F32)
    sOneRow = perm("sOneRow", [1, 128], F32)
    sOneRowB = perm("sOneRowB", [1, 512], BF16)
    sOneColB = perm("sOneColB", [128, 1], BF16)
    sTinyB = perm("sTinyB", [1, 1], BF16)
    sI2 = perm("sI2", [128, 256], BF16)
    zrow = perm("zrow", [16, 128], F32)
    tmsl = perm("tmsl", [128, NCH * 64], F32)
    colsA = perm("colsA", [128, 16], F32)
    colsB = perm("colsB", [128, 16], F32)
    epscol = perm("epscol", [128, 1], F32)

    arena = Arena(nc, ARENA_BASE, ARENA_LIMIT)

    pb = [stk.enter_context(nc.psum_tensor(f"pb{i}", [128, 512], F32)) for i in range(5)]
    pbE = stk.enter_context(nc.psum_tensor("pbE", [128, 1024], F32))
    pT = stk.enter_context(nc.psum_tensor("pT", [128, 1024], BF16))

    def DMAS(out_ap, in_ap, waits=()):
        return em.op("sync", lambda h, o=out_ap, i=in_ap: h.dma_start(out=o, in_=i),
                     sem="dls", n=16, waits=waits)

    def DMAT(out_ap, in_ap, waits=()):
        return em.op("sync", lambda h, o=out_ap, i=in_ap: h.dma_start_transpose(o, i),
                     sem="dls", n=16, waits=waits)

    def DMAG(out_ap, in_ap, waits=()):
        return em.op("gpsimd", lambda h, o=out_ap, i=in_ap: h.dma_start(out=o, in_=i),
                     sem="dlg", n=16, waits=waits)

    def DMAA(out_ap, in_ap, waits=()):
        return em.op("scalar", lambda h, o=out_ap, i=in_ap: h.dma_start(out=o, in_=i),
                     sem="dla", n=16, waits=waits)

    DMAS(sIbf[:], cIbf); DMAS(sIf[:], cIf)
    DMAS(sMaskW[:], cMaskW); DMAS(sMaskM[:], cMaskM)
    DMAS(sOneRow[:], cOneRow); DMAS(sOneRowB[:], cOneRowB)
    DMAS(sOneColB[:], cOneColB); DMAS(sTinyB[:], cTinyB)
    DMAS(sI2[:, 0:128], cIbf); DMAS(sI2[:, 128:256], cIbf)
    em.op("vector", lambda h: h.memset(zrow[:], 0.0), sem="dv")
    em.op("vector", lambda h: h.memset(epscol[:], 1e-5), sem="dv")
    c_init = em.val("dls")
    em.wait("sync", "dls", c_init)
    em.barrier("init")

    xt = [arena.alloc("xt", [128, D], F32) for _ in range(2)]
    sqs = [arena.alloc("sqs", [128, D], F32) for _ in range(2)]
    xnb = [arena.alloc("xnb", [128, D], BF16) for _ in range(2)]
    lastc = {}
    for i in range(T // 128):
        p = i % 2
        cols = colsA if p == 0 else colsB
        csum, cssq, cngm, cssd, cnvr, clnv, crsd = (cols[:, j:j+1] for j in range(7))
        w = lastc.get(("xt", p), [])
        ld = DMAS(xt[p][:], x_kv[i*128:(i+1)*128, :], waits=w)
        em.op("scalar", lambda h, o=sqs[p][:], i_=xt[p][:], a=csum:
              h.activation(o, i_, AF.Copy, accum_out=a), sem="ac",
              waits=[("dls", ld)] + lastc.get(("cols", p), []))
        em.op("scalar", lambda h, o=sqs[p][:], i_=xt[p][:], a=cssq:
              h.activation(o, i_, AF.Square, accum_out=a), sem="ac")
        a_sq = em.val("ac")
        em.op("vector", lambda h, o=cngm, i_=csum: h.tensor_scalar_mul(o, i_, -1.0 / D),
              sem="dv", waits=[("ac", a_sq)])
        em.op("vector", lambda h, o=cssd, i_=cssq: h.tensor_scalar_mul(o, i_, 1.0 / D), sem="dv")
        em.op("vector", lambda h, o=cnvr, i0=cngm, s=cngm, i1=cssd:
              h.scalar_tensor_tensor(o, i0, s, i1, ALU.mult, ALU.subtract), sem="dv")
        d_var = em.val("dv")
        em.op("scalar", lambda h, o=clnv, i_=cnvr: h.activation(o, i_, AF.Ln, scale=-1.0, bias=epscol[:]),
              sem="ac", waits=[("dv", d_var)])
        em.op("scalar", lambda h, o=crsd, i_=clnv: h.activation(o, i_, AF.Exp, scale=-0.5), sem="ac")
        a_rs = em.val("ac")
        em.op("vector", lambda h, o=xnb[p][:], i_=xt[p][:], s1=cngm, s2=crsd:
              h.tensor_scalar(o, i_, s1, s2, ALU.add, ALU.mult), sem="dv",
              waits=[("ac", a_rs)] + lastc.get(("xnb", p), []))
        d_app = em.val("dv")
        st = DMAS(xn1[i*128:(i+1)*128, :], xnb[p][:], waits=[("dv", d_app)])
        lastc[("xt", p)] = [("ac", a_sq), ("dv", d_app)]
        lastc[("cols", p)] = [("dv", d_app)]
        lastc[("xnb", p)] = [("dls", st)]
        if i == T // 128 // 2 - 1:
            ln1_half = st
    ln1_full = em.val("dls")
    arena.reset()

    if NPHASES >= 2:
        xng = arena.alloc("xng", [128, 16 * QTN], BF16)
        wbuf = [arena.alloc("wbuf", [128, 16 * 1024], BF16) for _ in range(2)]
        stg = [arena.alloc("stg", [128, QTN], BF16) for _ in range(2)]
        stgab = [arena.alloc("stgab", [32, 2048], F32) for _ in range(2)]
        wbuf_war = [[], []]
        stg_war = [[], []]
        stgab_war = [[], []]
        bank_war = [[] for _ in range(7)]
        bkctr = [0]

        xng_war = [[]]

        def proj_pass(tg, t0, tw, segs):
            lnw = [("dls", ln1_half if tg == 0 else ln1_full)]
            tl = 0
            for dc in range(16):
                tl = DMAT(xng[:, dc * tw:(dc + 1) * tw],
                          xn1[t0:t0 + tw, dc * 128:(dc + 1) * 128],
                          waits=xng_war[0] + lnw)
            xng_war[0] = []
            ntsub_full, rem = divmod(tw, 512)
            tsubs = [(i * 512, 512) for i in range(ntsub_full)]
            if rem:
                tsubs.append((ntsub_full * 512, rem))
            for (W, OFW, slab, slabw, f32out) in segs:
                for of0 in range(0, OFW, 1024):
                    sw = min(1024, OFW - of0)
                    wp = (of0 // 1024) % 2
                    wl = 0
                    for kc in range(16):
                        wl = DMAG(wbuf[wp][:, kc * sw:(kc + 1) * sw],
                                  W[kc * 128:(kc + 1) * 128, of0:of0 + sw],
                                  waits=wbuf_war[wp])
                    wbuf_war[wp] = []
                    for of8 in range(max(1, sw // 128)):
                        m = min(128, sw)
                        ofc = of0 + of8 * 128
                        sp = (ofc // 128) % 2
                        stgw = stgab_war[sp] if f32out else stg_war[sp]
                        sbuf_stg = stgab[sp] if f32out else stg[sp]
                        for (ts0, tsw) in tsubs:
                            bk = bkctr[0] % 4
                            bkctr[0] += 1
                            pbk = pb[bk]
                            for kc in range(16):
                                wts = []
                                if kc == 0:
                                    wts = [("dlg", wl), ("dls", tl)] + bank_war[bk]
                                    bank_war[bk] = []
                                em.op("tensor",
                                      lambda h, o=pbk[0:m, 0:tsw],
                                      l=wbuf[wp][:, kc * sw + of8 * 128: kc * sw + of8 * 128 + m],
                                      r=xng[:, kc * tw + ts0: kc * tw + ts0 + tsw],
                                      s=(kc == 0), e=(kc == 15):
                                      h.matmul(o, l, r, start=s, stop=e),
                                      sem="pe" if kc == 15 else None, waits=wts)
                            mmv = em.val("pe")
                            if bkctr[0] % 2 == 0:
                                ev = em.op("vector",
                                           lambda h, o=sbuf_stg[0:m, ts0:ts0 + tsw], i_=pbk[0:m, 0:tsw]:
                                           h.tensor_copy(o, i_), sem="dv",
                                           waits=[("pe", mmv)] + stgw)
                                bank_war[bk] = [("dv", ev)]
                            else:
                                ev = em.op("scalar",
                                           lambda h, o=sbuf_stg[0:m, ts0:ts0 + tsw], i_=pbk[0:m, 0:tsw]:
                                           h.activation(o, i_, AF.Copy), sem="ac",
                                           waits=[("pe", mmv)] + stgw)
                                bank_war[bk] = [("ac", ev)]
                            stgw.clear()
                        ofr = (ofc // 128) if not f32out else 0
                        dst = slab[ofc:ofc + m, slabw[0]:slabw[0] + tw] if f32out else \
                            slab[ofc:ofc + m, slabw[0]:slabw[0] + tw]
                        wv = [("dv", em.val("dv")), ("ac", em.val("ac"))]
                        st = DMAS(dst, sbuf_stg[0:m, 0:tw], waits=wv)
                        if f32out:
                            stgab_war[sp] = [("dls", st)]
                        else:
                            stg_war[sp] = [("dls", st)]
                    wbuf_war[wp] = [("pe", em.val("pe"))]
            xng_war[0] = [("pe", em.val("pe"))]

        for tg in range(2):
            t0 = tg * 2048
            proj_pass(tg, t0, 2048, [
                (Wab, 32, abz, (t0,), True),
                (Wk, 2048, kr, (t0,), False),
                (Wv, NV, vr, (t0,), False),
            ])
        proj_pass(2, QT0, QTN, [(Wq, 2048, qr, (0,), False)])
        em.wait("sync", "dls", em.val("dls"))
        em.barrier("proj")
        arena.reset()

    if NPHASES >= 3:
        Ab = arena.alloc("sc_a", [16, T], F32)
        Bb = arena.alloc("sc_b", [16, T], F32)
        Sb = arena.alloc("sc_s", [16, T], F32)
        Lb = arena.alloc("sc_l", [16, T], F32)
        CMb = arena.alloc("sc_cm", [16, T], F32)
        LCb = arena.alloc("sc_lc", [16, T], F32)
        sacl = arena.alloc("sc_acl", [16, NCH], F32)

        l1 = DMAS(Ab[:], abz[0:16, :])
        l2 = DMAS(Bb[:], abz[16:32, :])
        l3 = DMAS(CMb[:], cm16)
        l4 = DMAS(LCb[:], lncm16)

        def ACTOP(dst, src, fn, sc=1.0, bi=0.0, waits=(), accum=None):
            return em.op("scalar", lambda h, o=dst, i_=src, f=fn, s=sc, b=bi, a=accum:
                         h.activation(o, i_, f, scale=s, bias=b, accum_out=a),
                         sem="ac", waits=waits)

        def logsig(dst, z, wz):
            ACTOP(Sb[:], z[:], AF.Abs, waits=wz)
            ACTOP(Sb[:], Sb[:], AF.Exp, sc=-1.0)
            a1 = ACTOP(Sb[:], Sb[:], AF.Ln, bi=1.0)
            em.op("vector", lambda h, o=dst[:], i_=z[:]: h.tensor_scalar_min(o, i_, 0.0),
                  sem="dv", waits=[("ac", a1)])
            return em.op("vector", lambda h, o=dst[:], a=dst[:], b=Sb[:]:
                         h.tensor_sub(o, a, b), sem="dv")

        logsig(Lb, Ab, [("dls", l1)])
        d_la = em.val("dv")
        logsig(Ab, Bb, [("dls", l2), ("dv", d_la)])
        a_sg = ACTOP(Sb[:], Bb[:], AF.Sigmoid, waits=[("dv", em.val("dv"))])
        em.op("vector", lambda h: h.tensor_mul(Bb[:], Sb[:], CMb[:]), sem="dv",
              waits=[("ac", a_sg), ("dls", l3)])
        em.op("vector", lambda h: h.tensor_add(Ab[:], Ab[:], LCb[:]), sem="dv",
              waits=[("dls", l4)])
        for j in range(NCH):
            em.op("vector", lambda h, j=j: h.tensor_tensor_scan(
                CMb[:, j * C:(j + 1) * C], Lb[:, j * C:(j + 1) * C], zrow[:],
                0.0, ALU.add, ALU.add), sem="dv")
        em.op("vector", lambda h: h.tensor_scalar_mul(LCb[:], CMb[:], -1.0), sem="dv")
        d_ca = em.val("dv")
        em.op("vector", lambda h: h.tensor_add(Sb[:], CMb[:], Ab[:]), sem="dv")
        d_wr = em.val("dv")
        a_naw = ACTOP(Ab[:], Sb[:], AF.Exp, waits=[("dv", d_wr)])
        em.op("vector", lambda h: h.tensor_scalar_mul(Ab[:], Ab[:], -1.0), sem="dv",
              waits=[("ac", a_naw)])
        for j in range(NCH):
            em.op("vector", lambda h, j=j: h.tensor_single_scalar(
                Lb[:, j * C:(j + 1) * C], CMb[:, j * C:(j + 1) * C],
                CMb[:, j * C + C - 1:j * C + C], ALU.subtract),
                sem="dv", waits=[("dv", d_ca)])
        d_rp = em.val("dv")
        a_rac = ACTOP(Lb[:], Lb[:], AF.Exp, sc=-1.0, waits=[("dv", d_rp)])
        a_acl = ACTOP(sacl[:], CMb[:, C - 1::C], AF.Exp, waits=[("dv", d_ca)])
        st1 = DMAS(wrow_d, Sb[:], waits=[("dv", em.val("dv"))])
        st2 = DMAS(carow_d, CMb[:])
        st3 = DMAS(aclast_d, sacl[:], waits=[("ac", a_acl)])
        d_all = em.val("dv"); a_all = em.val("ac")
        tm_war = []
        for j in range(NCH):
            srcs = [LCb, Bb, Ab, Lb]
            for r in range(4):
                w = ([("dv", d_all), ("ac", a_all)] + tm_war) if r == 0 else ()
                em.op("tensor", lambda h, s=srcs[r], j=j, r=r:
                      h.transpose(pb[4][0:128, r * 16:(r + 1) * 16],
                                  s[:, j * C:(j + 1) * C], sIf[0:16, 0:16]),
                      sem="pe" if r == 3 else None, waits=w)
            pev = em.val("pe")
            ev = em.op("vector", lambda h, j=j: h.tensor_copy(
                tmsl[:, j * 64:(j + 1) * 64], pb[4][0:128, 0:64]),
                sem="dv", waits=[("pe", pev)])
            tm_war = [("dv", ev)]
        if DEBUG_OUT:
            DMAS(tmsl_d, tmsl[:], waits=[("dv", em.val("dv"))])
        em.wait("sync", "dls", em.val("dls"))
        em.barrier("scal")
        arena.reset()

    ctx = dict(locals())
    if NPHASES >= 4:
        emit_mix(ctx)
    if NPHASES >= 5:
        emit_wo(ctx)
    if NPHASES >= 6:
        emit_ffn(ctx)

    for s in ("dls", "dlg", "dla"):
        em.wait("sync", s, em.val(s))
    em.barrier("fin")

    blk = stk.enter_context(nc.Block())
    blk.tensor(lambda h: em.play("tensor", h))
    blk.vector(lambda h: em.play("vector", h))
    blk.scalar(lambda h: em.play("scalar", h))
    blk.gpsimd(lambda h: em.play("gpsimd", h))
    blk.sync(lambda h: em.play("sync", h))
    stk.close()
    return nc


def emit_mix(ctx):
    em = ctx["em"]; arena = ctx["arena"]
    pb = ctx["pb"]; pbE = ctx["pbE"]; pT = ctx["pT"]
    kr, vr, qr = ctx["kr"], ctx["vr"], ctx["qr"]
    wrow_d, carow_d, aclast_d = ctx["wrow_d"], ctx["carow_d"], ctx["aclast_d"]
    outs_d = ctx["outs_d"]; tmsl = ctx["tmsl"]
    sIbf, sIf = ctx["sIbf"], ctx["sIf"]
    sMaskW, sMaskM = ctx["sMaskW"], ctx["sMaskM"]
    sOneRow, sOneRowB, sTinyB = ctx["sOneRow"], ctx["sOneRowB"], ctx["sTinyB"]
    sOneColB = ctx["sOneColB"]; sI2 = ctx["sI2"]
    qcw, qcb, kcw, kcb, vcw, vcb = (ctx[k] for k in ("qcw", "qcb", "kcw", "kcb", "vcw", "vcb"))
    DMAS, DMAG = ctx["DMAS"], ctx["DMAG"]
    PAD = 4

    slabK = arena.alloc("mxK", [128, PAD + T], BF16)
    slabV0 = arena.alloc("mxV0", [128, PAD + T], BF16)
    slabV1 = arena.alloc("mxV1", [128, PAD + T], BF16)
    slabQ = arena.alloc("mxQ", [128, PAD + QTN], BF16)
    q2 = arena.alloc("mxQ2", [128, QTN], BF16)
    sqt = arena.alloc("mxSq", [128, 512], BF16)
    krnb = arena.alloc("mxKrn", [1, T], BF16)
    wrowh = arena.alloc("mxWrow", [1, T], F32)
    mrow = arena.alloc("mxMrow", [1, QTN], F32)
    lnscr = arena.alloc("mxLns", [1, 1024], F32)
    amq = arena.alloc("mxAmq", [1, QTN], BF16)
    aclr = arena.alloc("mxAcl", [1, NCH], F32)
    ACrep = arena.alloc("mxACr", [128, NCH], F32)
    cwq = arena.alloc("mxCwq", [128, 4], F32); cbq = arena.alloc("mxCbq", [128, 1], F32)
    cwk = arena.alloc("mxCwk", [128, 4], F32); cbk = arena.alloc("mxCbk", [128, 1], F32)
    cwv0 = arena.alloc("mxCv0", [128, 4], F32); cbv0 = arena.alloc("mxCb0", [128, 1], F32)
    cwv1 = arena.alloc("mxCv1", [128, 4], F32); cbv1 = arena.alloc("mxCb1", [128, 1], F32)
    dg = [arena.alloc(f"mxDg{i}", [128, 128], BF16) for i in range(16)]
    GLAG = 3
    WtMt2 = [arena.alloc(f"mxWtMt{i}", [128, 512], BF16) for i in range(GLAG)]
    Tnv2 = [arena.alloc(f"mxTnv{i}", [128, 256], BF16) for i in range(GLAG)]
    bV2 = [arena.alloc(f"mxbV{i}", [128, 512], BF16) for i in range(GLAG)]
    Ktm2 = [arena.alloc(f"mxKtm{i}", [128, 256], BF16) for i in range(GLAG)]
    W02g = [arena.alloc(f"mxW02{i}", [128, 256], BF16) for i in range(2)]
    PQ2g = [[arena.alloc(f"mxPQ{i}_{p}", [128, 512], BF16) for p in range(2)] for i in range(2)]
    P62g = [arena.alloc(f"mxP62{i}", [128, 256], BF16) for i in range(2)]
    PT2g = [[arena.alloc(f"mxPT{i}_{p}", [128, 256], BF16) for p in range(2)] for i in range(2)]
    LWM2 = arena.alloc("mxLWM", [128, 512], F32)
    AWK2 = [arena.alloc(f"mxAWK{i}", [128, 256], BF16) for i in range(GLAG)]
    X1b2 = [arena.alloc(f"mxX1{i}", [128, 256], BF16) for i in range(GLAG)]
    X1T2 = [arena.alloc(f"mxXT{i}", [128, 256], BF16) for i in range(GLAG)]
    Uloc2 = [arena.alloc(f"mxUL{i}", [128, 512], BF16) for i in range(GLAG)]
    B2 = [arena.alloc(f"mxB{i}", [128, 512], BF16) for i in range(GLAG)]
    MT2 = [arena.alloc(f"mxMT{i}", [128, 256], BF16) for i in range(GLAG)]
    Ut = arena.alloc("mxU", [128, 256], BF16)
    St = [arena.alloc(f"mxS{i}", [128, 256], BF16) for i in range(2)]
    outs_s = arena.alloc("mxOuts", [128, 2 * OWN], BF16)

    def MM(bank_ap, lhsT, rhs, st, sp, sem="pe", waits=()):
        return em.op("tensor", lambda h, o=bank_ap, l=lhsT, r=rhs, s=st, e=sp:
                     h.matmul(o, l, r, start=s, stop=e), sem=sem, waits=waits)

    def TRANS(bank_ap, in_ap, ident, waits=()):
        return em.op("tensor", lambda h, o=bank_ap, i_=in_ap, d=ident:
                     h.transpose(o, i_, d), sem="pe", waits=waits)

    def ACOPY(dst, src, waits=()):
        return em.op("scalar", lambda h, o=dst, i_=src:
                     h.activation(o, i_, AF.Copy), sem="ac", waits=waits)

    def AEXP(dst, src, bias, waits=(), scale=1.0):
        return em.op("scalar", lambda h, o=dst, i_=src, b=bias, s=scale:
                     h.activation(o, i_, AF.Exp, bias=b, scale=s), sem="ac", waits=waits)

    prev_stores = []
    slab_war = []
    for hd in range(HK):
        lw = list(slab_war)
        lK = DMAS(slabK[:, PAD:], kr[hd * 128:(hd + 1) * 128, :], waits=lw)
        lV0 = DMAS(slabV0[:, PAD:], vr[hd * 256:hd * 256 + 128, :], waits=lw)
        lV1 = DMAS(slabV1[:, PAD:], vr[hd * 256 + 128:hd * 256 + 256, :], waits=lw)
        lQ = DMAS(slabQ[:, PAD:], qr[hd * 128:(hd + 1) * 128, :], waits=lw)
        lrw = DMAS(wrowh[:], wrow_d[hd:hd + 1, :], waits=lw)
        lmr = DMAG(mrow[:], carow_d[hd:hd + 1, QT0:], waits=lw)
        lacl = DMAG(aclr[:], aclast_d[hd:hd + 1, :])
        lcw = DMAG(cwq[:], qcw[hd * 128:(hd + 1) * 128, :])
        DMAG(cbq[:], qcb[hd * 128:(hd + 1) * 128, :])
        DMAG(cwk[:], kcw[hd * 128:(hd + 1) * 128, :])
        DMAG(cbk[:], kcb[hd * 128:(hd + 1) * 128, :])
        DMAG(cwv0[:], vcw[hd * 256:hd * 256 + 128, :])
        DMAG(cbv0[:], vcb[hd * 256:hd * 256 + 128, :])
        DMAG(cwv1[:], vcw[hd * 256 + 128:hd * 256 + 256, :])
        lcb = DMAG(cbv1[:], vcb[hd * 256 + 128:hd * 256 + 256, :])
        z1 = em.op("vector", lambda h: h.memset(slabK[:, 0:PAD], 0.0), sem="dv",
                   waits=[("dls", lK)])
        em.op("vector", lambda h: h.memset(slabV0[:, 0:PAD], 0.0), sem="dv")
        em.op("vector", lambda h: h.memset(slabV1[:, 0:PAD], 0.0), sem="dv")
        em.op("vector", lambda h: h.memset(slabQ[:, 0:PAD], 0.0), sem="dv")

        slabs = [(slabK, T, cwk, cbk, 0), (slabV0, T, cwv0, cbv0, 1),
                 (slabV1, T, cwv1, cbv1, 2), (slabQ, QTN, cwq, cbq, 3)]
        for si, (slab, ncols, cw, cb, di) in enumerate(slabs):
            for j in range(4):
                em.op("vector", lambda h, j=j, c=cw, di=di: h.tensor_scalar_mul(
                    dg[di * 4 + j][:], sIbf[:], c[:, j:j + 1]), sem="dv",
                    waits=[("dlg", lcb), ("pe", em.val("pe"))])
        dtap = em.val("dv")
        a_prep0 = em.val("ac")
        steps = []
        for si, (slab, ncols, cw, cb, di) in enumerate(slabs):
            tiles = []
            t0v = 0
            while t0v < ncols:
                tw = min(512, ncols - t0v)
                tiles.append((si, t0v, tw))
                t0v += tw
            steps.append(list(reversed(tiles)))
        order = []
        mx = max(len(s) for s in steps)
        for i in range(mx):
            for s in steps:
                if i < len(s):
                    order.append(s[i])
        silu_g = {}
        for n_, (si, t0v, tw) in enumerate(order):
            slab, ncols, cw, cb, di = slabs[si]
            bki = n_ % 4
            bk = pb[bki]
            for j in range(4):
                MM(bk[0:128, 0:tw], dg[di * 4 + j][:],
                   slab[:, PAD + t0v - 3 + j:PAD + t0v - 3 + j + tw],
                   j == 0, j == 3,
                   sem="pe" if j == 3 else None,
                   waits=[("dv", dtap), ("ac", max(a_prep0, silu_g.get(bki, 0)))] if j == 0 else ())
            pev = em.val("pe")
            silu_g[bki] = em.op("scalar", lambda h, o=slab[:, PAD + t0v:PAD + t0v + tw],
                                i_=bk[0:128, 0:tw], b=cb:
                                h.activation(o, i_, AF.Silu, bias=b[:]), sem="ac",
                                waits=[("pe", pev), ("dlg", lcb)])
        a_conv_q = em.val("ac")

        for rnd in range(4):
            for half in range(2):
                ti = rnd * 2 + half
                a_sq = em.op("scalar", lambda h, o=sqt[:], i_=slabK[:, PAD + ti * 512:PAD + (ti + 1) * 512]:
                             h.activation(o, i_, AF.Square), sem="ac",
                             waits=[("dv", em.val("dv")), ("pe", em.val("pe"))])
                MM(pbE[0:1, half * 512:half * 512 + 512], sOneColB[:], sqt[:], True, False,
                   sem=None, waits=[("ac", a_sq)])
                MM(pbE[0:1, half * 512:half * 512 + 512], sTinyB[:], sOneRowB[:], False, True,
                   sem="pe")
            pev = em.val("pe")
            a_ln = em.op("scalar", lambda h, o=lnscr[:], i_=pbE[0:1, 0:1024]:
                         h.activation(o, i_, AF.Ln), sem="ac", waits=[("pe", pev)])
            em.op("scalar", lambda h, o=krnb[0:1, rnd * 1024:(rnd + 1) * 1024], i_=lnscr[:]:
                  h.activation(o, i_, AF.Exp, scale=-0.5), sem="ac")
        a_krn = em.val("ac")
        for ti in range(8):
            mmv = MM(pb[2][0:128, 0:512], sOneRowB[0:1, 0:128],
                     krnb[0:1, ti * 512:(ti + 1) * 512], True, True,
                     waits=[("ac", a_krn), ("dv", em.val("dv"))])
            em.op("vector", lambda h, ti=ti, o=slabK[:, PAD + ti * 512:PAD + (ti + 1) * 512],
                  a=slabK[:, PAD + ti * 512:PAD + (ti + 1) * 512], b=pb[2][0:128, 0:512]:
                  h.tensor_mul(o, a, b), sem="dv", waits=[("pe", mmv)])
        d_khat = em.val("dv")

        qtiles = [(0, 512), (512, 512), (1024, 512), (1536, 512), (2048, 128)]
        for rnd, grp in enumerate([qtiles[0:2], qtiles[2:4], qtiles[4:5]]):
            for gi, (t0v, tw) in enumerate(grp):
                a_sq = em.op("scalar", lambda h, o=sqt[0:128, 0:tw],
                             i_=slabQ[:, PAD + t0v:PAD + t0v + tw], tw=tw:
                             h.activation(o, i_, AF.Square), sem="ac",
                             waits=[("dv", em.val("dv")), ("pe", em.val("pe"))])
                MM(pbE[0:1, gi * 512:gi * 512 + tw], sOneColB[:], sqt[0:128, 0:tw], True, False,
                   sem=None, waits=[("ac", a_sq)])
                MM(pbE[0:1, gi * 512:gi * 512 + tw], sTinyB[:], sOneRowB[0:1, 0:tw], False, True,
                   sem="pe")
            gw = sum(tw for _, tw in grp)
            g0 = grp[0][0]
            pev = em.val("pe")
            a_ln = em.op("scalar", lambda h, o=lnscr[0:1, 0:gw], i_=pbE[0:1, 0:gw], gw=gw:
                         h.activation(o, i_, AF.Ln), sem="ac", waits=[("pe", pev)])
            em.op("vector", lambda h, o=mrow[0:1, g0:g0 + gw], i0=lnscr[0:1, 0:gw],
                  i1=mrow[0:1, g0:g0 + gw], gw=gw:
                  h.scalar_tensor_tensor(o, i0, -0.5, i1, ALU.mult, ALU.add),
                  sem="dv", waits=[("ac", a_ln), ("dlg", lmr)])
        d_mrow = em.val("dv")
        a_amq = em.op("scalar", lambda h: h.activation(amq[:], mrow[:], AF.Exp),
                      sem="ac", waits=[("dv", d_mrow)])
        for (t0v, tw) in qtiles:
            mmv = MM(pb[2][0:128, 0:tw], sOneRowB[0:1, 0:128], amq[0:1, t0v:t0v + tw],
                     True, True, waits=[("ac", a_amq), ("dv", em.val("dv"))])
            em.op("vector", lambda h, t0v=t0v, tw=tw, o=q2[:, t0v:t0v + tw],
                  a=slabQ[:, PAD + t0v:PAD + t0v + tw], b=pb[2][0:128, 0:tw]:
                  h.tensor_mul(o, a, b), sem="dv", waits=[("pe", mmv)])
        mmv = MM(pb[2][0:128, 0:NCH], sOneRow[:], aclr[:], True, True,
                 waits=[("dlg", lacl), ("dv", em.val("dv"))])
        d_acr = em.op("vector", lambda h: h.tensor_copy(ACrep[:], pb[2][0:128, 0:NCH]),
                      sem="dv", waits=[("pe", mmv)])
        em.op("vector", lambda h: h.memset(St[0][:], 0.0), sem="dv",
              waits=prev_stores)
        em.op("vector", lambda h: h.memset(outs_s[:], 0.0), sem="dv")
        d_prep = em.val("dv")

        a_prep = em.val("ac")
        hs = {
            "d_chain_prev": d_prep, "g_pT": [("ac", a_prep)],
            "g_p0": [("ac", a_prep)], "g_p1": [("ac", a_prep)],
            "g_S0": [("ac", a_prep)], "g_S1": [("ac", a_prep)],
            "g_U0": [("ac", a_prep)], "g_U1": [("ac", a_prep)],
            "d_lwm_war": d_prep,
            "sq_bank_g": {2: [("dv", d_prep)], 3: [("dv", d_prep)], 4: [("dv", d_prep)]},
        }
        mstate = {}

        def col(j, r):
            return tmsl[:, j * 64 + 16 * r + hd:j * 64 + 16 * r + hd + 1]

        def mach_s0(g):
            gi = g % GLAG
            jA, jB = 2 * g, 2 * g + 1
            wv = [("dv", d_khat), ("ac", a_conv_q), ("dv", em.val("dv"))] + hs["g_pT"]
            tkv = 0
            for (j, base) in ((jA, 0), (jB, 512)):
                jt = PAD + j * 128
                TRANS(pT[:, base + 128:base + 256], slabV0[:, jt:jt + 128], sIbf[:],
                      waits=wv); wv = ()
                TRANS(pT[:, base + 256:base + 384], slabV1[:, jt:jt + 128], sIbf[:])
                tkv = TRANS(pT[:, base + 384:base + 512], slabK[:, jt:jt + 128], sIbf[:])
            d_bv = d_ktm = 0
            for (j, base, cp) in ((jA, 0, 0), (jB, 512, 1)):
                d_bv = em.op("vector", lambda h, b=col(j, 1), base=base, cp=cp, gi=gi:
                             h.tensor_scalar_mul(bV2[gi][:, cp * 256:cp * 256 + 256],
                                                 pT[:, base + 128:base + 384], b),
                             sem="dv", waits=[("pe", tkv)])
                em.op("vector", lambda h, r=col(j, 3), base=base, cp=cp, gi=gi:
                      h.tensor_scalar_mul(Ktm2[gi][:, cp * 128:cp * 128 + 128],
                                          pT[:, base + 384:base + 512], r), sem="dv")
                d_ktm = em.op("vector", lambda h, n=col(j, 2), base=base, cp=cp, gi=gi:
                              h.tensor_scalar_mul(AWK2[gi][:, cp * 128:cp * 128 + 128],
                                                  pT[:, base + 384:base + 512], n), sem="dv")
            mstate[g] = dict(d_bv=d_bv, d_ktm=d_ktm)

        def mach_s1(g):
            gi = g % GLAG
            jA, jB = 2 * g, 2 * g + 1
            is_out = jA >= OCH0
            kk0 = 0
            for (j, base) in ((jA, 0), (jB, 256)):
                jt = PAD + j * 128
                w = ([("dv", em.val("dv"))] + hs["g_p0"]) if base == 0 else ()
                kk0 = MM(pb[0][0:128, base:base + 128], slabK[:, jt:jt + 128],
                         slabK[:, jt:jt + 128], True, True, waits=w)
                if is_out:
                    qloc = j * 128 - QT0
                    kk0 = MM(pb[0][0:128, base + 128:base + 256], slabK[:, jt:jt + 128],
                             slabQ[:, PAD + qloc:PAD + qloc + 128], True, True)
            ew = 0
            for (j, base) in ((jA, 0), (jB, 256)):
                w = hs["g_p1"] if base == 0 else ()
                MM(pb[1][0:128, base:base + 128], sMaskW[:], sIf[:], True, False,
                   sem=None, waits=w)
                ew = MM(pb[1][0:128, base:base + 128], sOneRow[:],
                        wrowh[0:1, j * 128:(j + 1) * 128], False, True,
                        waits=[("dls", lrw)])
                if is_out:
                    qloc = j * 128 - QT0
                    MM(pb[1][0:128, base + 128:base + 256], sMaskM[:], sIf[:], True, False,
                       sem=None)
                    ew = MM(pb[1][0:128, base + 128:base + 256], sOneRow[:],
                            mrow[0:1, qloc:qloc + 128], False, True)
            wid = 256 if is_out else 128
            a_lwm = 0
            for (j, base) in ((jA, 0), (jB, 256)):
                a_lwm = AEXP(LWM2[0:128, base:base + wid], pb[1][0:128, base:base + wid],
                             col(j, 0), waits=[("pe", ew), ("dv", hs["d_lwm_war"])])
            hs["g_p1"] = [("ac", a_lwm)]
            if is_out:
                d_wtmt = em.op("vector", lambda h, gi=gi: h.tensor_mul(
                    WtMt2[gi][:], LWM2[:], pb[0][0:128, 0:512]),
                    sem="dv", waits=[("ac", a_lwm), ("pe", kk0)])
            else:
                d_wtmt = em.op("vector", lambda h, gi=gi: h.tensor_mul(
                    WtMt2[gi].rearrange("p (c x) -> p c x", x=128)[:, 0::2, :],
                    LWM2.rearrange("p (c x) -> p c x", x=128)[:, 0::2, :],
                    pb[0].rearrange("p (c x) -> p c x", x=128)[:, 0::2, :]),
                    sem="dv", waits=[("ac", a_lwm), ("pe", kk0)])
            hs["d_lwm_war"] = d_wtmt
            hs["g_p0"] = [("dv", d_wtmt)]
            mstate[g]["d_wtmt"] = d_wtmt
            if g == NCH // 2 - 1:
                hs["slab_free"] = [("pe", em.val("pe")), ("dv", em.val("dv"))]

        def mach_s2(g):
            gi = g % GLAG
            ms = mstate[g]
            W02 = W02g[g % 2]
            PT2 = PT2g[g % 2]
            d_wtmt, d_ktm = ms["d_wtmt"], ms["d_ktm"]
            tw0 = 0
            for (cp, base) in ((0, 0), (1, 512)):
                tw0 = TRANS(pT[:, base:base + 128], WtMt2[gi][:, cp * 256:cp * 256 + 128],
                            sIbf[:], waits=[("dv", d_wtmt), ("dv", d_ktm)] if cp == 0 else ())
            a_w0 = em.op("scalar", lambda h, W02=W02: h.activation(
                W02.rearrange("p (c x) -> p c x", x=128)[:, :, :],
                pT.rearrange("p (c x) -> p c x", x=128)[:, 0::4, :], AF.Copy),
                sem="ac", waits=[("pe", tw0)])
            hs["g_pT"] = [("ac", a_w0)]
            d_pt0 = em.op("vector", lambda h, gi=gi, PT2=PT2: h.tensor_sub(
                PT2[0].rearrange("p (c x) -> p c x", x=128)[:, :, :],
                sI2.rearrange("p (c x) -> p c x", x=128)[:, :, :],
                WtMt2[gi].rearrange("p (c x) -> p c x", x=128)[:, 0::2, :]),
                sem="dv", waits=[("dv", d_wtmt)])
            ms.update(PA=(W02[:, 0:128], WtMt2[gi][:, 0:128]),
                      PB=(W02[:, 128:256], WtMt2[gi][:, 256:384]),
                      sq_ev=[("ac", a_w0), ("dv", d_wtmt)],
                      PTcur=PT2[0][:], d_pt=d_pt0)

        def mach_lvl(g, k):
            gi = g % GLAG
            PQ2 = PQ2g[g % 2]
            P62 = P62g[g % 2]
            PT2 = PT2g[g % 2]
            ms = mstate[g]
            PA, PB = ms["PA"], ms["PB"]
            sq_ev = ms["sq_ev"]
            bank_g = hs["sq_bank_g"]
            PTcur = ms["PTcur"]
            d_pt = ms["d_pt"]
            bki = 2 + (k % 2)
            bk = pb[bki]
            if k < 6:
                MM(bk[0:128, 0:128], PA[1], PA[0], True, True, sem=None,
                   waits=sq_ev + bank_g[bki])
                MM(bk[0:128, 128:256], PA[0], PA[1], True, True, sem=None)
                MM(bk[0:128, 256:384], PB[1], PB[0], True, True, sem=None)
                mmq = MM(bk[0:128, 384:512], PB[0], PB[1], True, True)
                dst = PQ2[k % 2]
                a_ev = ACOPY(dst[:], bk[0:128, 0:512], waits=[("pe", mmq)])
                PA = (dst[:, 0:128], dst[:, 128:256])
                PB = (dst[:, 256:384], dst[:, 384:512])
                PkA, PkB = PA[0], PB[0]
            else:
                MM(bk[0:128, 0:128], PA[1], PA[0], True, True, sem=None,
                   waits=sq_ev + bank_g[bki])
                mmq = MM(bk[0:128, 128:256], PB[1], PB[0], True, True)
                a_ev = ACOPY(P62[:], bk[0:128, 0:256], waits=[("pe", mmq)])
                PkA, PkB = P62[:, 0:128], P62[:, 128:256]
            bank_g[bki] = [("ac", a_ev)]
            MM(pb[4][0:128, 0:128], PkA, PTcur[:, 0:128], True, True, sem=None,
               waits=[("ac", a_ev), ("dv", d_pt)] + bank_g[4])
            bank_g[4] = []
            mmp = MM(pb[4][0:128, 128:256], PkB, PTcur[:, 128:256], True, True)
            PTnew = Tnv2[gi][:] if k == 6 else PT2[k % 2][:]
            d_pt = em.op("vector", lambda h, o=PTnew, a=pb[4][0:128, 0:256], b=PTcur:
                         h.tensor_add(o, b, a), sem="dv", waits=[("pe", mmp)])
            bank_g[4] = [("dv", d_pt)]
            ms.update(PA=PA, PB=PB, sq_ev=[("ac", a_ev)], PTcur=PTnew, d_pt=d_pt)

        def tail_x1(g):
            gi = g % GLAG
            ms = mstate[g]
            is_out = 2 * g >= OCH0
            dz = em.val("dv")
            x1 = 0
            for cp in range(2):
                x1 = MM(pb[1][0:128, cp * 128:cp * 128 + 128],
                        Tnv2[gi][:, cp * 128:cp * 128 + 128],
                        AWK2[gi][:, cp * 128:cp * 128 + 128], True, True,
                        waits=([("dv", ms["d_pt"]), ("dv", dz)] + hs["g_p1"]) if cp == 0 else ())
                if is_out:
                    x1 = MM(pb[1][0:128, 256 + cp * 128:256 + cp * 128 + 128],
                            AWK2[gi][:, cp * 128:cp * 128 + 128],
                            Tnv2[gi][:, cp * 128:cp * 128 + 128], True, True)
            a_x1 = ACOPY(X1b2[gi][:], pb[1][0:128, 0:256], waits=[("pe", x1)])
            if is_out:
                a_x1 = ACOPY(X1T2[gi][:], pb[1][0:128, 256:512])
            ms["a_x1"] = a_x1

        def tail_ul(g):
            gi = g % GLAG
            ms = mstate[g]
            ul = 0
            for cp in range(2):
                ul = MM(pb[1][0:128, cp * 256:cp * 256 + 256],
                        Tnv2[gi][:, cp * 128:cp * 128 + 128],
                        bV2[gi][:, cp * 256:cp * 256 + 256], True, True,
                        waits=[("ac", ms["a_x1"])] if cp == 0 else ())
            ms["a_ul"] = ACOPY(Uloc2[gi][:], pb[1][0:128, 0:512], waits=[("pe", ul)])

        def tail_b(g):
            gi = g % GLAG
            ms = mstate[g]
            bb = 0
            for cp in range(2):
                bb = MM(pb[1][0:128, cp * 256:cp * 256 + 256],
                        Ktm2[gi][:, cp * 128:cp * 128 + 128],
                        Uloc2[gi][:, cp * 256:cp * 256 + 256], True, True,
                        waits=[("ac", ms["a_ul"])] if cp == 0 else ())
            ms["a_b"] = ACOPY(B2[gi][:], pb[1][0:128, 0:512], waits=[("pe", bb)])

        def tail_mt(g):
            gi = g % GLAG
            ms = mstate[g]
            mt = 0
            for cp in range(2):
                mt = MM(pb[1][0:128, cp * 128:cp * 128 + 128],
                        X1b2[gi][:, cp * 128:cp * 128 + 128],
                        Ktm2[gi][:, cp * 128:cp * 128 + 128], True, True,
                        waits=[("ac", ms["a_b"])] if cp == 0 else ())
            d_mt = 0
            for (jj, cp) in ((2 * g, 0), (2 * g + 1, 1)):
                d_mt = em.op("vector", lambda h, jj=jj, cp=cp, gi=gi:
                             h.scalar_tensor_tensor(
                                 MT2[gi][:, cp * 128:cp * 128 + 128], sIbf[:],
                                 ACrep[:, jj:jj + 1],
                                 pb[1][0:128, cp * 128:cp * 128 + 128],
                                 ALU.mult, ALU.add),
                             sem="dv", waits=[("pe", mt), ("dv", d_acr)])
            hs["g_p1"] = [("dv", d_mt)]
            ms["d_mt"] = d_mt

        def stages_of(g):
            st = [lambda g=g: mach_s0(g), lambda g=g: mach_s1(g), lambda g=g: mach_s2(g)]
            st += [lambda g=g, k=k: mach_lvl(g, k) for k in range(1, 7)]
            st += [lambda g=g: tail_x1(g), lambda g=g: tail_ul(g),
                   lambda g=g: tail_b(g), lambda g=g: tail_mt(g)]
            return st

        def chain(j):
            gi = (j // 2) % GLAG
            cp = j % 2
            cur = j % 2
            is_out = j >= OCH0
            qloc = j * 128 - QT0
            ms = mstate[j // 2]
            p = j % 2
            Sb = pbE[0:128, 0:256] if p == 0 else pbE[0:128, 512:768]
            Ub = pbE[0:128, 768:1024] if p == 0 else pbE[0:128, 256:512]
            gS = hs["g_S0"] if p == 0 else hs["g_S1"]
            gX = hs["g_U1"] if p == 0 else hs["g_U0"]
            s1 = MM(Sb, MT2[gi][:, cp * 128:cp * 128 + 128], St[cur][:], True, True,
                    waits=[("dv", hs["d_chain_prev"]), ("dv", ms["d_mt"])] + gS + gX)
            d_S = em.op("vector", lambda h, gi=gi, cp=cp, cur=cur, Sb=Sb:
                        h.tensor_add(St[1 - cur][:], B2[gi][:, cp * 256:cp * 256 + 256], Sb),
                        sem="dv", waits=[("pe", s1), ("ac", ms["a_b"])])
            hs["d_chain_prev"] = d_S
            if p == 0:
                hs["g_S0"] = [("dv", d_S)]
            else:
                hs["g_S1"] = [("dv", d_S)]

            if is_out:
                gU = hs["g_U0"] if p == 0 else hs["g_U1"]
                u2 = MM(Ub, X1T2[gi][:, cp * 128:cp * 128 + 128], St[cur][:], True, True,
                        waits=[("ac", ms["a_x1"])] + gU)
                d_U = em.op("vector", lambda h, gi=gi, cp=cp, Ub=Ub: h.tensor_add(
                    Ut[:], Uloc2[gi][:, cp * 256:cp * 256 + 256], Ub),
                    sem="dv", waits=[("pe", u2)])
                ob = j - OCH0
                mt = WtMt2[gi][:, cp * 256 + 128:cp * 256 + 256]
                MM(Ub[0:128, 0:128], St[cur][:, 0:128], q2[:, qloc:qloc + 128],
                   True, False, sem=None, waits=[("dv", d_U)])
                MM(Ub[0:128, 0:128], Ut[:, 0:128], mt, False, True, sem=None)
                MM(Ub[0:128, 128:256], St[cur][:, 128:256], q2[:, qloc:qloc + 128],
                   True, False, sem=None)
                oo = MM(Ub[0:128, 128:256], Ut[:, 128:256], mt, False, True)
                a_o2 = em.op("scalar", lambda h, ob=ob, Ub=Ub: h.activation(
                    outs_s.rearrange("p (v t) -> p v t", t=OWN)[:, :, ob * 128:(ob + 1) * 128],
                    Ub.rearrange("p (v t) -> p v t", t=128)[:, 0:2, :], AF.Copy),
                    sem="ac", waits=[("pe", oo)])
                if p == 0:
                    hs["g_U0"] = [("ac", a_o2)]
                else:
                    hs["g_U1"] = [("ac", a_o2)]

        NG = NCH // 2
        pend = {g: stages_of(g) for g in range(NG)}

        def pump(g, n):
            for _ in range(n):
                if pend.get(g):
                    pend[g].pop(0)()

        pump(0, 13)
        pump(1, 6)
        for g in range(NG):
            chain(2 * g)
            a = pend.get(g + 1, [])
            b = pend.get(g + 2, [])
            na, nb = len(a), min(6, len(b))
            for i in range(max(na, nb)):
                if i < na:
                    pump(g + 1, 1)
                if i < nb:
                    pump(g + 2, 1)
            chain(2 * g + 1)
        a_out_prev = em.val("ac")

        st1 = DMAS(outs_d[:, (2 * hd) * OWN:(2 * hd + 1) * OWN], outs_s[:, 0:OWN],
                   waits=[("ac", a_out_prev), ("dv", em.val("dv"))])
        st2 = DMAS(outs_d[:, (2 * hd + 1) * OWN:(2 * hd + 2) * OWN], outs_s[:, OWN:2 * OWN])
        prev_stores = [("dls", st2)]
        slab_war = hs.get("slab_free", [("pe", em.val("pe")), ("dv", em.val("dv"))])

    em.wait("sync", "dls", em.val("dls"))
    em.barrier("mix")
    arena.reset()


def emit_wo(ctx):
    em = ctx["em"]; arena = ctx["arena"]
    pb = ctx["pb"]; pbE = ctx["pbE"]
    Wo = ctx["Wo"]; x_out = ctx["x_out"]
    outs_d, out1_d, xn2_d = ctx["outs_d"], ctx["out1_d"], ctx["xn2_d"]
    DMAS, DMAG, DMAA = ctx["DMAS"], ctx["DMAG"], ctx["DMAA"]
    colsA, colsB, epscol = ctx["colsA"], ctx["colsB"], ctx["epscol"]

    wo_s = arena.alloc("woW", [128, 32 * 1024], BF16)
    os_s = [arena.alloc("woO", [128, 32 * 512], BF16) for _ in range(2)]
    xo_s = [arena.alloc("woX", [128, 1024], F32) for _ in range(2)]
    o1 = [arena.alloc("woS", [128, 1024], F32) for _ in range(2)]
    outs3 = outs_d.rearrange("p (vc t) -> p vc t", t=OWN)

    os_war = [[], []]
    xo_war = [[], []]
    o1_war = [[], []]
    bank_g = [[], [], [], []]
    wo_war = []
    for dh in range(2):
        wlv = []
        for vc in range(32):
            wlv.append(DMAG(wo_s[:, vc * 1024:(vc + 1) * 1024],
                            Wo[vc * 128:(vc + 1) * 128, dh * 1024:(dh + 1) * 1024],
                            waits=wo_war))
        wo_war = []
        for tq in range(4):
            qp = tq % 2
            lo = DMAS(os_s[qp].rearrange("p (vc t) -> p vc t", t=512),
                      outs3[:, :, tq * 512:(tq + 1) * 512], waits=os_war[qp])
            os_war[qp] = []
            for ti in range(4):
                tch = tq * 4 + ti
                tp = tch % 2
                lx = DMAA(xo_s[tp][:], x_out[tch * 128:(tch + 1) * 128,
                                             dh * 1024:(dh + 1) * 1024],
                          waits=xo_war[tp])
                for dc in range(2):
                    bki = 2 * tp + dc
                    bk = pb[bki]
                    for vc in range(32):
                        w = [("dlg", wlv[vc])]
                        if vc == 0:
                            w += [("dls", lo)] + bank_g[bki]
                            bank_g[bki] = []
                        mmv = MM_g(em, bk[0:128, 0:512],
                                   os_s[qp][:, vc * 512 + ti * 128:vc * 512 + ti * 128 + 128],
                                   wo_s[:, vc * 1024 + dc * 512:vc * 1024 + dc * 512 + 512],
                                   vc == 0, vc == 31, w)
                    ev = em.op("vector", lambda h, o=o1[tp][:, dc * 512:(dc + 1) * 512],
                               a=bk[0:128, 0:512],
                               b=xo_s[tp][:, dc * 512:(dc + 1) * 512]:
                               h.tensor_add(o, a, b), sem="dv",
                               waits=[("pe", mmv), ("dla", lx)] + o1_war[tp])
                    o1_war[tp] = []
                    bank_g[bki] = [("dv", ev)]
                st = DMAS(out1_d[tch * 128:(tch + 1) * 128, dh * 1024:(dh + 1) * 1024],
                          o1[tp][:], waits=[("dv", em.val("dv"))])
                o1_war[tp] = [("dls", st)]
                xo_war[tp] = [("dv", em.val("dv"))]
            os_war[qp] = [("pe", em.val("pe"))]
        wo_war = [("pe", em.val("pe"))]
    em.wait("sync", "dls", em.val("dls"))
    em.barrier("wo_mm")
    arena.reset()

    xt = [arena.alloc("l2x", [128, D], F32) for _ in range(2)]
    sqs = [arena.alloc("l2q", [128, D], F32) for _ in range(2)]
    xnb = [arena.alloc("l2n", [128, D], BF16) for _ in range(2)]
    lastc = {}
    for i in range(OWN // 128):
        p = i % 2
        cols = colsA if p == 0 else colsB
        csum, cssq, cngm, cssd, cnvr, clnv, crsd = (cols[:, jj:jj+1] for jj in range(7))
        w = lastc.get(("xt", p), [])
        ld = DMAS(xt[p][:], out1_d[i*128:(i+1)*128, :], waits=w)
        em.op("scalar", lambda h, o=sqs[p][:], i_=xt[p][:], a=csum:
              h.activation(o, i_, AF.Copy, accum_out=a), sem="ac",
              waits=[("dls", ld)] + lastc.get(("cols", p), []))
        em.op("scalar", lambda h, o=sqs[p][:], i_=xt[p][:], a=cssq:
              h.activation(o, i_, AF.Square, accum_out=a), sem="ac")
        a_sq = em.val("ac")
        em.op("vector", lambda h, o=cngm, i_=csum: h.tensor_scalar_mul(o, i_, -1.0 / D),
              sem="dv", waits=[("ac", a_sq)])
        em.op("vector", lambda h, o=cssd, i_=cssq: h.tensor_scalar_mul(o, i_, 1.0 / D), sem="dv")
        em.op("vector", lambda h, o=cnvr, i0=cngm, s=cngm, i1=cssd:
              h.scalar_tensor_tensor(o, i0, s, i1, ALU.mult, ALU.subtract), sem="dv")
        d_var = em.val("dv")
        em.op("scalar", lambda h, o=clnv, i_=cnvr:
              h.activation(o, i_, AF.Ln, scale=-1.0, bias=epscol[:]),
              sem="ac", waits=[("dv", d_var)])
        em.op("scalar", lambda h, o=crsd, i_=clnv: h.activation(o, i_, AF.Exp, scale=-0.5),
              sem="ac")
        a_rs = em.val("ac")
        em.op("vector", lambda h, o=xnb[p][:], i_=xt[p][:], s1=cngm, s2=crsd:
              h.tensor_scalar(o, i_, s1, s2, ALU.add, ALU.mult), sem="dv",
              waits=[("ac", a_rs)] + lastc.get(("xnb", p), []))
        d_app = em.val("dv")
        st = DMAS(xn2_d[i*128:(i+1)*128, :], xnb[p][:], waits=[("dv", d_app)])
        lastc[("xt", p)] = [("ac", a_sq), ("dv", d_app)]
        lastc[("cols", p)] = [("dv", d_app)]
        lastc[("xnb", p)] = [("dls", st)]
    em.wait("sync", "dls", em.val("dls"))
    em.barrier("ln2")
    arena.reset()


def MM_g(em, out_ap, lhsT, rhs, st, sp, waits=()):
    return em.op("tensor", lambda h, o=out_ap, l=lhsT, r=rhs, s=st, e=sp:
                 h.matmul(o, l, r, start=s, stop=e),
                 sem="pe" if sp else None, waits=waits)


def emit_ffn(ctx):
    em = ctx["em"]; arena = ctx["arena"]
    pb = ctx["pb"]
    WgT, WuT, WdT = ctx["WgT"], ctx["WuT"], ctx["WdT"]
    xn2_d, gated_d, out1_d = ctx["xn2_d"], ctx["gated_d"], ctx["out1_d"]
    out = ctx["out"]
    DMAS, DMAG, DMAA = ctx["DMAS"], ctx["DMAG"], ctx["DMAA"]

    xf = arena.alloc("ffX", [128, 16 * OWN], BF16)
    wstr2 = [arena.alloc(f"ffW{i}", [128, 16 * 1024], BF16) for i in range(2)]
    gs = arena.alloc("ffG", [128, 8 * OWN], BF16)
    gmu = [arena.alloc("ffM", [128, OWN], BF16) for _ in range(2)]
    tl = 0
    for dc in range(16):
        tl = ctx["DMAT"](xf[:, dc * OWN:(dc + 1) * OWN],
                         xn2_d[:, dc * 128:(dc + 1) * 128])
    bank_g = [[] for _ in range(4)]
    gmu_war = [[], []]
    gs_war = []
    wldv = {}

    def wload(idx, guard):
        he, phase = idx // 2, idx % 2
        wstr = wstr2[idx % 2]
        h0 = he * 1024
        wl = 0
        for kc in range(16):
            wl = DMAG(wstr[:, kc * 1024:(kc + 1) * 1024],
                      (WgT if phase == 0 else WuT)[kc * 128:(kc + 1) * 128, h0:h0 + 1024],
                      waits=guard)
        wldv[idx] = wl

    wload(0, [])
    wload(1, [])
    for he in range(8):
        h0 = he * 1024
        for phase in range(2):
            idx = he * 2 + phase
            wstr = wstr2[idx % 2]
            wl = wldv[idx]
            for hof in range(8):
                for ts in range(4):
                    bki = (hof * 4 + ts) % 4
                    bk = pb[bki]
                    for kc in range(16):
                        w = []
                        if kc == 0:
                            w = [("dlg", wl), ("dls", tl)] + bank_g[bki]
                            bank_g[bki] = []
                        mmv = MM_g(em, bk[0:128, 0:512],
                                   wstr[:, kc * 1024 + hof * 128:kc * 1024 + hof * 128 + 128],
                                   xf[:, kc * OWN + ts * 512:kc * OWN + ts * 512 + 512],
                                   kc == 0, kc == 15, w)
                    if phase == 0:
                        a_ev = em.op("scalar", lambda h, o=gs[:, hof * OWN + ts * 512:hof * OWN + ts * 512 + 512],
                                     i_=bk[0:128, 0:512]:
                                     h.activation(o, i_, AF.Silu), sem="ac",
                                     waits=[("pe", mmv)] + gs_war)
                        gs_war = []
                        bank_g[bki] = [("ac", a_ev)]
                    else:
                        hp = hof % 2
                        ev = em.op("vector", lambda h, o=gmu[hp][:, ts * 512:ts * 512 + 512],
                                   a=bk[0:128, 0:512],
                                   b=gs[:, hof * OWN + ts * 512:hof * OWN + ts * 512 + 512]:
                                   h.tensor_mul(o, a, b), sem="dv",
                                   waits=[("pe", mmv), ("ac", a_gate)] + gmu_war[hp])
                        gmu_war[hp] = []
                        bank_g[bki] = [("dv", ev)]
                if phase == 1:
                    hp = hof % 2
                    st = DMAS(gated_d[h0 + hof * 128:h0 + hof * 128 + 128, :],
                              gmu[hp][:], waits=[("dv", em.val("dv"))])
                    gmu_war[hp] = [("dls", st)]
            if idx + 2 < 16:
                wload(idx + 2, [("pe", em.val("pe"))])
            if phase == 0:
                a_gate = em.val("ac")
            else:
                gs_war = [("dv", em.val("dv"))]
    em.wait("sync", "dls", em.val("dls"))
    em.barrier("gup")
    arena.reset()

    wd_s2 = [arena.alloc("fdW0", [128, 64 * 512], BF16)]
    gld = [arena.alloc("fdG", [128, 64 * 256], BF16) for _ in range(2)]
    o1t = [arena.alloc("fdO", [128, 512], F32) for _ in range(2)]
    ost = [arena.alloc("fdS", [128, 512], F32) for _ in range(2)]
    g3 = gated_d.rearrange("(hc p) t -> p hc t", p=128)
    bank_g = [[] for _ in range(4)]
    gld_war = [[], []]
    o1t_war = [[], []]
    ost_war = [[], []]
    wdldv = {}

    def wdload(dq, guard):
        wl = 0
        for hc in range(64):
            wl = DMAG(wd_s2[0][:, hc * 512:(hc + 1) * 512],
                      WdT[hc * 128:(hc + 1) * 128, dq * 512:dq * 512 + 512], waits=guard)
        wdldv[dq] = wl

    wdload(0, [])
    for dq in range(4):
        d0 = dq * 512
        wd_s = wd_s2[0]
        wl = wdldv[dq]
        for tp2 in range(8):
            pp = tp2 % 2
            lg = DMAS(gld[pp].rearrange("p (hc t) -> p hc t", t=256),
                      g3[:, :, tp2 * 256:(tp2 + 1) * 256], waits=gld_war[pp])
            gld_war[pp] = []
            for ti in range(2):
                tch = tp2 * 2 + ti
                bki = tch % 2 + 2 * (dq % 2)
                bk = pb[bki]
                lx = DMAA(o1t[pp][:], out1_d[tch * 128:(tch + 1) * 128, d0:d0 + 512],
                          waits=o1t_war[pp])
                for hc in range(64):
                    w = []
                    if hc == 0:
                        w = [("dlg", wl), ("dls", lg)] + bank_g[bki]
                        bank_g[bki] = []
                    mmv = MM_g(em, bk[0:128, 0:512],
                               gld[pp][:, hc * 256 + ti * 128:hc * 256 + ti * 128 + 128],
                               wd_s[:, hc * 512:(hc + 1) * 512],
                               hc == 0, hc == 63, w)
                ev = em.op("vector", lambda h, o=ost[pp][:], a=bk[0:128, 0:512],
                           b=o1t[pp][:]:
                           h.tensor_add(o, a, b), sem="dv",
                           waits=[("pe", mmv), ("dla", lx)] + ost_war[pp])
                bank_g[bki] = [("dv", ev)]
                o1t_war[pp] = [("dv", ev)]
                st = DMAS(out[tch * 128:(tch + 1) * 128, d0:d0 + 512], ost[pp][:],
                          waits=[("dv", ev)])
                ost_war[pp] = [("dls", st)]
            gld_war[pp] = [("pe", em.val("pe"))]
        if dq + 1 < 4:
            wdload(dq + 1, [("pe", em.val("pe"))])



def _consts():
    r = np.arange(128)
    cIbf = np.eye(128, dtype=ml_dtypes.bfloat16)
    cIf = np.eye(128, dtype=np.float32)
    cMaskW = np.where(r[None, :] >= r[:, None], np.float32(NEG), np.float32(0))
    cMaskM = np.where(r[None, :] > r[:, None], np.float32(NEG), np.float32(0))
    return {
        "cIbf": cIbf, "cIf": cIf,
        "cMaskW": cMaskW.astype(np.float32), "cMaskM": cMaskM.astype(np.float32),
        "cOneRow": np.ones((1, 128), np.float32),
        "cOneRowB": np.ones((1, 512), ml_dtypes.bfloat16),
        "cOneColB": np.ones((128, 1), ml_dtypes.bfloat16),
        "cTinyB": np.full((1, 1), 1e-24, ml_dtypes.bfloat16),
    }


def make_in_maps(inputs):
    x = np.ascontiguousarray(inputs["L0_attn_x"], np.float32)
    Wab = np.ascontiguousarray(
        np.concatenate([inputs["L0_attn_Wa"], inputs["L0_attn_Wb"]], axis=1), np.float32)
    shared = dict(
        Wk=np.ascontiguousarray(inputs["L0_attn_Wk"], np.float32),
        Wv=np.ascontiguousarray(inputs["L0_attn_Wv"], np.float32),
        Wq=np.ascontiguousarray(inputs["L0_attn_Wq"], np.float32),
        Wab=Wab,
        Wo=np.ascontiguousarray(inputs["L0_attn_Wo"], np.float32),
        qcw=np.ascontiguousarray(inputs["L0_attn_qcw"], np.float32),
        qcb=np.ascontiguousarray(inputs["L0_attn_qcb"], np.float32).reshape(D, 1),
        kcw=np.ascontiguousarray(inputs["L0_attn_kcw"], np.float32),
        kcb=np.ascontiguousarray(inputs["L0_attn_kcb"], np.float32).reshape(D, 1),
        vcw=np.ascontiguousarray(inputs["L0_attn_vcw"], np.float32),
        vcb=np.ascontiguousarray(inputs["L0_attn_vcb"], np.float32).reshape(NV, 1),
        WgT=np.ascontiguousarray(inputs["L0_ffn_w_gate"].T, np.float32),
        WuT=np.ascontiguousarray(inputs["L0_ffn_w_up"].T, np.float32),
        WdT=np.ascontiguousarray(inputs["L0_ffn_w_down"].T, np.float32),
        **_consts(),
    )
    ones_half = np.ones((HK, OWN), np.float32)
    zeros_half = np.zeros((HK, OWN), np.float32)
    cm_h0 = np.concatenate([zeros_half, ones_half], axis=1)
    cm_h1 = np.concatenate([ones_half, ones_half], axis=1)
    lncm_h0 = np.concatenate([np.full((HK, OWN), NEG, np.float32), zeros_half], axis=1)
    lncm_h1 = np.zeros((HK, T), np.float32)
    in_maps = []
    for c in range(8):
        b, hf = c // 2, c % 2
        if hf == 1:
            x_kvc = x[b]
            x_outc = x[b, OWN:]
            cm, lncm = cm_h1, lncm_h0 * 0
        else:
            x_kvc = np.concatenate([np.zeros((OWN, D), np.float32), x[b, :OWN]], axis=0)
            x_outc = x[b, :OWN]
            cm, lncm = cm_h0, lncm_h0
        in_maps.append(dict(shared,
                            x_kv=np.ascontiguousarray(x_kvc),
                            x_out=np.ascontiguousarray(x_outc),
                            cm16=cm, lncm16=lncm))
    return in_maps


_NC_CACHE = {}


def kernel(**inputs):
    key = (NPHASES, DEBUG_OUT)
    if key not in _NC_CACHE:
        _NC_CACHE[key] = build_program()
    nc = _NC_CACHE[key]
    in_maps = make_in_maps(inputs)
    res = run_bass_kernel_spmd(nc, in_maps, core_ids=list(range(8)))
    out = np.empty((B, T, D), np.float32)
    for c in range(8):
        b, hf = c // 2, c % 2
        out[b, hf * OWN:(hf + 1) * OWN] = res.results[c]["out"]
    kernel.last_results = res
    return out


# version 15
# speedup vs baseline: 1.1984x; 1.0149x over previous
import os
import numpy as np
import ml_dtypes
from contextlib import ExitStack

import concourse.bass as bass
import concourse.mybir as mybir
from concourse.bass_utils import run_bass_kernel_spmd

F32 = mybir.dt.float32
BF16 = mybir.dt.bfloat16
AF = mybir.ActivationFunctionType
ALU = mybir.AluOpType
AX = mybir.AxisListType

B = 4
T = 4096
D = 2048
HK = 16
DK = 128
VE = 256
NV = HK * VE
DH = 8192
C = 128
NCH = T // C
OWN = T // 2
OCH0 = NCH // 2
QT0 = OWN - 128
QTN = T - QT0
NQT = QTN // 512
NEG = -1.0e30

NPHASES = int(os.environ.get("KM_NPH", "6"))
DEBUG_OUT = os.environ.get("KM_DEBUG", "0") == "1"

ENGS = ("tensor", "vector", "scalar", "gpsimd", "sync")


class EM:

    def __init__(self, nc, stack):
        self.nc = nc
        self.stack = stack
        self.ops = {e: [] for e in ENGS}
        self.sems = {}
        self.counts = {}
        self.hwm = {e: {} for e in ENGS}

    def sem(self, name):
        if name not in self.sems:
            self.sems[name] = self.stack.enter_context(self.nc.semaphore(name))
            self.counts[name] = 0
        return self.sems[name]

    def val(self, name):
        return self.counts.get(name, 0)

    def wait(self, eng, name, v=None):
        if v is None:
            v = self.counts.get(name, 0)
        if v <= 0 or self.hwm[eng].get(name, 0) >= v:
            return
        self.hwm[eng][name] = v
        s = self.sem(name)
        self.ops[eng].append(lambda h, s=s, v=v: h.wait_ge(s, v))

    def op(self, eng, builder, sem=None, n=1, waits=()):
        for w in waits:
            if isinstance(w, tuple):
                self.wait(eng, w[0], w[1])
            else:
                self.wait(eng, w)
        if sem is not None:
            s = self.sem(sem)
            self.counts[sem] += n
            def play(h, builder=builder, s=s, n=n):
                builder(h).then_inc(s, n)
        else:
            def play(h, builder=builder):
                builder(h)
        self.ops[eng].append(play)
        return self.counts.get(sem, 0) if sem else None

    def play(self, eng, h):
        for fn in self.ops[eng]:
            fn(h)

    def barrier(self, tag):
        s = self.sem("bar")
        target = self.counts["bar"] + len(ENGS)
        for e in ENGS:
            self.counts["bar"] += 1
            self.ops[e].append(lambda h, s=s: h.sem_inc(s, 1))
        for e in ENGS:
            self.hwm[e]["bar"] = target
            self.ops[e].append(lambda h, s=s, v=target: h.wait_ge(s, v))


class Arena:

    def __init__(self, nc, base, limit):
        self.nc = nc
        self.base = base
        self.limit = limit
        self.cur = base
        self.n = 0

    def alloc(self, name, shape, dtype):
        esz = {F32: 4, BF16: 2, mybir.dt.int32: 4}[dtype]
        nb = int(np.prod(shape[1:])) * esz
        off = (self.cur + 63) & ~63
        assert off + nb <= self.limit, f"arena overflow: {name} {off}+{nb}>{self.limit}"
        self.cur = off + nb
        self.n += 1
        return self.nc.alloc_sbuf_tensor_at(f"{name}_{self.n}", list(shape), dtype, offset=off)

    def reset(self):
        self.cur = self.base


class Bank:

    def __init__(self, h):
        self.h = h
        self.prod = None
        self.cons = None


def dget(dt_np):
    return {np.dtype(np.float32): F32}.get(np.dtype(dt_np), F32)


ARENA_BASE = 44 * 1024
ARENA_LIMIT = 224 * 1024 - 64


def build_program():
    nc = bass.Bass("TRN2", target_bir_lowering=False)
    stk = ExitStack()
    em = EM(nc, stk)

    def inp(name, shape, dt=F32):
        return nc.dram_tensor(name, list(shape), dt, kind="ExternalInput").ap()

    x_kv = inp("x_kv", (T, D))
    x_out = inp("x_out", (OWN, D))
    cm16 = inp("cm16", (HK, T))
    lncm16 = inp("lncm16", (HK, T))
    Wk = inp("Wk", (D, D)); Wv = inp("Wv", (D, NV)); Wq = inp("Wq", (D, D))
    Wab = inp("Wab", (D, 32))
    Wo = inp("Wo", (NV, D))
    qcw = inp("qcw", (D, 4)); qcb = inp("qcb", (D, 1))
    kcw = inp("kcw", (D, 4)); kcb = inp("kcb", (D, 1))
    vcw = inp("vcw", (NV, 4)); vcb = inp("vcb", (NV, 1))
    WgT = inp("WgT", (D, DH)); WuT = inp("WuT", (D, DH)); WdT = inp("WdT", (DH, D))
    cIbf = inp("cIbf", (128, 128), BF16)
    cIf = inp("cIf", (128, 128), F32)
    cMaskW = inp("cMaskW", (128, 128), F32)
    cMaskM = inp("cMaskM", (128, 128), F32)
    cOneRow = inp("cOneRow", (1, 128), F32)
    cOneRowB = inp("cOneRowB", (1, 512), BF16)
    cOneColB = inp("cOneColB", (128, 1), BF16)
    cTinyB = inp("cTinyB", (1, 1), BF16)

    kind_s = "ExternalOutput" if DEBUG_OUT else "Internal"
    def scratch(name, shape, dt):
        return nc.dram_tensor(name, list(shape), dt, kind=kind_s).ap()

    xn1 = scratch("xn1", (T, D), BF16)
    kr = scratch("kr", (D, T), BF16)
    vr = scratch("vr", (NV, T), BF16)
    qr = scratch("qr", (D, QTN), BF16)
    abz = scratch("abz", (32, T), F32)
    wrow_d = scratch("wrow_d", (HK, T), F32)
    carow_d = scratch("carow_d", (HK, T), F32)
    aclast_d = scratch("aclast_d", (HK, NCH), F32)
    tmsl_d = scratch("tmsl_d", (128, NCH * 64), F32)
    outs_d = scratch("outs_d", (128, 32 * OWN), BF16)
    out1_d = scratch("out1_d", (OWN, D), F32)
    xn2_d = scratch("xn2_d", (OWN, D), BF16)
    gated_d = scratch("gated_d", (DH, OWN), BF16)
    out = nc.dram_tensor("out", [OWN, D], F32, kind="ExternalOutput").ap()

    def perm(name, shape, dt):
        return stk.enter_context(nc.sbuf_tensor(name, list(shape), dt))

    sIbf = perm("sIbf", [128, 128], BF16)
    sIf = perm("sIf", [128, 128], F32)
    sMaskW = perm("sMaskW", [128, 128], F32)
    sMaskM = perm("sMaskM", [128, 128], F32)
    sOneRow = perm("sOneRow", [1, 128], F32)
    sOneRowB = perm("sOneRowB", [1, 512], BF16)
    sOneColB = perm("sOneColB", [128, 1], BF16)
    sTinyB = perm("sTinyB", [1, 1], BF16)
    sI2 = perm("sI2", [128, 256], BF16)
    sMaskWb = perm("sMaskWb", [128, 128], BF16)
    sMaskMb = perm("sMaskMb", [128, 128], BF16)
    zrow = perm("zrow", [16, 128], F32)
    tmsl = perm("tmsl", [128, NCH * 64], F32)
    colsA = perm("colsA", [128, 16], F32)
    colsB = perm("colsB", [128, 16], F32)
    epscol = perm("epscol", [128, 1], F32)

    arena = Arena(nc, ARENA_BASE, ARENA_LIMIT)

    pb = [stk.enter_context(nc.psum_tensor(f"pb{i}", [128, 512], F32)) for i in range(5)]
    pbE = stk.enter_context(nc.psum_tensor("pbE", [128, 1024], F32))
    pT = stk.enter_context(nc.psum_tensor("pT", [128, 1024], BF16))

    def DMAS(out_ap, in_ap, waits=()):
        return em.op("sync", lambda h, o=out_ap, i=in_ap: h.dma_start(out=o, in_=i),
                     sem="dls", n=16, waits=waits)

    def DMAT(out_ap, in_ap, waits=()):
        return em.op("sync", lambda h, o=out_ap, i=in_ap: h.dma_start_transpose(o, i),
                     sem="dls", n=16, waits=waits)

    def DMAG(out_ap, in_ap, waits=()):
        return em.op("gpsimd", lambda h, o=out_ap, i=in_ap: h.dma_start(out=o, in_=i),
                     sem="dlg", n=16, waits=waits)

    def DMAA(out_ap, in_ap, waits=()):
        return em.op("scalar", lambda h, o=out_ap, i=in_ap: h.dma_start(out=o, in_=i),
                     sem="dla", n=16, waits=waits)

    DMAS(sIbf[:], cIbf); DMAS(sIf[:], cIf)
    DMAS(sMaskW[:], cMaskW); DMAS(sMaskM[:], cMaskM)
    DMAS(sOneRow[:], cOneRow); DMAS(sOneRowB[:], cOneRowB)
    DMAS(sOneColB[:], cOneColB); DMAS(sTinyB[:], cTinyB)
    DMAS(sI2[:, 0:128], cIbf); DMAS(sI2[:, 128:256], cIbf)
    DMAG(sMaskWb[:], cMaskW); DMAG(sMaskMb[:], cMaskM)
    em.op("vector", lambda h: h.memset(zrow[:], 0.0), sem="dv")
    em.op("vector", lambda h: h.memset(epscol[:], 1e-5), sem="dv")
    c_init = em.val("dls")
    em.wait("sync", "dls", c_init)
    em.barrier("init")

    xt = [arena.alloc("xt", [128, D], F32) for _ in range(2)]
    sqs = [arena.alloc("sqs", [128, D], F32) for _ in range(2)]
    xnb = [arena.alloc("xnb", [128, D], BF16) for _ in range(2)]
    lastc = {}
    for i in range(T // 128):
        p = i % 2
        cols = colsA if p == 0 else colsB
        csum, cssq, cngm, cssd, cnvr, clnv, crsd = (cols[:, j:j+1] for j in range(7))
        w = lastc.get(("xt", p), [])
        ld = DMAS(xt[p][:], x_kv[i*128:(i+1)*128, :], waits=w)
        em.op("scalar", lambda h, o=sqs[p][:], i_=xt[p][:], a=csum:
              h.activation(o, i_, AF.Copy, accum_out=a), sem="ac",
              waits=[("dls", ld)] + lastc.get(("cols", p), []))
        em.op("scalar", lambda h, o=sqs[p][:], i_=xt[p][:], a=cssq:
              h.activation(o, i_, AF.Square, accum_out=a), sem="ac")
        a_sq = em.val("ac")
        em.op("vector", lambda h, o=cngm, i_=csum: h.tensor_scalar_mul(o, i_, -1.0 / D),
              sem="dv", waits=[("ac", a_sq)])
        em.op("vector", lambda h, o=cssd, i_=cssq: h.tensor_scalar_mul(o, i_, 1.0 / D), sem="dv")
        em.op("vector", lambda h, o=cnvr, i0=cngm, s=cngm, i1=cssd:
              h.scalar_tensor_tensor(o, i0, s, i1, ALU.mult, ALU.subtract), sem="dv")
        d_var = em.val("dv")
        em.op("scalar", lambda h, o=clnv, i_=cnvr: h.activation(o, i_, AF.Ln, scale=-1.0, bias=epscol[:]),
              sem="ac", waits=[("dv", d_var)])
        em.op("scalar", lambda h, o=crsd, i_=clnv: h.activation(o, i_, AF.Exp, scale=-0.5), sem="ac")
        a_rs = em.val("ac")
        em.op("vector", lambda h, o=xnb[p][:], i_=xt[p][:], s1=cngm, s2=crsd:
              h.tensor_scalar(o, i_, s1, s2, ALU.add, ALU.mult), sem="dv",
              waits=[("ac", a_rs)] + lastc.get(("xnb", p), []))
        d_app = em.val("dv")
        st = DMAS(xn1[i*128:(i+1)*128, :], xnb[p][:], waits=[("dv", d_app)])
        lastc[("xt", p)] = [("ac", a_sq), ("dv", d_app)]
        lastc[("cols", p)] = [("dv", d_app)]
        lastc[("xnb", p)] = [("dls", st)]
        if i == T // 128 // 2 - 1:
            ln1_half = st
    ln1_full = em.val("dls")
    arena.reset()

    if NPHASES >= 2:
        xng = arena.alloc("xng", [128, 16 * QTN], BF16)
        wbuf = [arena.alloc("wbuf", [128, 16 * 1024], BF16) for _ in range(2)]
        stg = [arena.alloc("stg", [128, QTN], BF16) for _ in range(2)]
        stgab = [arena.alloc("stgab", [32, 2048], F32) for _ in range(2)]
        wbuf_war = [[], []]
        stg_war = [[], []]
        stgab_war = [[], []]
        bank_war = [[] for _ in range(7)]
        bkctr = [0]

        xng_war = [[]]

        def proj_pass(tg, t0, tw, segs):
            lnw = [("dls", ln1_half if tg == 0 else ln1_full)]
            tl = 0
            for dc in range(16):
                tl = DMAT(xng[:, dc * tw:(dc + 1) * tw],
                          xn1[t0:t0 + tw, dc * 128:(dc + 1) * 128],
                          waits=xng_war[0] + lnw)
            xng_war[0] = []
            ntsub_full, rem = divmod(tw, 512)
            tsubs = [(i * 512, 512) for i in range(ntsub_full)]
            if rem:
                tsubs.append((ntsub_full * 512, rem))
            for (W, OFW, slab, slabw, f32out) in segs:
                for of0 in range(0, OFW, 1024):
                    sw = min(1024, OFW - of0)
                    wp = (of0 // 1024) % 2
                    wl = 0
                    for kc in range(16):
                        wl = DMAG(wbuf[wp][:, kc * sw:(kc + 1) * sw],
                                  W[kc * 128:(kc + 1) * 128, of0:of0 + sw],
                                  waits=wbuf_war[wp])
                    wbuf_war[wp] = []
                    for of8 in range(max(1, sw // 128)):
                        m = min(128, sw)
                        ofc = of0 + of8 * 128
                        sp = (ofc // 128) % 2
                        stgw = stgab_war[sp] if f32out else stg_war[sp]
                        sbuf_stg = stgab[sp] if f32out else stg[sp]
                        for (ts0, tsw) in tsubs:
                            bk = bkctr[0] % 4
                            bkctr[0] += 1
                            pbk = pb[bk]
                            for kc in range(16):
                                wts = []
                                if kc == 0:
                                    wts = [("dlg", wl), ("dls", tl)] + bank_war[bk]
                                    bank_war[bk] = []
                                em.op("tensor",
                                      lambda h, o=pbk[0:m, 0:tsw],
                                      l=wbuf[wp][:, kc * sw + of8 * 128: kc * sw + of8 * 128 + m],
                                      r=xng[:, kc * tw + ts0: kc * tw + ts0 + tsw],
                                      s=(kc == 0), e=(kc == 15):
                                      h.matmul(o, l, r, start=s, stop=e),
                                      sem="pe" if kc == 15 else None, waits=wts)
                            mmv = em.val("pe")
                            if bkctr[0] % 2 == 0:
                                ev = em.op("vector",
                                           lambda h, o=sbuf_stg[0:m, ts0:ts0 + tsw], i_=pbk[0:m, 0:tsw]:
                                           h.tensor_copy(o, i_), sem="dv",
                                           waits=[("pe", mmv)] + stgw)
                                bank_war[bk] = [("dv", ev)]
                            else:
                                ev = em.op("scalar",
                                           lambda h, o=sbuf_stg[0:m, ts0:ts0 + tsw], i_=pbk[0:m, 0:tsw]:
                                           h.activation(o, i_, AF.Copy), sem="ac",
                                           waits=[("pe", mmv)] + stgw)
                                bank_war[bk] = [("ac", ev)]
                            stgw.clear()
                        ofr = (ofc // 128) if not f32out else 0
                        dst = slab[ofc:ofc + m, slabw[0]:slabw[0] + tw] if f32out else \
                            slab[ofc:ofc + m, slabw[0]:slabw[0] + tw]
                        wv = [("dv", em.val("dv")), ("ac", em.val("ac"))]
                        st = DMAS(dst, sbuf_stg[0:m, 0:tw], waits=wv)
                        if f32out:
                            stgab_war[sp] = [("dls", st)]
                        else:
                            stg_war[sp] = [("dls", st)]
                    wbuf_war[wp] = [("pe", em.val("pe"))]
            xng_war[0] = [("pe", em.val("pe"))]

        for tg in range(2):
            t0 = tg * 2048
            proj_pass(tg, t0, 2048, [
                (Wab, 32, abz, (t0,), True),
                (Wk, 2048, kr, (t0,), False),
                (Wv, NV, vr, (t0,), False),
            ])
        proj_pass(2, QT0, QTN, [(Wq, 2048, qr, (0,), False)])
        em.wait("sync", "dls", em.val("dls"))
        em.barrier("proj")
        arena.reset()

    if NPHASES >= 3:
        Ab = arena.alloc("sc_a", [16, T], F32)
        Bb = arena.alloc("sc_b", [16, T], F32)
        Sb = arena.alloc("sc_s", [16, T], F32)
        Lb = arena.alloc("sc_l", [16, T], F32)
        CMb = arena.alloc("sc_cm", [16, T], F32)
        LCb = arena.alloc("sc_lc", [16, T], F32)
        sacl = arena.alloc("sc_acl", [16, NCH], F32)

        l1 = DMAS(Ab[:], abz[0:16, :])
        l2 = DMAS(Bb[:], abz[16:32, :])
        l3 = DMAS(CMb[:], cm16)
        l4 = DMAS(LCb[:], lncm16)

        def ACTOP(dst, src, fn, sc=1.0, bi=0.0, waits=(), accum=None):
            return em.op("scalar", lambda h, o=dst, i_=src, f=fn, s=sc, b=bi, a=accum:
                         h.activation(o, i_, f, scale=s, bias=b, accum_out=a),
                         sem="ac", waits=waits)

        def logsig(dst, z, wz):
            ACTOP(Sb[:], z[:], AF.Abs, waits=wz)
            ACTOP(Sb[:], Sb[:], AF.Exp, sc=-1.0)
            a1 = ACTOP(Sb[:], Sb[:], AF.Ln, bi=1.0)
            em.op("vector", lambda h, o=dst[:], i_=z[:]: h.tensor_scalar_min(o, i_, 0.0),
                  sem="dv", waits=[("ac", a1)])
            return em.op("vector", lambda h, o=dst[:], a=dst[:], b=Sb[:]:
                         h.tensor_sub(o, a, b), sem="dv")

        logsig(Lb, Ab, [("dls", l1)])
        d_la = em.val("dv")
        logsig(Ab, Bb, [("dls", l2), ("dv", d_la)])
        a_sg = ACTOP(Sb[:], Bb[:], AF.Sigmoid, waits=[("dv", em.val("dv"))])
        em.op("vector", lambda h: h.tensor_mul(Bb[:], Sb[:], CMb[:]), sem="dv",
              waits=[("ac", a_sg), ("dls", l3)])
        em.op("vector", lambda h: h.tensor_add(Ab[:], Ab[:], LCb[:]), sem="dv",
              waits=[("dls", l4)])
        for j in range(NCH):
            em.op("vector", lambda h, j=j: h.tensor_tensor_scan(
                CMb[:, j * C:(j + 1) * C], Lb[:, j * C:(j + 1) * C], zrow[:],
                0.0, ALU.add, ALU.add), sem="dv")
        em.op("vector", lambda h: h.tensor_scalar_mul(LCb[:], CMb[:], -1.0), sem="dv")
        d_ca = em.val("dv")
        em.op("vector", lambda h: h.tensor_add(Sb[:], CMb[:], Ab[:]), sem="dv")
        d_wr = em.val("dv")
        a_naw = ACTOP(Ab[:], Sb[:], AF.Exp, waits=[("dv", d_wr)])
        em.op("vector", lambda h: h.tensor_scalar_mul(Ab[:], Ab[:], -1.0), sem="dv",
              waits=[("ac", a_naw)])
        for j in range(NCH):
            em.op("vector", lambda h, j=j: h.tensor_single_scalar(
                Lb[:, j * C:(j + 1) * C], CMb[:, j * C:(j + 1) * C],
                CMb[:, j * C + C - 1:j * C + C], ALU.subtract),
                sem="dv", waits=[("dv", d_ca)])
        d_rp = em.val("dv")
        a_rac = ACTOP(Lb[:], Lb[:], AF.Exp, sc=-1.0, waits=[("dv", d_rp)])
        a_acl = ACTOP(sacl[:], CMb[:, C - 1::C], AF.Exp, waits=[("dv", d_ca)])
        st1 = DMAS(wrow_d, Sb[:], waits=[("dv", em.val("dv"))])
        st2 = DMAS(carow_d, CMb[:])
        st3 = DMAS(aclast_d, sacl[:], waits=[("ac", a_acl)])
        d_all = em.val("dv"); a_all = em.val("ac")
        tm_war = []
        for j in range(NCH):
            srcs = [LCb, Bb, Ab, Lb]
            for r in range(4):
                w = ([("dv", d_all), ("ac", a_all)] + tm_war) if r == 0 else ()
                em.op("tensor", lambda h, s=srcs[r], j=j, r=r:
                      h.transpose(pb[4][0:128, r * 16:(r + 1) * 16],
                                  s[:, j * C:(j + 1) * C], sIf[0:16, 0:16]),
                      sem="pe" if r == 3 else None, waits=w)
            pev = em.val("pe")
            ev = em.op("vector", lambda h, j=j: h.tensor_copy(
                tmsl[:, j * 64:(j + 1) * 64], pb[4][0:128, 0:64]),
                sem="dv", waits=[("pe", pev)])
            tm_war = [("dv", ev)]
        if DEBUG_OUT:
            DMAS(tmsl_d, tmsl[:], waits=[("dv", em.val("dv"))])
        em.wait("sync", "dls", em.val("dls"))
        em.barrier("scal")
        arena.reset()

    ctx = dict(locals())
    if NPHASES >= 4:
        emit_mix(ctx)
    if NPHASES >= 5:
        emit_wo(ctx)
    if NPHASES >= 6:
        emit_ffn(ctx)

    for s in ("dls", "dlg", "dla"):
        em.wait("sync", s, em.val(s))
    em.barrier("fin")

    blk = stk.enter_context(nc.Block())
    blk.tensor(lambda h: em.play("tensor", h))
    blk.vector(lambda h: em.play("vector", h))
    blk.scalar(lambda h: em.play("scalar", h))
    blk.gpsimd(lambda h: em.play("gpsimd", h))
    blk.sync(lambda h: em.play("sync", h))
    stk.close()
    return nc


def emit_mix(ctx):
    em = ctx["em"]; arena = ctx["arena"]
    pb = ctx["pb"]; pbE = ctx["pbE"]; pT = ctx["pT"]
    kr, vr, qr = ctx["kr"], ctx["vr"], ctx["qr"]
    wrow_d, carow_d, aclast_d = ctx["wrow_d"], ctx["carow_d"], ctx["aclast_d"]
    outs_d = ctx["outs_d"]; tmsl = ctx["tmsl"]
    sIbf, sIf = ctx["sIbf"], ctx["sIf"]
    sMaskW, sMaskM = ctx["sMaskW"], ctx["sMaskM"]
    sMaskWb, sMaskMb = ctx["sMaskWb"], ctx["sMaskMb"]
    sOneRow, sOneRowB, sTinyB = ctx["sOneRow"], ctx["sOneRowB"], ctx["sTinyB"]
    sOneColB = ctx["sOneColB"]; sI2 = ctx["sI2"]
    qcw, qcb, kcw, kcb, vcw, vcb = (ctx[k] for k in ("qcw", "qcb", "kcw", "kcb", "vcw", "vcb"))
    DMAS, DMAG = ctx["DMAS"], ctx["DMAG"]
    PAD = 4

    slabK = arena.alloc("mxK", [128, PAD + T], BF16)
    slabV0 = arena.alloc("mxV0", [128, PAD + T], BF16)
    slabV1 = arena.alloc("mxV1", [128, PAD + T], BF16)
    slabQ = arena.alloc("mxQ", [128, PAD + QTN], BF16)
    q2 = arena.alloc("mxQ2", [128, QTN], BF16)
    sqt = arena.alloc("mxSq", [128, 512], BF16)
    krnb = arena.alloc("mxKrn", [1, T], BF16)
    wrowh = arena.alloc("mxWrow", [1, T], F32)
    mrow = arena.alloc("mxMrow", [1, QTN], F32)
    lnscr = arena.alloc("mxLns", [1, 1024], F32)
    amq = arena.alloc("mxAmq", [1, QTN], BF16)
    aclr = arena.alloc("mxAcl", [1, NCH], F32)
    ACrep = arena.alloc("mxACr", [128, NCH], F32)
    cwq = arena.alloc("mxCwq", [128, 4], F32); cbq = arena.alloc("mxCbq", [128, 1], F32)
    cwk = arena.alloc("mxCwk", [128, 4], F32); cbk = arena.alloc("mxCbk", [128, 1], F32)
    cwv0 = arena.alloc("mxCv0", [128, 4], F32); cbv0 = arena.alloc("mxCb0", [128, 1], F32)
    cwv1 = arena.alloc("mxCv1", [128, 4], F32); cbv1 = arena.alloc("mxCb1", [128, 1], F32)
    dg = [arena.alloc(f"mxDg{i}", [128, 128], BF16) for i in range(16)]
    GLAG = 3
    WtMt2 = [arena.alloc(f"mxWtMt{i}", [128, 512], BF16) for i in range(GLAG)]
    Tnv2 = [arena.alloc(f"mxTnv{i}", [128, 256], BF16) for i in range(GLAG)]
    bV2 = [arena.alloc(f"mxbV{i}", [128, 512], BF16) for i in range(GLAG)]
    Ktm2 = [arena.alloc(f"mxKtm{i}", [128, 256], BF16) for i in range(GLAG)]
    W02g = [arena.alloc(f"mxW02{i}", [128, 256], BF16) for i in range(2)]
    PQ2g = [[arena.alloc(f"mxPQ{i}_{p}", [128, 512], BF16) for p in range(2)] for i in range(2)]
    P62g = [arena.alloc(f"mxP62{i}", [128, 256], BF16) for i in range(2)]
    PT2g = [[arena.alloc(f"mxPT{i}_{p}", [128, 256], BF16) for p in range(2)] for i in range(2)]
    LWM2 = arena.alloc("mxLWM", [128, 512], F32)
    AWK2 = [arena.alloc(f"mxAWK{i}", [128, 256], BF16) for i in range(GLAG)]
    X1b2 = [arena.alloc(f"mxX1{i}", [128, 256], BF16) for i in range(GLAG)]
    X1T2 = [arena.alloc(f"mxXT{i}", [128, 256], BF16) for i in range(GLAG)]
    Uloc2 = [arena.alloc(f"mxUL{i}", [128, 512], BF16) for i in range(GLAG)]
    B2 = [arena.alloc(f"mxB{i}", [128, 512], BF16) for i in range(GLAG)]
    MT2 = [arena.alloc(f"mxMT{i}", [128, 256], BF16) for i in range(GLAG)]
    Ut = arena.alloc("mxU", [128, 256], BF16)
    St = [arena.alloc(f"mxS{i}", [128, 256], BF16) for i in range(2)]
    outs_s = arena.alloc("mxOuts", [128, 2 * OWN], BF16)

    def MM(bank_ap, lhsT, rhs, st, sp, sem="pe", waits=()):
        return em.op("tensor", lambda h, o=bank_ap, l=lhsT, r=rhs, s=st, e=sp:
                     h.matmul(o, l, r, start=s, stop=e), sem=sem, waits=waits)

    def TRANS(bank_ap, in_ap, ident, waits=()):
        return em.op("tensor", lambda h, o=bank_ap, i_=in_ap, d=ident:
                     h.transpose(o, i_, d), sem="pe", waits=waits)

    def ACOPY(dst, src, waits=()):
        return em.op("scalar", lambda h, o=dst, i_=src:
                     h.activation(o, i_, AF.Copy), sem="ac", waits=waits)

    def AEXP(dst, src, bias, waits=(), scale=1.0):
        return em.op("scalar", lambda h, o=dst, i_=src, b=bias, s=scale:
                     h.activation(o, i_, AF.Exp, bias=b, scale=s), sem="ac", waits=waits)

    prev_stores = []
    slab_war = []
    for hd in range(HK):
        lw = list(slab_war)
        lK = DMAS(slabK[:, PAD:], kr[hd * 128:(hd + 1) * 128, :], waits=lw)
        lV0 = DMAS(slabV0[:, PAD:], vr[hd * 256:hd * 256 + 128, :], waits=lw)
        lV1 = DMAS(slabV1[:, PAD:], vr[hd * 256 + 128:hd * 256 + 256, :], waits=lw)
        lQ = DMAS(slabQ[:, PAD:], qr[hd * 128:(hd + 1) * 128, :], waits=lw)
        lrw = DMAS(wrowh[:], wrow_d[hd:hd + 1, :], waits=lw)
        lmr = DMAG(mrow[:], carow_d[hd:hd + 1, QT0:], waits=lw)
        lacl = DMAG(aclr[:], aclast_d[hd:hd + 1, :])
        lcw = DMAG(cwq[:], qcw[hd * 128:(hd + 1) * 128, :])
        DMAG(cbq[:], qcb[hd * 128:(hd + 1) * 128, :])
        DMAG(cwk[:], kcw[hd * 128:(hd + 1) * 128, :])
        DMAG(cbk[:], kcb[hd * 128:(hd + 1) * 128, :])
        DMAG(cwv0[:], vcw[hd * 256:hd * 256 + 128, :])
        DMAG(cbv0[:], vcb[hd * 256:hd * 256 + 128, :])
        DMAG(cwv1[:], vcw[hd * 256 + 128:hd * 256 + 256, :])
        lcb = DMAG(cbv1[:], vcb[hd * 256 + 128:hd * 256 + 256, :])
        z1 = em.op("vector", lambda h: h.memset(slabK[:, 0:PAD], 0.0), sem="dv",
                   waits=[("dls", lK)])
        em.op("vector", lambda h: h.memset(slabV0[:, 0:PAD], 0.0), sem="dv")
        em.op("vector", lambda h: h.memset(slabV1[:, 0:PAD], 0.0), sem="dv")
        em.op("vector", lambda h: h.memset(slabQ[:, 0:PAD], 0.0), sem="dv")

        slabs = [(slabK, T, cwk, cbk, 0), (slabV0, T, cwv0, cbv0, 1),
                 (slabV1, T, cwv1, cbv1, 2), (slabQ, QTN, cwq, cbq, 3)]
        for si, (slab, ncols, cw, cb, di) in enumerate(slabs):
            for j in range(4):
                em.op("vector", lambda h, j=j, c=cw, di=di: h.tensor_scalar_mul(
                    dg[di * 4 + j][:], sIbf[:], c[:, j:j + 1]), sem="dv",
                    waits=[("dlg", lcb), ("pe", em.val("pe"))])
        dtap = em.val("dv")
        a_prep0 = em.val("ac")
        steps = []
        for si, (slab, ncols, cw, cb, di) in enumerate(slabs):
            tiles = []
            t0v = 0
            while t0v < ncols:
                tw = min(512, ncols - t0v)
                tiles.append((si, t0v, tw))
                t0v += tw
            steps.append(list(reversed(tiles)))
        order = []
        mx = max(len(s) for s in steps)
        for i in range(mx):
            for s in steps:
                if i < len(s):
                    order.append(s[i])
        silu_g = {}
        for n_, (si, t0v, tw) in enumerate(order):
            slab, ncols, cw, cb, di = slabs[si]
            bki = n_ % 4
            bk = pb[bki]
            for j in range(4):
                MM(bk[0:128, 0:tw], dg[di * 4 + j][:],
                   slab[:, PAD + t0v - 3 + j:PAD + t0v - 3 + j + tw],
                   j == 0, j == 3,
                   sem="pe" if j == 3 else None,
                   waits=[("dv", dtap), ("ac", max(a_prep0, silu_g.get(bki, 0)))] if j == 0 else ())
            pev = em.val("pe")
            silu_g[bki] = em.op("scalar", lambda h, o=slab[:, PAD + t0v:PAD + t0v + tw],
                                i_=bk[0:128, 0:tw], b=cb:
                                h.activation(o, i_, AF.Silu, bias=b[:]), sem="ac",
                                waits=[("pe", pev), ("dlg", lcb)])
        a_conv_q = em.val("ac")

        for rnd in range(4):
            for half in range(2):
                ti = rnd * 2 + half
                a_sq = em.op("scalar", lambda h, o=sqt[:], i_=slabK[:, PAD + ti * 512:PAD + (ti + 1) * 512]:
                             h.activation(o, i_, AF.Square), sem="ac",
                             waits=[("dv", em.val("dv")), ("pe", em.val("pe"))])
                MM(pbE[0:1, half * 512:half * 512 + 512], sOneColB[:], sqt[:], True, False,
                   sem=None, waits=[("ac", a_sq)])
                MM(pbE[0:1, half * 512:half * 512 + 512], sTinyB[:], sOneRowB[:], False, True,
                   sem="pe")
            pev = em.val("pe")
            a_ln = em.op("scalar", lambda h, o=lnscr[:], i_=pbE[0:1, 0:1024]:
                         h.activation(o, i_, AF.Ln), sem="ac", waits=[("pe", pev)])
            em.op("scalar", lambda h, o=krnb[0:1, rnd * 1024:(rnd + 1) * 1024], i_=lnscr[:]:
                  h.activation(o, i_, AF.Exp, scale=-0.5), sem="ac")
        a_krn = em.val("ac")
        for ti in range(8):
            mmv = MM(pb[2][0:128, 0:512], sOneRowB[0:1, 0:128],
                     krnb[0:1, ti * 512:(ti + 1) * 512], True, True,
                     waits=[("ac", a_krn), ("dv", em.val("dv"))])
            em.op("vector", lambda h, ti=ti, o=slabK[:, PAD + ti * 512:PAD + (ti + 1) * 512],
                  a=slabK[:, PAD + ti * 512:PAD + (ti + 1) * 512], b=pb[2][0:128, 0:512]:
                  h.tensor_mul(o, a, b), sem="dv", waits=[("pe", mmv)])
        d_khat = em.val("dv")

        qtiles = [(0, 512), (512, 512), (1024, 512), (1536, 512), (2048, 128)]
        for rnd, grp in enumerate([qtiles[0:2], qtiles[2:4], qtiles[4:5]]):
            for gi, (t0v, tw) in enumerate(grp):
                a_sq = em.op("scalar", lambda h, o=sqt[0:128, 0:tw],
                             i_=slabQ[:, PAD + t0v:PAD + t0v + tw], tw=tw:
                             h.activation(o, i_, AF.Square), sem="ac",
                             waits=[("dv", em.val("dv")), ("pe", em.val("pe"))])
                MM(pbE[0:1, gi * 512:gi * 512 + tw], sOneColB[:], sqt[0:128, 0:tw], True, False,
                   sem=None, waits=[("ac", a_sq)])
                MM(pbE[0:1, gi * 512:gi * 512 + tw], sTinyB[:], sOneRowB[0:1, 0:tw], False, True,
                   sem="pe")
            gw = sum(tw for _, tw in grp)
            g0 = grp[0][0]
            pev = em.val("pe")
            a_ln = em.op("scalar", lambda h, o=lnscr[0:1, 0:gw], i_=pbE[0:1, 0:gw], gw=gw:
                         h.activation(o, i_, AF.Ln), sem="ac", waits=[("pe", pev)])
            em.op("vector", lambda h, o=mrow[0:1, g0:g0 + gw], i0=lnscr[0:1, 0:gw],
                  i1=mrow[0:1, g0:g0 + gw], gw=gw:
                  h.scalar_tensor_tensor(o, i0, -0.5, i1, ALU.mult, ALU.add),
                  sem="dv", waits=[("ac", a_ln), ("dlg", lmr)])
        d_mrow = em.val("dv")
        a_amq = em.op("scalar", lambda h: h.activation(amq[:], mrow[:], AF.Exp),
                      sem="ac", waits=[("dv", d_mrow)])
        for (t0v, tw) in qtiles:
            mmv = MM(pb[2][0:128, 0:tw], sOneRowB[0:1, 0:128], amq[0:1, t0v:t0v + tw],
                     True, True, waits=[("ac", a_amq), ("dv", em.val("dv"))])
            em.op("vector", lambda h, t0v=t0v, tw=tw, o=q2[:, t0v:t0v + tw],
                  a=slabQ[:, PAD + t0v:PAD + t0v + tw], b=pb[2][0:128, 0:tw]:
                  h.tensor_mul(o, a, b), sem="dv", waits=[("pe", mmv)])
        mmv = MM(pb[2][0:128, 0:NCH], sOneRow[:], aclr[:], True, True,
                 waits=[("dlg", lacl), ("dv", em.val("dv"))])
        d_acr = em.op("vector", lambda h: h.tensor_copy(ACrep[:], pb[2][0:128, 0:NCH]),
                      sem="dv", waits=[("pe", mmv)])
        em.op("vector", lambda h: h.memset(St[0][:], 0.0), sem="dv",
              waits=prev_stores)
        em.op("vector", lambda h: h.memset(outs_s[:], 0.0), sem="dv")
        d_prep = em.val("dv")

        a_prep = em.val("ac")
        hs = {
            "d_chain_prev": d_prep, "g_pT": [("ac", a_prep)],
            "g_p0": [("ac", a_prep)], "g_p1": [("ac", a_prep)],
            "g_S0": [("ac", a_prep)], "g_S1": [("ac", a_prep)],
            "g_U0": [("ac", a_prep)], "g_U1": [("ac", a_prep)],
            "d_lwm_war": d_prep,
            "sq_bank_g": {2: [("dv", d_prep)], 3: [("dv", d_prep)], 4: [("dv", d_prep)]},
        }
        mstate = {}

        def col(j, r):
            return tmsl[:, j * 64 + 16 * r + hd:j * 64 + 16 * r + hd + 1]

        def mach_s0(g):
            gi = g % GLAG
            jA, jB = 2 * g, 2 * g + 1
            wv = [("dv", d_khat), ("ac", a_conv_q), ("dv", em.val("dv"))] + hs["g_pT"]
            tkv = 0
            for (j, base) in ((jA, 0), (jB, 512)):
                jt = PAD + j * 128
                TRANS(pT[:, base + 128:base + 256], slabV0[:, jt:jt + 128], sIbf[:],
                      waits=wv); wv = ()
                TRANS(pT[:, base + 256:base + 384], slabV1[:, jt:jt + 128], sIbf[:])
                tkv = TRANS(pT[:, base + 384:base + 512], slabK[:, jt:jt + 128], sIbf[:])
            d_bv = d_ktm = 0
            for (j, base, cp) in ((jA, 0, 0), (jB, 512, 1)):
                d_bv = em.op("vector", lambda h, b=col(j, 1), base=base, cp=cp, gi=gi:
                             h.tensor_scalar_mul(bV2[gi][:, cp * 256:cp * 256 + 256],
                                                 pT[:, base + 128:base + 384], b),
                             sem="dv", waits=[("pe", tkv)])
                em.op("vector", lambda h, r=col(j, 3), base=base, cp=cp, gi=gi:
                      h.tensor_scalar_mul(Ktm2[gi][:, cp * 128:cp * 128 + 128],
                                          pT[:, base + 384:base + 512], r), sem="dv")
                d_ktm = em.op("vector", lambda h, n=col(j, 2), base=base, cp=cp, gi=gi:
                              h.tensor_scalar_mul(AWK2[gi][:, cp * 128:cp * 128 + 128],
                                                  pT[:, base + 384:base + 512], n), sem="dv")
            mstate[g] = dict(d_bv=d_bv, d_ktm=d_ktm)

        def mach_s1(g):
            gi = g % GLAG
            jA, jB = 2 * g, 2 * g + 1
            is_out = jA >= OCH0
            kk0 = 0
            for (j, base) in ((jA, 0), (jB, 256)):
                jt = PAD + j * 128
                w = ([("dv", em.val("dv"))] + hs["g_p0"]) if base == 0 else ()
                kk0 = MM(pb[0][0:128, base:base + 128], slabK[:, jt:jt + 128],
                         slabK[:, jt:jt + 128], True, True, waits=w)
                if is_out:
                    qloc = j * 128 - QT0
                    kk0 = MM(pb[0][0:128, base + 128:base + 256], slabK[:, jt:jt + 128],
                             slabQ[:, PAD + qloc:PAD + qloc + 128], True, True)
            ew = 0
            for (j, base) in ((jA, 0), (jB, 256)):
                w = hs["g_p1"] if base == 0 else ()
                MM(pb[1][0:128, base:base + 128], sMaskWb[:], sIbf[:], True, False,
                   sem=None, waits=w)
                ew = MM(pb[1][0:128, base:base + 128], sOneRow[:],
                        wrowh[0:1, j * 128:(j + 1) * 128], False, True,
                        waits=[("dls", lrw)])
                if is_out:
                    qloc = j * 128 - QT0
                    MM(pb[1][0:128, base + 128:base + 256], sMaskMb[:], sIbf[:], True, False,
                       sem=None)
                    ew = MM(pb[1][0:128, base + 128:base + 256], sOneRow[:],
                            mrow[0:1, qloc:qloc + 128], False, True)
            wid = 256 if is_out else 128
            a_lwm = 0
            for (j, base) in ((jA, 0), (jB, 256)):
                a_lwm = AEXP(LWM2[0:128, base:base + wid], pb[1][0:128, base:base + wid],
                             col(j, 0), waits=[("pe", ew), ("dv", hs["d_lwm_war"])])
            hs["g_p1"] = [("ac", a_lwm)]
            if is_out:
                d_wtmt = em.op("vector", lambda h, gi=gi: h.tensor_mul(
                    WtMt2[gi][:], LWM2[:], pb[0][0:128, 0:512]),
                    sem="dv", waits=[("ac", a_lwm), ("pe", kk0)])
            else:
                d_wtmt = em.op("vector", lambda h, gi=gi: h.tensor_mul(
                    WtMt2[gi].rearrange("p (c x) -> p c x", x=128)[:, 0::2, :],
                    LWM2.rearrange("p (c x) -> p c x", x=128)[:, 0::2, :],
                    pb[0].rearrange("p (c x) -> p c x", x=128)[:, 0::2, :]),
                    sem="dv", waits=[("ac", a_lwm), ("pe", kk0)])
            hs["d_lwm_war"] = d_wtmt
            hs["g_p0"] = [("dv", d_wtmt)]
            mstate[g]["d_wtmt"] = d_wtmt

        def mach_s2(g):
            gi = g % GLAG
            ms = mstate[g]
            W02 = W02g[g % 2]
            PT2 = PT2g[g % 2]
            d_wtmt, d_ktm = ms["d_wtmt"], ms["d_ktm"]
            tw0 = 0
            for (cp, base) in ((0, 0), (1, 512)):
                tw0 = TRANS(pT[:, base:base + 128], WtMt2[gi][:, cp * 256:cp * 256 + 128],
                            sIbf[:], waits=[("dv", d_wtmt), ("dv", d_ktm)] if cp == 0 else ())
            a_w0 = em.op("scalar", lambda h, W02=W02: h.activation(
                W02.rearrange("p (c x) -> p c x", x=128)[:, :, :],
                pT.rearrange("p (c x) -> p c x", x=128)[:, 0::4, :], AF.Copy),
                sem="ac", waits=[("pe", tw0)])
            hs["g_pT"] = [("ac", a_w0)]
            d_pt0 = em.op("vector", lambda h, gi=gi, PT2=PT2: h.tensor_sub(
                PT2[0].rearrange("p (c x) -> p c x", x=128)[:, :, :],
                sI2.rearrange("p (c x) -> p c x", x=128)[:, :, :],
                WtMt2[gi].rearrange("p (c x) -> p c x", x=128)[:, 0::2, :]),
                sem="dv", waits=[("dv", d_wtmt)])
            ms.update(PA=(W02[:, 0:128], WtMt2[gi][:, 0:128]),
                      PB=(W02[:, 128:256], WtMt2[gi][:, 256:384]),
                      sq_ev=[("ac", a_w0), ("dv", d_wtmt)],
                      PTcur=PT2[0][:], d_pt=d_pt0)

        def mach_lvl(g, k):
            gi = g % GLAG
            PQ2 = PQ2g[g % 2]
            P62 = P62g[g % 2]
            PT2 = PT2g[g % 2]
            ms = mstate[g]
            PA, PB = ms["PA"], ms["PB"]
            sq_ev = ms["sq_ev"]
            bank_g = hs["sq_bank_g"]
            PTcur = ms["PTcur"]
            d_pt = ms["d_pt"]
            bki = 2 + (k % 2)
            bk = pb[bki]
            if k < 6:
                MM(bk[0:128, 0:128], PA[1], PA[0], True, True, sem=None,
                   waits=sq_ev + bank_g[bki])
                MM(bk[0:128, 128:256], PA[0], PA[1], True, True, sem=None)
                MM(bk[0:128, 256:384], PB[1], PB[0], True, True, sem=None)
                mmq = MM(bk[0:128, 384:512], PB[0], PB[1], True, True)
                dst = PQ2[k % 2]
                a_ev = ACOPY(dst[:], bk[0:128, 0:512], waits=[("pe", mmq)])
                PA = (dst[:, 0:128], dst[:, 128:256])
                PB = (dst[:, 256:384], dst[:, 384:512])
                PkA, PkB = PA[0], PB[0]
            else:
                MM(bk[0:128, 0:128], PA[1], PA[0], True, True, sem=None,
                   waits=sq_ev + bank_g[bki])
                mmq = MM(bk[0:128, 128:256], PB[1], PB[0], True, True)
                a_ev = ACOPY(P62[:], bk[0:128, 0:256], waits=[("pe", mmq)])
                PkA, PkB = P62[:, 0:128], P62[:, 128:256]
            bank_g[bki] = [("ac", a_ev)]
            MM(pb[4][0:128, 0:128], PkA, PTcur[:, 0:128], True, True, sem=None,
               waits=[("ac", a_ev), ("dv", d_pt)] + bank_g[4])
            bank_g[4] = []
            mmp = MM(pb[4][0:128, 128:256], PkB, PTcur[:, 128:256], True, True)
            PTnew = Tnv2[gi][:] if k == 6 else PT2[k % 2][:]
            d_pt = em.op("vector", lambda h, o=PTnew, a=pb[4][0:128, 0:256], b=PTcur:
                         h.tensor_add(o, b, a), sem="dv", waits=[("pe", mmp)])
            bank_g[4] = [("dv", d_pt)]
            ms.update(PA=PA, PB=PB, sq_ev=[("ac", a_ev)], PTcur=PTnew, d_pt=d_pt)

        def tail_x1(g):
            gi = g % GLAG
            ms = mstate[g]
            is_out = 2 * g >= OCH0
            dz = em.val("dv")
            x1 = 0
            for cp in range(2):
                x1 = MM(pb[1][0:128, cp * 128:cp * 128 + 128],
                        Tnv2[gi][:, cp * 128:cp * 128 + 128],
                        AWK2[gi][:, cp * 128:cp * 128 + 128], True, True,
                        waits=([("dv", ms["d_pt"]), ("dv", dz)] + hs["g_p1"]) if cp == 0 else ())
                if is_out:
                    x1 = MM(pb[1][0:128, 256 + cp * 128:256 + cp * 128 + 128],
                            AWK2[gi][:, cp * 128:cp * 128 + 128],
                            Tnv2[gi][:, cp * 128:cp * 128 + 128], True, True)
            a_x1 = ACOPY(X1b2[gi][:], pb[1][0:128, 0:256], waits=[("pe", x1)])
            if is_out:
                a_x1 = ACOPY(X1T2[gi][:], pb[1][0:128, 256:512])
            ms["a_x1"] = a_x1

        def tail_ul(g):
            gi = g % GLAG
            ms = mstate[g]
            ul = 0
            for cp in range(2):
                ul = MM(pb[1][0:128, cp * 256:cp * 256 + 256],
                        Tnv2[gi][:, cp * 128:cp * 128 + 128],
                        bV2[gi][:, cp * 256:cp * 256 + 256], True, True,
                        waits=[("ac", ms["a_x1"])] if cp == 0 else ())
            ms["a_ul"] = ACOPY(Uloc2[gi][:], pb[1][0:128, 0:512], waits=[("pe", ul)])

        def tail_b(g):
            gi = g % GLAG
            ms = mstate[g]
            bb = 0
            for cp in range(2):
                bb = MM(pb[1][0:128, cp * 256:cp * 256 + 256],
                        Ktm2[gi][:, cp * 128:cp * 128 + 128],
                        Uloc2[gi][:, cp * 256:cp * 256 + 256], True, True,
                        waits=[("ac", ms["a_ul"])] if cp == 0 else ())
            ms["a_b"] = ACOPY(B2[gi][:], pb[1][0:128, 0:512], waits=[("pe", bb)])

        def tail_mt(g):
            gi = g % GLAG
            ms = mstate[g]
            mt = 0
            for cp in range(2):
                mt = MM(pb[1][0:128, cp * 128:cp * 128 + 128],
                        X1b2[gi][:, cp * 128:cp * 128 + 128],
                        Ktm2[gi][:, cp * 128:cp * 128 + 128], True, True,
                        waits=[("ac", ms["a_b"])] if cp == 0 else ())
            d_mt = 0
            for (jj, cp) in ((2 * g, 0), (2 * g + 1, 1)):
                d_mt = em.op("vector", lambda h, jj=jj, cp=cp, gi=gi:
                             h.scalar_tensor_tensor(
                                 MT2[gi][:, cp * 128:cp * 128 + 128], sIbf[:],
                                 ACrep[:, jj:jj + 1],
                                 pb[1][0:128, cp * 128:cp * 128 + 128],
                                 ALU.mult, ALU.add),
                             sem="dv", waits=[("pe", mt), ("dv", d_acr)])
            hs["g_p1"] = [("dv", d_mt)]
            ms["d_mt"] = d_mt

        def stages_of(g):
            st = [lambda g=g: mach_s0(g), lambda g=g: mach_s1(g), lambda g=g: mach_s2(g)]
            st += [lambda g=g, k=k: mach_lvl(g, k) for k in range(1, 7)]
            st += [lambda g=g: tail_x1(g), lambda g=g: tail_ul(g),
                   lambda g=g: tail_b(g), lambda g=g: tail_mt(g)]
            return st

        def chain(j):
            gi = (j // 2) % GLAG
            cp = j % 2
            cur = j % 2
            is_out = j >= OCH0
            qloc = j * 128 - QT0
            ms = mstate[j // 2]
            p = j % 2
            Sb = pbE[0:128, 0:256] if p == 0 else pbE[0:128, 512:768]
            Ub = pbE[0:128, 768:1024] if p == 0 else pbE[0:128, 256:512]
            gS = hs["g_S0"] if p == 0 else hs["g_S1"]
            gX = hs["g_U1"] if p == 0 else hs["g_U0"]
            s1 = MM(Sb, MT2[gi][:, cp * 128:cp * 128 + 128], St[cur][:], True, True,
                    waits=[("dv", hs["d_chain_prev"]), ("dv", ms["d_mt"])] + gS + gX)
            d_S = em.op("vector", lambda h, gi=gi, cp=cp, cur=cur, Sb=Sb:
                        h.tensor_add(St[1 - cur][:], B2[gi][:, cp * 256:cp * 256 + 256], Sb),
                        sem="dv", waits=[("pe", s1), ("ac", ms["a_b"])])
            hs["d_chain_prev"] = d_S
            if p == 0:
                hs["g_S0"] = [("dv", d_S)]
            else:
                hs["g_S1"] = [("dv", d_S)]

            if is_out:
                gU = hs["g_U0"] if p == 0 else hs["g_U1"]
                u2 = MM(Ub, X1T2[gi][:, cp * 128:cp * 128 + 128], St[cur][:], True, True,
                        waits=[("ac", ms["a_x1"])] + gU)
                d_U = em.op("vector", lambda h, gi=gi, cp=cp, Ub=Ub: h.tensor_add(
                    Ut[:], Uloc2[gi][:, cp * 256:cp * 256 + 256], Ub),
                    sem="dv", waits=[("pe", u2)])
                ob = j - OCH0
                mt = WtMt2[gi][:, cp * 256 + 128:cp * 256 + 256]
                MM(Ub[0:128, 0:128], St[cur][:, 0:128], q2[:, qloc:qloc + 128],
                   True, False, sem=None, waits=[("dv", d_U)])
                MM(Ub[0:128, 0:128], Ut[:, 0:128], mt, False, True, sem=None)
                MM(Ub[0:128, 128:256], St[cur][:, 128:256], q2[:, qloc:qloc + 128],
                   True, False, sem=None)
                oo = MM(Ub[0:128, 128:256], Ut[:, 128:256], mt, False, True)
                a_o2 = em.op("scalar", lambda h, ob=ob, Ub=Ub: h.activation(
                    outs_s.rearrange("p (v t) -> p v t", t=OWN)[:, :, ob * 128:(ob + 1) * 128],
                    Ub.rearrange("p (v t) -> p v t", t=128)[:, 0:2, :], AF.Copy),
                    sem="ac", waits=[("pe", oo)])
                if p == 0:
                    hs["g_U0"] = [("ac", a_o2)]
                else:
                    hs["g_U1"] = [("ac", a_o2)]

        NG = NCH // 2
        pend = {g: stages_of(g) for g in range(NG)}

        def pump(g, n):
            for _ in range(n):
                if pend.get(g):
                    pend[g].pop(0)()

        pump(0, 13)
        pump(1, 6)
        for g in range(NG):
            chain(2 * g)
            a = pend.get(g + 1, [])
            b = pend.get(g + 2, [])
            na, nb = len(a), min(6, len(b))
            for i in range(max(na, nb)):
                if i < na:
                    pump(g + 1, 1)
                if i < nb:
                    pump(g + 2, 1)
            chain(2 * g + 1)
        a_out_prev = em.val("ac")

        st1 = DMAS(outs_d[:, (2 * hd) * OWN:(2 * hd + 1) * OWN], outs_s[:, 0:OWN],
                   waits=[("ac", a_out_prev), ("dv", em.val("dv"))])
        st2 = DMAS(outs_d[:, (2 * hd + 1) * OWN:(2 * hd + 2) * OWN], outs_s[:, OWN:2 * OWN])
        prev_stores = [("dls", st2)]
        slab_war = [("pe", em.val("pe")), ("dv", em.val("dv")), ("ac", em.val("ac"))]

    em.wait("sync", "dls", em.val("dls"))
    em.barrier("mix")
    arena.reset()


def emit_wo(ctx):
    em = ctx["em"]; arena = ctx["arena"]
    pb = ctx["pb"]; pbE = ctx["pbE"]
    Wo = ctx["Wo"]; x_out = ctx["x_out"]
    outs_d, out1_d, xn2_d = ctx["outs_d"], ctx["out1_d"], ctx["xn2_d"]
    DMAS, DMAG, DMAA = ctx["DMAS"], ctx["DMAG"], ctx["DMAA"]
    colsA, colsB, epscol = ctx["colsA"], ctx["colsB"], ctx["epscol"]

    wo_s = arena.alloc("woW", [128, 32 * 1024], BF16)
    os_s = [arena.alloc("woO", [128, 32 * 512], BF16) for _ in range(2)]
    xo_s = [arena.alloc("woX", [128, 1024], F32) for _ in range(2)]
    o1 = [arena.alloc("woS", [128, 1024], F32) for _ in range(2)]
    outs3 = outs_d.rearrange("p (vc t) -> p vc t", t=OWN)

    os_war = [[], []]
    xo_war = [[], []]
    o1_war = [[], []]
    bank_g = [[], [], [], []]
    wo_war = []
    for dh in range(2):
        wlv = []
        for vc in range(32):
            wlv.append(DMAG(wo_s[:, vc * 1024:(vc + 1) * 1024],
                            Wo[vc * 128:(vc + 1) * 128, dh * 1024:(dh + 1) * 1024],
                            waits=wo_war))
        wo_war = []
        for tq in range(4):
            qp = tq % 2
            lo = DMAS(os_s[qp].rearrange("p (vc t) -> p vc t", t=512),
                      outs3[:, :, tq * 512:(tq + 1) * 512], waits=os_war[qp])
            os_war[qp] = []
            for ti in range(4):
                tch = tq * 4 + ti
                tp = tch % 2
                lx = DMAA(xo_s[tp][:], x_out[tch * 128:(tch + 1) * 128,
                                             dh * 1024:(dh + 1) * 1024],
                          waits=xo_war[tp])
                for dc in range(2):
                    bki = 2 * tp + dc
                    bk = pb[bki]
                    for vc in range(32):
                        w = [("dlg", wlv[vc])]
                        if vc == 0:
                            w += [("dls", lo)] + bank_g[bki]
                            bank_g[bki] = []
                        mmv = MM_g(em, bk[0:128, 0:512],
                                   os_s[qp][:, vc * 512 + ti * 128:vc * 512 + ti * 128 + 128],
                                   wo_s[:, vc * 1024 + dc * 512:vc * 1024 + dc * 512 + 512],
                                   vc == 0, vc == 31, w)
                    ev = em.op("vector", lambda h, o=o1[tp][:, dc * 512:(dc + 1) * 512],
                               a=bk[0:128, 0:512],
                               b=xo_s[tp][:, dc * 512:(dc + 1) * 512]:
                               h.tensor_add(o, a, b), sem="dv",
                               waits=[("pe", mmv), ("dla", lx)] + o1_war[tp])
                    o1_war[tp] = []
                    bank_g[bki] = [("dv", ev)]
                st = DMAS(out1_d[tch * 128:(tch + 1) * 128, dh * 1024:(dh + 1) * 1024],
                          o1[tp][:], waits=[("dv", em.val("dv"))])
                o1_war[tp] = [("dls", st)]
                xo_war[tp] = [("dv", em.val("dv"))]
            os_war[qp] = [("pe", em.val("pe"))]
        wo_war = [("pe", em.val("pe"))]
    em.wait("sync", "dls", em.val("dls"))
    em.barrier("wo_mm")
    arena.reset()

    xt = [arena.alloc("l2x", [128, D], F32) for _ in range(2)]
    sqs = [arena.alloc("l2q", [128, D], F32) for _ in range(2)]
    xnb = [arena.alloc("l2n", [128, D], BF16) for _ in range(2)]
    lastc = {}
    for i in range(OWN // 128):
        p = i % 2
        cols = colsA if p == 0 else colsB
        csum, cssq, cngm, cssd, cnvr, clnv, crsd = (cols[:, jj:jj+1] for jj in range(7))
        w = lastc.get(("xt", p), [])
        ld = DMAS(xt[p][:], out1_d[i*128:(i+1)*128, :], waits=w)
        em.op("scalar", lambda h, o=sqs[p][:], i_=xt[p][:], a=csum:
              h.activation(o, i_, AF.Copy, accum_out=a), sem="ac",
              waits=[("dls", ld)] + lastc.get(("cols", p), []))
        em.op("scalar", lambda h, o=sqs[p][:], i_=xt[p][:], a=cssq:
              h.activation(o, i_, AF.Square, accum_out=a), sem="ac")
        a_sq = em.val("ac")
        em.op("vector", lambda h, o=cngm, i_=csum: h.tensor_scalar_mul(o, i_, -1.0 / D),
              sem="dv", waits=[("ac", a_sq)])
        em.op("vector", lambda h, o=cssd, i_=cssq: h.tensor_scalar_mul(o, i_, 1.0 / D), sem="dv")
        em.op("vector", lambda h, o=cnvr, i0=cngm, s=cngm, i1=cssd:
              h.scalar_tensor_tensor(o, i0, s, i1, ALU.mult, ALU.subtract), sem="dv")
        d_var = em.val("dv")
        em.op("scalar", lambda h, o=clnv, i_=cnvr:
              h.activation(o, i_, AF.Ln, scale=-1.0, bias=epscol[:]),
              sem="ac", waits=[("dv", d_var)])
        em.op("scalar", lambda h, o=crsd, i_=clnv: h.activation(o, i_, AF.Exp, scale=-0.5),
              sem="ac")
        a_rs = em.val("ac")
        em.op("vector", lambda h, o=xnb[p][:], i_=xt[p][:], s1=cngm, s2=crsd:
              h.tensor_scalar(o, i_, s1, s2, ALU.add, ALU.mult), sem="dv",
              waits=[("ac", a_rs)] + lastc.get(("xnb", p), []))
        d_app = em.val("dv")
        st = DMAS(xn2_d[i*128:(i+1)*128, :], xnb[p][:], waits=[("dv", d_app)])
        lastc[("xt", p)] = [("ac", a_sq), ("dv", d_app)]
        lastc[("cols", p)] = [("dv", d_app)]
        lastc[("xnb", p)] = [("dls", st)]
    em.wait("sync", "dls", em.val("dls"))
    em.barrier("ln2")
    arena.reset()


def MM_g(em, out_ap, lhsT, rhs, st, sp, waits=()):
    return em.op("tensor", lambda h, o=out_ap, l=lhsT, r=rhs, s=st, e=sp:
                 h.matmul(o, l, r, start=s, stop=e),
                 sem="pe" if sp else None, waits=waits)


def emit_ffn(ctx):
    em = ctx["em"]; arena = ctx["arena"]
    pb = ctx["pb"]
    WgT, WuT, WdT = ctx["WgT"], ctx["WuT"], ctx["WdT"]
    xn2_d, gated_d, out1_d = ctx["xn2_d"], ctx["gated_d"], ctx["out1_d"]
    out = ctx["out"]
    DMAS, DMAG, DMAA = ctx["DMAS"], ctx["DMAG"], ctx["DMAA"]

    xf = arena.alloc("ffX", [128, 16 * OWN], BF16)
    wstr2 = [arena.alloc(f"ffW{i}", [128, 16 * 1024], BF16) for i in range(2)]
    gs = arena.alloc("ffG", [128, 8 * OWN], BF16)
    gmu = [arena.alloc("ffM", [128, OWN], BF16) for _ in range(2)]
    tl = 0
    for dc in range(16):
        tl = ctx["DMAT"](xf[:, dc * OWN:(dc + 1) * OWN],
                         xn2_d[:, dc * 128:(dc + 1) * 128])
    bank_g = [[] for _ in range(4)]
    gmu_war = [[], []]
    gs_war = []
    wldv = {}

    def wload(idx, guard):
        he, phase = idx // 2, idx % 2
        wstr = wstr2[idx % 2]
        h0 = he * 1024
        wl = 0
        for kc in range(16):
            wl = DMAG(wstr[:, kc * 1024:(kc + 1) * 1024],
                      (WgT if phase == 0 else WuT)[kc * 128:(kc + 1) * 128, h0:h0 + 1024],
                      waits=guard)
        wldv[idx] = wl

    wload(0, [])
    wload(1, [])
    for he in range(8):
        h0 = he * 1024
        for phase in range(2):
            idx = he * 2 + phase
            wstr = wstr2[idx % 2]
            wl = wldv[idx]
            for hof in range(8):
                for ts in range(4):
                    bki = (hof * 4 + ts) % 4
                    bk = pb[bki]
                    for kc in range(16):
                        w = []
                        if kc == 0:
                            w = [("dlg", wl), ("dls", tl)] + bank_g[bki]
                            bank_g[bki] = []
                        mmv = MM_g(em, bk[0:128, 0:512],
                                   wstr[:, kc * 1024 + hof * 128:kc * 1024 + hof * 128 + 128],
                                   xf[:, kc * OWN + ts * 512:kc * OWN + ts * 512 + 512],
                                   kc == 0, kc == 15, w)
                    if phase == 0:
                        a_ev = em.op("scalar", lambda h, o=gs[:, hof * OWN + ts * 512:hof * OWN + ts * 512 + 512],
                                     i_=bk[0:128, 0:512]:
                                     h.activation(o, i_, AF.Silu), sem="ac",
                                     waits=[("pe", mmv)] + gs_war)
                        gs_war = []
                        bank_g[bki] = [("ac", a_ev)]
                    else:
                        hp = hof % 2
                        ev = em.op("vector", lambda h, o=gmu[hp][:, ts * 512:ts * 512 + 512],
                                   a=bk[0:128, 0:512],
                                   b=gs[:, hof * OWN + ts * 512:hof * OWN + ts * 512 + 512]:
                                   h.tensor_mul(o, a, b), sem="dv",
                                   waits=[("pe", mmv), ("ac", a_gate)] + gmu_war[hp])
                        gmu_war[hp] = []
                        bank_g[bki] = [("dv", ev)]
                if phase == 1:
                    hp = hof % 2
                    st = DMAS(gated_d[h0 + hof * 128:h0 + hof * 128 + 128, :],
                              gmu[hp][:], waits=[("dv", em.val("dv"))])
                    gmu_war[hp] = [("dls", st)]
            if idx + 2 < 16:
                wload(idx + 2, [("pe", em.val("pe"))])
            if phase == 0:
                a_gate = em.val("ac")
            else:
                gs_war = [("dv", em.val("dv"))]
    em.wait("sync", "dls", em.val("dls"))
    em.barrier("gup")
    arena.reset()

    wd_s2 = [arena.alloc("fdW0", [128, 64 * 512], BF16)]
    gld = [arena.alloc("fdG", [128, 64 * 256], BF16) for _ in range(2)]
    o1t = [arena.alloc("fdO", [128, 512], F32) for _ in range(2)]
    ost = [arena.alloc("fdS", [128, 512], F32) for _ in range(2)]
    g3 = gated_d.rearrange("(hc p) t -> p hc t", p=128)
    bank_g = [[] for _ in range(4)]
    gld_war = [[], []]
    o1t_war = [[], []]
    ost_war = [[], []]
    wdldv = {}

    def wdload(dq, guard):
        wl = 0
        for hc in range(64):
            wl = DMAG(wd_s2[0][:, hc * 512:(hc + 1) * 512],
                      WdT[hc * 128:(hc + 1) * 128, dq * 512:dq * 512 + 512], waits=guard)
        wdldv[dq] = wl

    wdload(0, [])
    for dq in range(4):
        d0 = dq * 512
        wd_s = wd_s2[0]
        wl = wdldv[dq]
        for tp2 in range(8):
            pp = tp2 % 2
            lg = DMAS(gld[pp].rearrange("p (hc t) -> p hc t", t=256),
                      g3[:, :, tp2 * 256:(tp2 + 1) * 256], waits=gld_war[pp])
            gld_war[pp] = []
            for ti in range(2):
                tch = tp2 * 2 + ti
                bki = tch % 2 + 2 * (dq % 2)
                bk = pb[bki]
                lx = DMAA(o1t[pp][:], out1_d[tch * 128:(tch + 1) * 128, d0:d0 + 512],
                          waits=o1t_war[pp])
                for hc in range(64):
                    w = []
                    if hc == 0:
                        w = [("dlg", wl), ("dls", lg)] + bank_g[bki]
                        bank_g[bki] = []
                    mmv = MM_g(em, bk[0:128, 0:512],
                               gld[pp][:, hc * 256 + ti * 128:hc * 256 + ti * 128 + 128],
                               wd_s[:, hc * 512:(hc + 1) * 512],
                               hc == 0, hc == 63, w)
                ev = em.op("vector", lambda h, o=ost[pp][:], a=bk[0:128, 0:512],
                           b=o1t[pp][:]:
                           h.tensor_add(o, a, b), sem="dv",
                           waits=[("pe", mmv), ("dla", lx)] + ost_war[pp])
                bank_g[bki] = [("dv", ev)]
                o1t_war[pp] = [("dv", ev)]
                st = DMAS(out[tch * 128:(tch + 1) * 128, d0:d0 + 512], ost[pp][:],
                          waits=[("dv", ev)])
                ost_war[pp] = [("dls", st)]
            gld_war[pp] = [("pe", em.val("pe"))]
        if dq + 1 < 4:
            wdload(dq + 1, [("pe", em.val("pe"))])



def _consts():
    r = np.arange(128)
    cIbf = np.eye(128, dtype=ml_dtypes.bfloat16)
    cIf = np.eye(128, dtype=np.float32)
    cMaskW = np.where(r[None, :] >= r[:, None], np.float32(NEG), np.float32(0))
    cMaskM = np.where(r[None, :] > r[:, None], np.float32(NEG), np.float32(0))
    return {
        "cIbf": cIbf, "cIf": cIf,
        "cMaskW": cMaskW.astype(np.float32), "cMaskM": cMaskM.astype(np.float32),
        "cOneRow": np.ones((1, 128), np.float32),
        "cOneRowB": np.ones((1, 512), ml_dtypes.bfloat16),
        "cOneColB": np.ones((128, 1), ml_dtypes.bfloat16),
        "cTinyB": np.full((1, 1), 1e-24, ml_dtypes.bfloat16),
    }


def make_in_maps(inputs):
    x = np.ascontiguousarray(inputs["L0_attn_x"], np.float32)
    Wab = np.ascontiguousarray(
        np.concatenate([inputs["L0_attn_Wa"], inputs["L0_attn_Wb"]], axis=1), np.float32)
    shared = dict(
        Wk=np.ascontiguousarray(inputs["L0_attn_Wk"], np.float32),
        Wv=np.ascontiguousarray(inputs["L0_attn_Wv"], np.float32),
        Wq=np.ascontiguousarray(inputs["L0_attn_Wq"], np.float32),
        Wab=Wab,
        Wo=np.ascontiguousarray(inputs["L0_attn_Wo"], np.float32),
        qcw=np.ascontiguousarray(inputs["L0_attn_qcw"], np.float32),
        qcb=np.ascontiguousarray(inputs["L0_attn_qcb"], np.float32).reshape(D, 1),
        kcw=np.ascontiguousarray(inputs["L0_attn_kcw"], np.float32),
        kcb=np.ascontiguousarray(inputs["L0_attn_kcb"], np.float32).reshape(D, 1),
        vcw=np.ascontiguousarray(inputs["L0_attn_vcw"], np.float32),
        vcb=np.ascontiguousarray(inputs["L0_attn_vcb"], np.float32).reshape(NV, 1),
        WgT=np.ascontiguousarray(inputs["L0_ffn_w_gate"].T, np.float32),
        WuT=np.ascontiguousarray(inputs["L0_ffn_w_up"].T, np.float32),
        WdT=np.ascontiguousarray(inputs["L0_ffn_w_down"].T, np.float32),
        **_consts(),
    )
    ones_half = np.ones((HK, OWN), np.float32)
    zeros_half = np.zeros((HK, OWN), np.float32)
    cm_h0 = np.concatenate([zeros_half, ones_half], axis=1)
    cm_h1 = np.concatenate([ones_half, ones_half], axis=1)
    lncm_h0 = np.concatenate([np.full((HK, OWN), NEG, np.float32), zeros_half], axis=1)
    lncm_h1 = np.zeros((HK, T), np.float32)
    in_maps = []
    for c in range(8):
        b, hf = c // 2, c % 2
        if hf == 1:
            x_kvc = x[b]
            x_outc = x[b, OWN:]
            cm, lncm = cm_h1, lncm_h0 * 0
        else:
            x_kvc = np.concatenate([np.zeros((OWN, D), np.float32), x[b, :OWN]], axis=0)
            x_outc = x[b, :OWN]
            cm, lncm = cm_h0, lncm_h0
        in_maps.append(dict(shared,
                            x_kv=np.ascontiguousarray(x_kvc),
                            x_out=np.ascontiguousarray(x_outc),
                            cm16=cm, lncm16=lncm))
    return in_maps


_NC_CACHE = {}


def kernel(**inputs):
    key = (NPHASES, DEBUG_OUT)
    if key not in _NC_CACHE:
        _NC_CACHE[key] = build_program()
    nc = _NC_CACHE[key]
    in_maps = make_in_maps(inputs)
    res = run_bass_kernel_spmd(nc, in_maps, core_ids=list(range(8)))
    out = np.empty((B, T, D), np.float32)
    for c in range(8):
        b, hf = c // 2, c % 2
        out[b, hf * OWN:(hf + 1) * OWN] = res.results[c]["out"]
    kernel.last_results = res
    return out


# version 16 (final)
# speedup vs baseline: 1.2016x; 1.0027x over previous
import os
import numpy as np
import ml_dtypes
from contextlib import ExitStack

import concourse.bass as bass
import concourse.mybir as mybir
from concourse.bass_utils import run_bass_kernel_spmd

F32 = mybir.dt.float32
BF16 = mybir.dt.bfloat16
AF = mybir.ActivationFunctionType
ALU = mybir.AluOpType
AX = mybir.AxisListType

B = 4
T = 4096
D = 2048
HK = 16
DK = 128
VE = 256
NV = HK * VE
DH = 8192
C = 128
NCH = T // C
OWN = T // 2
OCH0 = NCH // 2
QT0 = OWN - 128
QTN = T - QT0
NQT = QTN // 512
NEG = -1.0e30

NPHASES = int(os.environ.get("KM_NPH", "6"))
DEBUG_OUT = os.environ.get("KM_DEBUG", "0") == "1"

ENGS = ("tensor", "vector", "scalar", "gpsimd", "sync")


class EM:

    def __init__(self, nc, stack):
        self.nc = nc
        self.stack = stack
        self.ops = {e: [] for e in ENGS}
        self.sems = {}
        self.counts = {}
        self.hwm = {e: {} for e in ENGS}

    def sem(self, name):
        if name not in self.sems:
            self.sems[name] = self.stack.enter_context(self.nc.semaphore(name))
            self.counts[name] = 0
        return self.sems[name]

    def val(self, name):
        return self.counts.get(name, 0)

    def wait(self, eng, name, v=None):
        if v is None:
            v = self.counts.get(name, 0)
        if v <= 0 or self.hwm[eng].get(name, 0) >= v:
            return
        self.hwm[eng][name] = v
        s = self.sem(name)
        self.ops[eng].append(lambda h, s=s, v=v: h.wait_ge(s, v))

    def op(self, eng, builder, sem=None, n=1, waits=()):
        for w in waits:
            if isinstance(w, tuple):
                self.wait(eng, w[0], w[1])
            else:
                self.wait(eng, w)
        if sem is not None:
            s = self.sem(sem)
            self.counts[sem] += n
            def play(h, builder=builder, s=s, n=n):
                builder(h).then_inc(s, n)
        else:
            def play(h, builder=builder):
                builder(h)
        self.ops[eng].append(play)
        return self.counts.get(sem, 0) if sem else None

    def play(self, eng, h):
        for fn in self.ops[eng]:
            fn(h)

    def barrier(self, tag):
        s = self.sem("bar")
        target = self.counts["bar"] + len(ENGS)
        for e in ENGS:
            self.counts["bar"] += 1
            self.ops[e].append(lambda h, s=s: h.sem_inc(s, 1))
        for e in ENGS:
            self.hwm[e]["bar"] = target
            self.ops[e].append(lambda h, s=s, v=target: h.wait_ge(s, v))


class Arena:

    def __init__(self, nc, base, limit):
        self.nc = nc
        self.base = base
        self.limit = limit
        self.cur = base
        self.n = 0

    def alloc(self, name, shape, dtype):
        esz = {F32: 4, BF16: 2, mybir.dt.int32: 4}[dtype]
        nb = int(np.prod(shape[1:])) * esz
        off = (self.cur + 63) & ~63
        assert off + nb <= self.limit, f"arena overflow: {name} {off}+{nb}>{self.limit}"
        self.cur = off + nb
        self.n += 1
        return self.nc.alloc_sbuf_tensor_at(f"{name}_{self.n}", list(shape), dtype, offset=off)

    def reset(self):
        self.cur = self.base


class Bank:

    def __init__(self, h):
        self.h = h
        self.prod = None
        self.cons = None


def dget(dt_np):
    return {np.dtype(np.float32): F32}.get(np.dtype(dt_np), F32)


ARENA_BASE = 44 * 1024
ARENA_LIMIT = 224 * 1024 - 64


def build_program():
    nc = bass.Bass("TRN2", target_bir_lowering=False)
    stk = ExitStack()
    em = EM(nc, stk)

    def inp(name, shape, dt=F32):
        return nc.dram_tensor(name, list(shape), dt, kind="ExternalInput").ap()

    x_kv = inp("x_kv", (T, D))
    x_out = inp("x_out", (OWN, D))
    cm16 = inp("cm16", (HK, T))
    lncm16 = inp("lncm16", (HK, T))
    Wk = inp("Wk", (D, D)); Wv = inp("Wv", (D, NV)); Wq = inp("Wq", (D, D))
    Wab = inp("Wab", (D, 32))
    Wo = inp("Wo", (NV, D))
    qcw = inp("qcw", (D, 4)); qcb = inp("qcb", (D, 1))
    kcw = inp("kcw", (D, 4)); kcb = inp("kcb", (D, 1))
    vcw = inp("vcw", (NV, 4)); vcb = inp("vcb", (NV, 1))
    WgT = inp("WgT", (D, DH)); WuT = inp("WuT", (D, DH)); WdT = inp("WdT", (DH, D))
    cIbf = inp("cIbf", (128, 128), BF16)
    cIf = inp("cIf", (128, 128), F32)
    cMaskW = inp("cMaskW", (128, 128), F32)
    cMaskM = inp("cMaskM", (128, 128), F32)
    cOneRow = inp("cOneRow", (1, 128), F32)
    cOneRowB = inp("cOneRowB", (1, 512), BF16)
    cOneColB = inp("cOneColB", (128, 1), BF16)
    cTinyB = inp("cTinyB", (1, 1), BF16)

    kind_s = "ExternalOutput" if DEBUG_OUT else "Internal"
    def scratch(name, shape, dt):
        return nc.dram_tensor(name, list(shape), dt, kind=kind_s).ap()

    xn1 = scratch("xn1", (T, D), BF16)
    kr = scratch("kr", (D, T), BF16)
    vr = scratch("vr", (NV, T), BF16)
    qr = scratch("qr", (D, QTN), BF16)
    abz = scratch("abz", (32, T), F32)
    wrow_d = scratch("wrow_d", (HK, T), F32)
    carow_d = scratch("carow_d", (HK, T), F32)
    aclast_d = scratch("aclast_d", (HK, NCH), F32)
    tmsl_d = scratch("tmsl_d", (128, NCH * 64), F32)
    outs_d = scratch("outs_d", (128, 32 * OWN), BF16)
    out1_d = scratch("out1_d", (OWN, D), F32)
    xn2_d = scratch("xn2_d", (OWN, D), BF16)
    gated_d = scratch("gated_d", (DH, OWN), BF16)
    out = nc.dram_tensor("out", [OWN, D], F32, kind="ExternalOutput").ap()

    def perm(name, shape, dt):
        return stk.enter_context(nc.sbuf_tensor(name, list(shape), dt))

    sIbf = perm("sIbf", [128, 128], BF16)
    sIf = perm("sIf", [128, 128], F32)
    sMaskW = perm("sMaskW", [128, 128], F32)
    sMaskM = perm("sMaskM", [128, 128], F32)
    sOneRow = perm("sOneRow", [1, 128], F32)
    sOneRowB = perm("sOneRowB", [1, 512], BF16)
    sOneColB = perm("sOneColB", [128, 1], BF16)
    sTinyB = perm("sTinyB", [1, 1], BF16)
    sI2 = perm("sI2", [128, 256], BF16)
    sMaskWb = perm("sMaskWb", [128, 128], BF16)
    sMaskMb = perm("sMaskMb", [128, 128], BF16)
    zrow = perm("zrow", [16, 128], F32)
    tmsl = perm("tmsl", [128, NCH * 64], F32)
    colsA = perm("colsA", [128, 16], F32)
    colsB = perm("colsB", [128, 16], F32)
    epscol = perm("epscol", [128, 1], F32)

    arena = Arena(nc, ARENA_BASE, ARENA_LIMIT)

    pb = [stk.enter_context(nc.psum_tensor(f"pb{i}", [128, 512], F32)) for i in range(5)]
    pbE = stk.enter_context(nc.psum_tensor("pbE", [128, 1024], F32))
    pT = stk.enter_context(nc.psum_tensor("pT", [128, 1024], BF16))

    def DMAS(out_ap, in_ap, waits=()):
        return em.op("sync", lambda h, o=out_ap, i=in_ap: h.dma_start(out=o, in_=i),
                     sem="dls", n=16, waits=waits)

    def DMAT(out_ap, in_ap, waits=()):
        return em.op("sync", lambda h, o=out_ap, i=in_ap: h.dma_start_transpose(o, i),
                     sem="dls", n=16, waits=waits)

    def DMAG(out_ap, in_ap, waits=()):
        return em.op("gpsimd", lambda h, o=out_ap, i=in_ap: h.dma_start(out=o, in_=i),
                     sem="dlg", n=16, waits=waits)

    def DMAA(out_ap, in_ap, waits=()):
        return em.op("scalar", lambda h, o=out_ap, i=in_ap: h.dma_start(out=o, in_=i),
                     sem="dla", n=16, waits=waits)

    DMAS(sIbf[:], cIbf); DMAS(sIf[:], cIf)
    DMAS(sMaskW[:], cMaskW); DMAS(sMaskM[:], cMaskM)
    DMAS(sOneRow[:], cOneRow); DMAS(sOneRowB[:], cOneRowB)
    DMAS(sOneColB[:], cOneColB); DMAS(sTinyB[:], cTinyB)
    DMAS(sI2[:, 0:128], cIbf); DMAS(sI2[:, 128:256], cIbf)
    DMAG(sMaskWb[:], cMaskW); DMAG(sMaskMb[:], cMaskM)
    em.op("vector", lambda h: h.memset(zrow[:], 0.0), sem="dv")
    em.op("vector", lambda h: h.memset(epscol[:], 1e-5), sem="dv")
    c_init = em.val("dls")
    em.wait("sync", "dls", c_init)
    em.wait("gpsimd", "dlg", em.val("dlg"))
    em.barrier("init")

    xt = [arena.alloc("xt", [128, D], F32) for _ in range(2)]
    sqs = [arena.alloc("sqs", [128, D], F32) for _ in range(2)]
    xnb = [arena.alloc("xnb", [128, D], BF16) for _ in range(2)]
    lastc = {}
    for i in range(T // 128):
        p = i % 2
        cols = colsA if p == 0 else colsB
        csum, cssq, cngm, cssd, cnvr, clnv, crsd = (cols[:, j:j+1] for j in range(7))
        w = lastc.get(("xt", p), [])
        ld = DMAS(xt[p][:], x_kv[i*128:(i+1)*128, :], waits=w)
        em.op("scalar", lambda h, o=sqs[p][:], i_=xt[p][:], a=csum:
              h.activation(o, i_, AF.Copy, accum_out=a), sem="ac",
              waits=[("dls", ld)] + lastc.get(("cols", p), []))
        em.op("scalar", lambda h, o=sqs[p][:], i_=xt[p][:], a=cssq:
              h.activation(o, i_, AF.Square, accum_out=a), sem="ac")
        a_sq = em.val("ac")
        em.op("vector", lambda h, o=cngm, i_=csum: h.tensor_scalar_mul(o, i_, -1.0 / D),
              sem="dv", waits=[("ac", a_sq)])
        em.op("vector", lambda h, o=cssd, i_=cssq: h.tensor_scalar_mul(o, i_, 1.0 / D), sem="dv")
        em.op("vector", lambda h, o=cnvr, i0=cngm, s=cngm, i1=cssd:
              h.scalar_tensor_tensor(o, i0, s, i1, ALU.mult, ALU.subtract), sem="dv")
        d_var = em.val("dv")
        em.op("scalar", lambda h, o=clnv, i_=cnvr: h.activation(o, i_, AF.Ln, scale=-1.0, bias=epscol[:]),
              sem="ac", waits=[("dv", d_var)])
        em.op("scalar", lambda h, o=crsd, i_=clnv: h.activation(o, i_, AF.Exp, scale=-0.5), sem="ac")
        a_rs = em.val("ac")
        em.op("vector", lambda h, o=xnb[p][:], i_=xt[p][:], s1=cngm, s2=crsd:
              h.tensor_scalar(o, i_, s1, s2, ALU.add, ALU.mult), sem="dv",
              waits=[("ac", a_rs)] + lastc.get(("xnb", p), []))
        d_app = em.val("dv")
        st = DMAS(xn1[i*128:(i+1)*128, :], xnb[p][:], waits=[("dv", d_app)])
        lastc[("xt", p)] = [("ac", a_sq), ("dv", d_app)]
        lastc[("cols", p)] = [("dv", d_app)]
        lastc[("xnb", p)] = [("dls", st)]
        if i == T // 128 // 2 - 1:
            ln1_half = st
    ln1_full = em.val("dls")
    arena.reset()

    if NPHASES >= 2:
        xng = arena.alloc("xng", [128, 16 * QTN], BF16)
        wbuf = [arena.alloc("wbuf", [128, 16 * 1024], BF16) for _ in range(2)]
        stg = [arena.alloc("stg", [128, QTN], BF16) for _ in range(2)]
        stgab = [arena.alloc("stgab", [32, 2048], F32) for _ in range(2)]
        wbuf_war = [[], []]
        stg_war = [[], []]
        stgab_war = [[], []]
        bank_war = [[] for _ in range(7)]
        bkctr = [0]

        xng_war = [[]]

        def proj_pass(tg, t0, tw, segs):
            lnw = [("dls", ln1_half if tg == 0 else ln1_full)]
            tl = 0
            for dc in range(16):
                tl = DMAT(xng[:, dc * tw:(dc + 1) * tw],
                          xn1[t0:t0 + tw, dc * 128:(dc + 1) * 128],
                          waits=xng_war[0] + lnw)
            xng_war[0] = []
            ntsub_full, rem = divmod(tw, 512)
            tsubs = [(i * 512, 512) for i in range(ntsub_full)]
            if rem:
                tsubs.append((ntsub_full * 512, rem))
            for (W, OFW, slab, slabw, f32out) in segs:
                for of0 in range(0, OFW, 1024):
                    sw = min(1024, OFW - of0)
                    wp = (of0 // 1024) % 2
                    wl = 0
                    for kc in range(16):
                        wl = DMAG(wbuf[wp][:, kc * sw:(kc + 1) * sw],
                                  W[kc * 128:(kc + 1) * 128, of0:of0 + sw],
                                  waits=wbuf_war[wp])
                    wbuf_war[wp] = []
                    for of8 in range(max(1, sw // 128)):
                        m = min(128, sw)
                        ofc = of0 + of8 * 128
                        sp = (ofc // 128) % 2
                        stgw = stgab_war[sp] if f32out else stg_war[sp]
                        sbuf_stg = stgab[sp] if f32out else stg[sp]
                        for (ts0, tsw) in tsubs:
                            bk = bkctr[0] % 4
                            bkctr[0] += 1
                            pbk = pb[bk]
                            for kc in range(16):
                                wts = []
                                if kc == 0:
                                    wts = [("dlg", wl), ("dls", tl)] + bank_war[bk]
                                    bank_war[bk] = []
                                em.op("tensor",
                                      lambda h, o=pbk[0:m, 0:tsw],
                                      l=wbuf[wp][:, kc * sw + of8 * 128: kc * sw + of8 * 128 + m],
                                      r=xng[:, kc * tw + ts0: kc * tw + ts0 + tsw],
                                      s=(kc == 0), e=(kc == 15):
                                      h.matmul(o, l, r, start=s, stop=e),
                                      sem="pe" if kc == 15 else None, waits=wts)
                            mmv = em.val("pe")
                            if bkctr[0] % 2 == 0:
                                ev = em.op("vector",
                                           lambda h, o=sbuf_stg[0:m, ts0:ts0 + tsw], i_=pbk[0:m, 0:tsw]:
                                           h.tensor_copy(o, i_), sem="dv",
                                           waits=[("pe", mmv)] + stgw)
                                bank_war[bk] = [("dv", ev)]
                            else:
                                ev = em.op("scalar",
                                           lambda h, o=sbuf_stg[0:m, ts0:ts0 + tsw], i_=pbk[0:m, 0:tsw]:
                                           h.activation(o, i_, AF.Copy), sem="ac",
                                           waits=[("pe", mmv)] + stgw)
                                bank_war[bk] = [("ac", ev)]
                            stgw.clear()
                        ofr = (ofc // 128) if not f32out else 0
                        dst = slab[ofc:ofc + m, slabw[0]:slabw[0] + tw] if f32out else \
                            slab[ofc:ofc + m, slabw[0]:slabw[0] + tw]
                        wv = [("dv", em.val("dv")), ("ac", em.val("ac"))]
                        st = DMAS(dst, sbuf_stg[0:m, 0:tw], waits=wv)
                        if f32out:
                            stgab_war[sp] = [("dls", st)]
                        else:
                            stg_war[sp] = [("dls", st)]
                    wbuf_war[wp] = [("pe", em.val("pe"))]
            xng_war[0] = [("pe", em.val("pe"))]

        for tg in range(2):
            t0 = tg * 2048
            proj_pass(tg, t0, 2048, [
                (Wab, 32, abz, (t0,), True),
                (Wk, 2048, kr, (t0,), False),
                (Wv, NV, vr, (t0,), False),
            ])
        proj_pass(2, QT0, QTN, [(Wq, 2048, qr, (0,), False)])
        em.wait("sync", "dls", em.val("dls"))
        em.barrier("proj")
        arena.reset()

    if NPHASES >= 3:
        Ab = arena.alloc("sc_a", [16, T], F32)
        Bb = arena.alloc("sc_b", [16, T], F32)
        Sb = arena.alloc("sc_s", [16, T], F32)
        Lb = arena.alloc("sc_l", [16, T], F32)
        CMb = arena.alloc("sc_cm", [16, T], F32)
        LCb = arena.alloc("sc_lc", [16, T], F32)
        sacl = arena.alloc("sc_acl", [16, NCH], F32)

        l1 = DMAS(Ab[:], abz[0:16, :])
        l2 = DMAS(Bb[:], abz[16:32, :])
        l3 = DMAS(CMb[:], cm16)
        l4 = DMAS(LCb[:], lncm16)

        def ACTOP(dst, src, fn, sc=1.0, bi=0.0, waits=(), accum=None):
            return em.op("scalar", lambda h, o=dst, i_=src, f=fn, s=sc, b=bi, a=accum:
                         h.activation(o, i_, f, scale=s, bias=b, accum_out=a),
                         sem="ac", waits=waits)

        def logsig(dst, z, wz):
            ACTOP(Sb[:], z[:], AF.Abs, waits=wz)
            ACTOP(Sb[:], Sb[:], AF.Exp, sc=-1.0)
            a1 = ACTOP(Sb[:], Sb[:], AF.Ln, bi=1.0)
            em.op("vector", lambda h, o=dst[:], i_=z[:]: h.tensor_scalar_min(o, i_, 0.0),
                  sem="dv", waits=[("ac", a1)])
            return em.op("vector", lambda h, o=dst[:], a=dst[:], b=Sb[:]:
                         h.tensor_sub(o, a, b), sem="dv")

        logsig(Lb, Ab, [("dls", l1)])
        d_la = em.val("dv")
        logsig(Ab, Bb, [("dls", l2), ("dv", d_la)])
        a_sg = ACTOP(Sb[:], Bb[:], AF.Sigmoid, waits=[("dv", em.val("dv"))])
        em.op("vector", lambda h: h.tensor_mul(Bb[:], Sb[:], CMb[:]), sem="dv",
              waits=[("ac", a_sg), ("dls", l3)])
        em.op("vector", lambda h: h.tensor_add(Ab[:], Ab[:], LCb[:]), sem="dv",
              waits=[("dls", l4)])
        for j in range(NCH):
            em.op("vector", lambda h, j=j: h.tensor_tensor_scan(
                CMb[:, j * C:(j + 1) * C], Lb[:, j * C:(j + 1) * C], zrow[:],
                0.0, ALU.add, ALU.add), sem="dv")
        em.op("vector", lambda h: h.tensor_scalar_mul(LCb[:], CMb[:], -1.0), sem="dv")
        d_ca = em.val("dv")
        em.op("vector", lambda h: h.tensor_add(Sb[:], CMb[:], Ab[:]), sem="dv")
        d_wr = em.val("dv")
        a_naw = ACTOP(Ab[:], Sb[:], AF.Exp, waits=[("dv", d_wr)])
        em.op("vector", lambda h: h.tensor_scalar_mul(Ab[:], Ab[:], -1.0), sem="dv",
              waits=[("ac", a_naw)])
        for j in range(NCH):
            em.op("vector", lambda h, j=j: h.tensor_single_scalar(
                Lb[:, j * C:(j + 1) * C], CMb[:, j * C:(j + 1) * C],
                CMb[:, j * C + C - 1:j * C + C], ALU.subtract),
                sem="dv", waits=[("dv", d_ca)])
        d_rp = em.val("dv")
        a_rac = ACTOP(Lb[:], Lb[:], AF.Exp, sc=-1.0, waits=[("dv", d_rp)])
        a_acl = ACTOP(sacl[:], CMb[:, C - 1::C], AF.Exp, waits=[("dv", d_ca)])
        st1 = DMAS(wrow_d, Sb[:], waits=[("dv", em.val("dv"))])
        st2 = DMAS(carow_d, CMb[:])
        st3 = DMAS(aclast_d, sacl[:], waits=[("ac", a_acl)])
        d_all = em.val("dv"); a_all = em.val("ac")
        tm_war = []
        for j in range(NCH):
            srcs = [LCb, Bb, Ab, Lb]
            for r in range(4):
                w = ([("dv", d_all), ("ac", a_all)] + tm_war) if r == 0 else ()
                em.op("tensor", lambda h, s=srcs[r], j=j, r=r:
                      h.transpose(pb[4][0:128, r * 16:(r + 1) * 16],
                                  s[:, j * C:(j + 1) * C], sIf[0:16, 0:16]),
                      sem="pe" if r == 3 else None, waits=w)
            pev = em.val("pe")
            ev = em.op("vector", lambda h, j=j: h.tensor_copy(
                tmsl[:, j * 64:(j + 1) * 64], pb[4][0:128, 0:64]),
                sem="dv", waits=[("pe", pev)])
            tm_war = [("dv", ev)]
        if DEBUG_OUT:
            DMAS(tmsl_d, tmsl[:], waits=[("dv", em.val("dv"))])
        em.wait("sync", "dls", em.val("dls"))
        em.barrier("scal")
        arena.reset()

    ctx = dict(locals())
    if NPHASES >= 4:
        emit_mix(ctx)
    if NPHASES >= 5:
        emit_wo(ctx)
    if NPHASES >= 6:
        emit_ffn(ctx)

    for s in ("dls", "dlg", "dla"):
        em.wait("sync", s, em.val(s))
    em.barrier("fin")

    blk = stk.enter_context(nc.Block())
    blk.tensor(lambda h: em.play("tensor", h))
    blk.vector(lambda h: em.play("vector", h))
    blk.scalar(lambda h: em.play("scalar", h))
    blk.gpsimd(lambda h: em.play("gpsimd", h))
    blk.sync(lambda h: em.play("sync", h))
    stk.close()
    return nc


def emit_mix(ctx):
    em = ctx["em"]; arena = ctx["arena"]
    pb = ctx["pb"]; pbE = ctx["pbE"]; pT = ctx["pT"]
    kr, vr, qr = ctx["kr"], ctx["vr"], ctx["qr"]
    wrow_d, carow_d, aclast_d = ctx["wrow_d"], ctx["carow_d"], ctx["aclast_d"]
    outs_d = ctx["outs_d"]; tmsl = ctx["tmsl"]
    sIbf, sIf = ctx["sIbf"], ctx["sIf"]
    sMaskW, sMaskM = ctx["sMaskW"], ctx["sMaskM"]
    sMaskWb, sMaskMb = ctx["sMaskWb"], ctx["sMaskMb"]
    sOneRow, sOneRowB, sTinyB = ctx["sOneRow"], ctx["sOneRowB"], ctx["sTinyB"]
    sOneColB = ctx["sOneColB"]; sI2 = ctx["sI2"]
    qcw, qcb, kcw, kcb, vcw, vcb = (ctx[k] for k in ("qcw", "qcb", "kcw", "kcb", "vcw", "vcb"))
    DMAS, DMAG = ctx["DMAS"], ctx["DMAG"]
    PAD = 4

    slabK = arena.alloc("mxK", [128, PAD + T], BF16)
    slabV0 = arena.alloc("mxV0", [128, PAD + T], BF16)
    slabV1 = arena.alloc("mxV1", [128, PAD + T], BF16)
    slabQ = arena.alloc("mxQ", [128, PAD + QTN], BF16)
    q2 = arena.alloc("mxQ2", [128, QTN], BF16)
    sqt = arena.alloc("mxSq", [128, 512], BF16)
    krnb = arena.alloc("mxKrn", [1, T], BF16)
    wrowh = arena.alloc("mxWrow", [1, T], F32)
    mrow = arena.alloc("mxMrow", [1, QTN], F32)
    lnscr = arena.alloc("mxLns", [1, 1024], F32)
    amq = arena.alloc("mxAmq", [1, QTN], BF16)
    aclr = arena.alloc("mxAcl", [1, NCH], F32)
    ACrep = arena.alloc("mxACr", [128, NCH], F32)
    cwq = arena.alloc("mxCwq", [128, 4], F32); cbq = arena.alloc("mxCbq", [128, 1], F32)
    cwk = arena.alloc("mxCwk", [128, 4], F32); cbk = arena.alloc("mxCbk", [128, 1], F32)
    cwv0 = arena.alloc("mxCv0", [128, 4], F32); cbv0 = arena.alloc("mxCb0", [128, 1], F32)
    cwv1 = arena.alloc("mxCv1", [128, 4], F32); cbv1 = arena.alloc("mxCb1", [128, 1], F32)
    dg = [arena.alloc(f"mxDg{i}", [128, 128], BF16) for i in range(16)]
    GLAG = 3
    WtMt2 = [arena.alloc(f"mxWtMt{i}", [128, 512], BF16) for i in range(GLAG)]
    Tnv2 = [arena.alloc(f"mxTnv{i}", [128, 256], BF16) for i in range(GLAG)]
    bV2 = [arena.alloc(f"mxbV{i}", [128, 512], BF16) for i in range(GLAG)]
    Ktm2 = [arena.alloc(f"mxKtm{i}", [128, 256], BF16) for i in range(GLAG)]
    W02g = [arena.alloc(f"mxW02{i}", [128, 256], BF16) for i in range(2)]
    PQ2g = [[arena.alloc(f"mxPQ{i}_{p}", [128, 512], BF16) for p in range(2)] for i in range(2)]
    P62g = [arena.alloc(f"mxP62{i}", [128, 256], BF16) for i in range(2)]
    PT2g = [[arena.alloc(f"mxPT{i}_{p}", [128, 256], BF16) for p in range(2)] for i in range(2)]
    LWM2 = arena.alloc("mxLWM", [128, 512], F32)
    AWK2 = [arena.alloc(f"mxAWK{i}", [128, 256], BF16) for i in range(GLAG)]
    X1b2 = [arena.alloc(f"mxX1{i}", [128, 256], BF16) for i in range(GLAG)]
    X1T2 = [arena.alloc(f"mxXT{i}", [128, 256], BF16) for i in range(GLAG)]
    Uloc2 = [arena.alloc(f"mxUL{i}", [128, 512], BF16) for i in range(GLAG)]
    B2 = [arena.alloc(f"mxB{i}", [128, 512], BF16) for i in range(GLAG)]
    MT2 = [arena.alloc(f"mxMT{i}", [128, 256], BF16) for i in range(GLAG)]
    Ut = arena.alloc("mxU", [128, 256], BF16)
    St = [arena.alloc(f"mxS{i}", [128, 256], BF16) for i in range(2)]
    outs_s = arena.alloc("mxOuts", [128, 2 * OWN], BF16)

    def MM(bank_ap, lhsT, rhs, st, sp, sem="pe", waits=()):
        return em.op("tensor", lambda h, o=bank_ap, l=lhsT, r=rhs, s=st, e=sp:
                     h.matmul(o, l, r, start=s, stop=e), sem=sem, waits=waits)

    def TRANS(bank_ap, in_ap, ident, waits=()):
        return em.op("tensor", lambda h, o=bank_ap, i_=in_ap, d=ident:
                     h.transpose(o, i_, d), sem="pe", waits=waits)

    def ACOPY(dst, src, waits=()):
        return em.op("scalar", lambda h, o=dst, i_=src:
                     h.activation(o, i_, AF.Copy), sem="ac", waits=waits)

    def AEXP(dst, src, bias, waits=(), scale=1.0):
        return em.op("scalar", lambda h, o=dst, i_=src, b=bias, s=scale:
                     h.activation(o, i_, AF.Exp, bias=b, scale=s), sem="ac", waits=waits)

    prev_stores = []
    slab_war = []
    for hd in range(HK):
        lw = list(slab_war)
        lK = DMAS(slabK[:, PAD:], kr[hd * 128:(hd + 1) * 128, :], waits=lw)
        lV0 = DMAS(slabV0[:, PAD:], vr[hd * 256:hd * 256 + 128, :], waits=lw)
        lV1 = DMAS(slabV1[:, PAD:], vr[hd * 256 + 128:hd * 256 + 256, :], waits=lw)
        lQ = DMAS(slabQ[:, PAD:], qr[hd * 128:(hd + 1) * 128, :], waits=lw)
        lrw = DMAS(wrowh[:], wrow_d[hd:hd + 1, :], waits=lw)
        lmr = DMAG(mrow[:], carow_d[hd:hd + 1, QT0:], waits=lw)
        lacl = DMAG(aclr[:], aclast_d[hd:hd + 1, :])
        lcw = DMAG(cwq[:], qcw[hd * 128:(hd + 1) * 128, :])
        DMAG(cbq[:], qcb[hd * 128:(hd + 1) * 128, :])
        DMAG(cwk[:], kcw[hd * 128:(hd + 1) * 128, :])
        DMAG(cbk[:], kcb[hd * 128:(hd + 1) * 128, :])
        DMAG(cwv0[:], vcw[hd * 256:hd * 256 + 128, :])
        DMAG(cbv0[:], vcb[hd * 256:hd * 256 + 128, :])
        DMAG(cwv1[:], vcw[hd * 256 + 128:hd * 256 + 256, :])
        lcb = DMAG(cbv1[:], vcb[hd * 256 + 128:hd * 256 + 256, :])
        z1 = em.op("vector", lambda h: h.memset(slabK[:, 0:PAD], 0.0), sem="dv",
                   waits=[("dls", lK)])
        em.op("vector", lambda h: h.memset(slabV0[:, 0:PAD], 0.0), sem="dv")
        em.op("vector", lambda h: h.memset(slabV1[:, 0:PAD], 0.0), sem="dv")
        em.op("vector", lambda h: h.memset(slabQ[:, 0:PAD], 0.0), sem="dv")

        slabs = [(slabK, T, cwk, cbk, 0), (slabV0, T, cwv0, cbv0, 1),
                 (slabV1, T, cwv1, cbv1, 2), (slabQ, QTN, cwq, cbq, 3)]
        for si, (slab, ncols, cw, cb, di) in enumerate(slabs):
            for j in range(4):
                em.op("vector", lambda h, j=j, c=cw, di=di: h.tensor_scalar_mul(
                    dg[di * 4 + j][:], sIbf[:], c[:, j:j + 1]), sem="dv",
                    waits=[("dlg", lcb), ("pe", em.val("pe"))])
        dtap = em.val("dv")
        a_prep0 = em.val("ac")
        steps = []
        for si, (slab, ncols, cw, cb, di) in enumerate(slabs):
            tiles = []
            t0v = 0
            while t0v < ncols:
                tw = min(512, ncols - t0v)
                tiles.append((si, t0v, tw))
                t0v += tw
            steps.append(list(reversed(tiles)))
        order = []
        mx = max(len(s) for s in steps)
        for i in range(mx):
            for s in steps:
                if i < len(s):
                    order.append(s[i])
        silu_g = {}
        for n_, (si, t0v, tw) in enumerate(order):
            slab, ncols, cw, cb, di = slabs[si]
            bki = n_ % 4
            bk = pb[bki]
            for j in range(4):
                MM(bk[0:128, 0:tw], dg[di * 4 + j][:],
                   slab[:, PAD + t0v - 3 + j:PAD + t0v - 3 + j + tw],
                   j == 0, j == 3,
                   sem="pe" if j == 3 else None,
                   waits=[("dv", dtap), ("ac", max(a_prep0, silu_g.get(bki, 0)))] if j == 0 else ())
            pev = em.val("pe")
            silu_g[bki] = em.op("scalar", lambda h, o=slab[:, PAD + t0v:PAD + t0v + tw],
                                i_=bk[0:128, 0:tw], b=cb:
                                h.activation(o, i_, AF.Silu, bias=b[:]), sem="ac",
                                waits=[("pe", pev), ("dlg", lcb)])
        a_conv_q = em.val("ac")

        for rnd in range(4):
            for half in range(2):
                ti = rnd * 2 + half
                a_sq = em.op("scalar", lambda h, o=sqt[:], i_=slabK[:, PAD + ti * 512:PAD + (ti + 1) * 512]:
                             h.activation(o, i_, AF.Square), sem="ac",
                             waits=[("dv", em.val("dv")), ("pe", em.val("pe"))])
                MM(pbE[0:1, half * 512:half * 512 + 512], sOneColB[:], sqt[:], True, False,
                   sem=None, waits=[("ac", a_sq)])
                MM(pbE[0:1, half * 512:half * 512 + 512], sTinyB[:], sOneRowB[:], False, True,
                   sem="pe")
            pev = em.val("pe")
            a_ln = em.op("scalar", lambda h, o=lnscr[:], i_=pbE[0:1, 0:1024]:
                         h.activation(o, i_, AF.Ln), sem="ac", waits=[("pe", pev)])
            em.op("scalar", lambda h, o=krnb[0:1, rnd * 1024:(rnd + 1) * 1024], i_=lnscr[:]:
                  h.activation(o, i_, AF.Exp, scale=-0.5), sem="ac")
        a_krn = em.val("ac")
        for ti in range(8):
            mmv = MM(pb[2][0:128, 0:512], sOneRowB[0:1, 0:128],
                     krnb[0:1, ti * 512:(ti + 1) * 512], True, True,
                     waits=[("ac", a_krn), ("dv", em.val("dv"))])
            em.op("vector", lambda h, ti=ti, o=slabK[:, PAD + ti * 512:PAD + (ti + 1) * 512],
                  a=slabK[:, PAD + ti * 512:PAD + (ti + 1) * 512], b=pb[2][0:128, 0:512]:
                  h.tensor_mul(o, a, b), sem="dv", waits=[("pe", mmv)])
        d_khat = em.val("dv")

        qtiles = [(0, 512), (512, 512), (1024, 512), (1536, 512), (2048, 128)]
        for rnd, grp in enumerate([qtiles[0:2], qtiles[2:4], qtiles[4:5]]):
            for gi, (t0v, tw) in enumerate(grp):
                a_sq = em.op("scalar", lambda h, o=sqt[0:128, 0:tw],
                             i_=slabQ[:, PAD + t0v:PAD + t0v + tw], tw=tw:
                             h.activation(o, i_, AF.Square), sem="ac",
                             waits=[("dv", em.val("dv")), ("pe", em.val("pe"))])
                MM(pbE[0:1, gi * 512:gi * 512 + tw], sOneColB[:], sqt[0:128, 0:tw], True, False,
                   sem=None, waits=[("ac", a_sq)])
                MM(pbE[0:1, gi * 512:gi * 512 + tw], sTinyB[:], sOneRowB[0:1, 0:tw], False, True,
                   sem="pe")
            gw = sum(tw for _, tw in grp)
            g0 = grp[0][0]
            pev = em.val("pe")
            a_ln = em.op("scalar", lambda h, o=lnscr[0:1, 0:gw], i_=pbE[0:1, 0:gw], gw=gw:
                         h.activation(o, i_, AF.Ln), sem="ac", waits=[("pe", pev)])
            em.op("vector", lambda h, o=mrow[0:1, g0:g0 + gw], i0=lnscr[0:1, 0:gw],
                  i1=mrow[0:1, g0:g0 + gw], gw=gw:
                  h.scalar_tensor_tensor(o, i0, -0.5, i1, ALU.mult, ALU.add),
                  sem="dv", waits=[("ac", a_ln), ("dlg", lmr)])
        d_mrow = em.val("dv")
        a_amq = em.op("scalar", lambda h: h.activation(amq[:], mrow[:], AF.Exp),
                      sem="ac", waits=[("dv", d_mrow)])
        for (t0v, tw) in qtiles:
            mmv = MM(pb[2][0:128, 0:tw], sOneRowB[0:1, 0:128], amq[0:1, t0v:t0v + tw],
                     True, True, waits=[("ac", a_amq), ("dv", em.val("dv"))])
            em.op("vector", lambda h, t0v=t0v, tw=tw, o=q2[:, t0v:t0v + tw],
                  a=slabQ[:, PAD + t0v:PAD + t0v + tw], b=pb[2][0:128, 0:tw]:
                  h.tensor_mul(o, a, b), sem="dv", waits=[("pe", mmv)])
        mmv = MM(pb[2][0:128, 0:NCH], sOneRow[:], aclr[:], True, True,
                 waits=[("dlg", lacl), ("dv", em.val("dv"))])
        d_acr = em.op("vector", lambda h: h.tensor_copy(ACrep[:], pb[2][0:128, 0:NCH]),
                      sem="dv", waits=[("pe", mmv)])
        em.op("vector", lambda h: h.memset(St[0][:], 0.0), sem="dv",
              waits=prev_stores)
        em.op("vector", lambda h: h.memset(outs_s[:], 0.0), sem="dv")
        d_prep = em.val("dv")

        a_prep = em.val("ac")
        hs = {
            "d_chain_prev": d_prep, "g_pT": [("ac", a_prep)],
            "g_p0": [("ac", a_prep)], "g_p1": [("ac", a_prep)],
            "g_S0": [("ac", a_prep)], "g_S1": [("ac", a_prep)],
            "g_U0": [("ac", a_prep)], "g_U1": [("ac", a_prep)],
            "d_lwm_war": d_prep,
            "sq_bank_g": {2: [("dv", d_prep)], 3: [("dv", d_prep)], 4: [("dv", d_prep)]},
        }
        mstate = {}

        def col(j, r):
            return tmsl[:, j * 64 + 16 * r + hd:j * 64 + 16 * r + hd + 1]

        def mach_s0(g):
            gi = g % GLAG
            jA, jB = 2 * g, 2 * g + 1
            wv = [("dv", d_khat), ("ac", a_conv_q), ("dv", em.val("dv"))] + hs["g_pT"]
            tkv = 0
            for (j, base) in ((jA, 0), (jB, 512)):
                jt = PAD + j * 128
                TRANS(pT[:, base + 128:base + 256], slabV0[:, jt:jt + 128], sIbf[:],
                      waits=wv); wv = ()
                TRANS(pT[:, base + 256:base + 384], slabV1[:, jt:jt + 128], sIbf[:])
                tkv = TRANS(pT[:, base + 384:base + 512], slabK[:, jt:jt + 128], sIbf[:])
            d_bv = d_ktm = 0
            for (j, base, cp) in ((jA, 0, 0), (jB, 512, 1)):
                d_bv = em.op("vector", lambda h, b=col(j, 1), base=base, cp=cp, gi=gi:
                             h.tensor_scalar_mul(bV2[gi][:, cp * 256:cp * 256 + 256],
                                                 pT[:, base + 128:base + 384], b),
                             sem="dv", waits=[("pe", tkv)])
                em.op("vector", lambda h, r=col(j, 3), base=base, cp=cp, gi=gi:
                      h.tensor_scalar_mul(Ktm2[gi][:, cp * 128:cp * 128 + 128],
                                          pT[:, base + 384:base + 512], r), sem="dv")
                d_ktm = em.op("vector", lambda h, n=col(j, 2), base=base, cp=cp, gi=gi:
                              h.tensor_scalar_mul(AWK2[gi][:, cp * 128:cp * 128 + 128],
                                                  pT[:, base + 384:base + 512], n), sem="dv")
            mstate[g] = dict(d_bv=d_bv, d_ktm=d_ktm)

        def mach_s1(g):
            gi = g % GLAG
            jA, jB = 2 * g, 2 * g + 1
            is_out = jA >= OCH0
            kk0 = 0
            for (j, base) in ((jA, 0), (jB, 256)):
                jt = PAD + j * 128
                w = ([("dv", em.val("dv"))] + hs["g_p0"]) if base == 0 else ()
                kk0 = MM(pb[0][0:128, base:base + 128], slabK[:, jt:jt + 128],
                         slabK[:, jt:jt + 128], True, True, waits=w)
                if is_out:
                    qloc = j * 128 - QT0
                    kk0 = MM(pb[0][0:128, base + 128:base + 256], slabK[:, jt:jt + 128],
                             slabQ[:, PAD + qloc:PAD + qloc + 128], True, True)
            ew = 0
            for (j, base) in ((jA, 0), (jB, 256)):
                w = hs["g_p1"] if base == 0 else ()
                MM(pb[1][0:128, base:base + 128], sMaskWb[:], sIbf[:], True, False,
                   sem=None, waits=w)
                ew = MM(pb[1][0:128, base:base + 128], sOneRow[:],
                        wrowh[0:1, j * 128:(j + 1) * 128], False, True,
                        waits=[("dls", lrw)])
                if is_out:
                    qloc = j * 128 - QT0
                    MM(pb[1][0:128, base + 128:base + 256], sMaskMb[:], sIbf[:], True, False,
                       sem=None)
                    ew = MM(pb[1][0:128, base + 128:base + 256], sOneRow[:],
                            mrow[0:1, qloc:qloc + 128], False, True)
            wid = 256 if is_out else 128
            a_lwm = 0
            for (j, base) in ((jA, 0), (jB, 256)):
                a_lwm = AEXP(LWM2[0:128, base:base + wid], pb[1][0:128, base:base + wid],
                             col(j, 0), waits=[("pe", ew), ("dv", hs["d_lwm_war"])])
            hs["g_p1"] = [("ac", a_lwm)]
            if is_out:
                d_wtmt = em.op("vector", lambda h, gi=gi: h.tensor_mul(
                    WtMt2[gi][:], LWM2[:], pb[0][0:128, 0:512]),
                    sem="dv", waits=[("ac", a_lwm), ("pe", kk0)])
            else:
                d_wtmt = em.op("vector", lambda h, gi=gi: h.tensor_mul(
                    WtMt2[gi].rearrange("p (c x) -> p c x", x=128)[:, 0::2, :],
                    LWM2.rearrange("p (c x) -> p c x", x=128)[:, 0::2, :],
                    pb[0].rearrange("p (c x) -> p c x", x=128)[:, 0::2, :]),
                    sem="dv", waits=[("ac", a_lwm), ("pe", kk0)])
            hs["d_lwm_war"] = d_wtmt
            hs["g_p0"] = [("dv", d_wtmt)]
            mstate[g]["d_wtmt"] = d_wtmt

        def mach_s2(g):
            gi = g % GLAG
            ms = mstate[g]
            W02 = W02g[g % 2]
            PT2 = PT2g[g % 2]
            d_wtmt, d_ktm = ms["d_wtmt"], ms["d_ktm"]
            tw0 = 0
            for (cp, base) in ((0, 0), (1, 512)):
                tw0 = TRANS(pT[:, base:base + 128], WtMt2[gi][:, cp * 256:cp * 256 + 128],
                            sIbf[:], waits=[("dv", d_wtmt), ("dv", d_ktm)] if cp == 0 else ())
            a_w0 = em.op("scalar", lambda h, W02=W02: h.activation(
                W02.rearrange("p (c x) -> p c x", x=128)[:, :, :],
                pT.rearrange("p (c x) -> p c x", x=128)[:, 0::4, :], AF.Copy),
                sem="ac", waits=[("pe", tw0)])
            hs["g_pT"] = [("ac", a_w0)]
            d_pt0 = em.op("vector", lambda h, gi=gi, PT2=PT2: h.tensor_sub(
                PT2[0].rearrange("p (c x) -> p c x", x=128)[:, :, :],
                sI2.rearrange("p (c x) -> p c x", x=128)[:, :, :],
                WtMt2[gi].rearrange("p (c x) -> p c x", x=128)[:, 0::2, :]),
                sem="dv", waits=[("dv", d_wtmt)])
            ms.update(PA=(W02[:, 0:128], WtMt2[gi][:, 0:128]),
                      PB=(W02[:, 128:256], WtMt2[gi][:, 256:384]),
                      sq_ev=[("ac", a_w0), ("dv", d_wtmt)],
                      PTcur=PT2[0][:], d_pt=d_pt0)

        def mach_lvl(g, k):
            gi = g % GLAG
            PQ2 = PQ2g[g % 2]
            P62 = P62g[g % 2]
            PT2 = PT2g[g % 2]
            ms = mstate[g]
            PA, PB = ms["PA"], ms["PB"]
            sq_ev = ms["sq_ev"]
            bank_g = hs["sq_bank_g"]
            PTcur = ms["PTcur"]
            d_pt = ms["d_pt"]
            bki = 2 + (k % 2)
            bk = pb[bki]
            if k < 6:
                MM(bk[0:128, 0:128], PA[1], PA[0], True, True, sem=None,
                   waits=sq_ev + bank_g[bki])
                MM(bk[0:128, 128:256], PA[0], PA[1], True, True, sem=None)
                MM(bk[0:128, 256:384], PB[1], PB[0], True, True, sem=None)
                mmq = MM(bk[0:128, 384:512], PB[0], PB[1], True, True)
                dst = PQ2[k % 2]
                a_ev = ACOPY(dst[:], bk[0:128, 0:512], waits=[("pe", mmq)])
                PA = (dst[:, 0:128], dst[:, 128:256])
                PB = (dst[:, 256:384], dst[:, 384:512])
                PkA, PkB = PA[0], PB[0]
            else:
                MM(bk[0:128, 0:128], PA[1], PA[0], True, True, sem=None,
                   waits=sq_ev + bank_g[bki])
                mmq = MM(bk[0:128, 128:256], PB[1], PB[0], True, True)
                a_ev = ACOPY(P62[:], bk[0:128, 0:256], waits=[("pe", mmq)])
                PkA, PkB = P62[:, 0:128], P62[:, 128:256]
            bank_g[bki] = [("ac", a_ev)]
            MM(pb[4][0:128, 0:128], PkA, PTcur[:, 0:128], True, True, sem=None,
               waits=[("ac", a_ev), ("dv", d_pt)] + bank_g[4])
            bank_g[4] = []
            mmp = MM(pb[4][0:128, 128:256], PkB, PTcur[:, 128:256], True, True)
            PTnew = Tnv2[gi][:] if k == 6 else PT2[k % 2][:]
            d_pt = em.op("vector", lambda h, o=PTnew, a=pb[4][0:128, 0:256], b=PTcur:
                         h.tensor_add(o, b, a), sem="dv", waits=[("pe", mmp)])
            bank_g[4] = [("dv", d_pt)]
            ms.update(PA=PA, PB=PB, sq_ev=[("ac", a_ev)], PTcur=PTnew, d_pt=d_pt)

        def tail_x1(g):
            gi = g % GLAG
            ms = mstate[g]
            is_out = 2 * g >= OCH0
            dz = em.val("dv")
            x1 = 0
            for cp in range(2):
                x1 = MM(pb[1][0:128, cp * 128:cp * 128 + 128],
                        Tnv2[gi][:, cp * 128:cp * 128 + 128],
                        AWK2[gi][:, cp * 128:cp * 128 + 128], True, True,
                        waits=([("dv", ms["d_pt"]), ("dv", dz)] + hs["g_p1"]) if cp == 0 else ())
                if is_out:
                    x1 = MM(pb[1][0:128, 256 + cp * 128:256 + cp * 128 + 128],
                            AWK2[gi][:, cp * 128:cp * 128 + 128],
                            Tnv2[gi][:, cp * 128:cp * 128 + 128], True, True)
            a_x1 = ACOPY(X1b2[gi][:], pb[1][0:128, 0:256], waits=[("pe", x1)])
            if is_out:
                a_x1 = ACOPY(X1T2[gi][:], pb[1][0:128, 256:512])
            ms["a_x1"] = a_x1

        def tail_ul(g):
            gi = g % GLAG
            ms = mstate[g]
            ul = 0
            for cp in range(2):
                ul = MM(pb[1][0:128, cp * 256:cp * 256 + 256],
                        Tnv2[gi][:, cp * 128:cp * 128 + 128],
                        bV2[gi][:, cp * 256:cp * 256 + 256], True, True,
                        waits=[("ac", ms["a_x1"])] if cp == 0 else ())
            ms["a_ul"] = ACOPY(Uloc2[gi][:], pb[1][0:128, 0:512], waits=[("pe", ul)])

        def tail_b(g):
            gi = g % GLAG
            ms = mstate[g]
            bb = 0
            for cp in range(2):
                bb = MM(pb[1][0:128, cp * 256:cp * 256 + 256],
                        Ktm2[gi][:, cp * 128:cp * 128 + 128],
                        Uloc2[gi][:, cp * 256:cp * 256 + 256], True, True,
                        waits=[("ac", ms["a_ul"])] if cp == 0 else ())
            ms["a_b"] = ACOPY(B2[gi][:], pb[1][0:128, 0:512], waits=[("pe", bb)])

        def tail_mt(g):
            gi = g % GLAG
            ms = mstate[g]
            mt = 0
            for cp in range(2):
                mt = MM(pb[1][0:128, cp * 128:cp * 128 + 128],
                        X1b2[gi][:, cp * 128:cp * 128 + 128],
                        Ktm2[gi][:, cp * 128:cp * 128 + 128], True, True,
                        waits=[("ac", ms["a_b"])] if cp == 0 else ())
            d_mt = 0
            for (jj, cp) in ((2 * g, 0), (2 * g + 1, 1)):
                d_mt = em.op("vector", lambda h, jj=jj, cp=cp, gi=gi:
                             h.scalar_tensor_tensor(
                                 MT2[gi][:, cp * 128:cp * 128 + 128], sIbf[:],
                                 ACrep[:, jj:jj + 1],
                                 pb[1][0:128, cp * 128:cp * 128 + 128],
                                 ALU.mult, ALU.add),
                             sem="dv", waits=[("pe", mt), ("dv", d_acr)])
            hs["g_p1"] = [("dv", d_mt)]
            ms["d_mt"] = d_mt

        def stages_of(g):
            st = [lambda g=g: mach_s0(g), lambda g=g: mach_s1(g), lambda g=g: mach_s2(g)]
            st += [lambda g=g, k=k: mach_lvl(g, k) for k in range(1, 7)]
            st += [lambda g=g: tail_x1(g), lambda g=g: tail_ul(g),
                   lambda g=g: tail_b(g), lambda g=g: tail_mt(g)]
            return st

        def chain(j):
            gi = (j // 2) % GLAG
            cp = j % 2
            cur = j % 2
            is_out = j >= OCH0
            qloc = j * 128 - QT0
            ms = mstate[j // 2]
            p = j % 2
            Sb = pbE[0:128, 0:256] if p == 0 else pbE[0:128, 512:768]
            Ub = pbE[0:128, 768:1024] if p == 0 else pbE[0:128, 256:512]
            gS = hs["g_S0"] if p == 0 else hs["g_S1"]
            gX = hs["g_U1"] if p == 0 else hs["g_U0"]
            s1 = MM(Sb, MT2[gi][:, cp * 128:cp * 128 + 128], St[cur][:], True, True,
                    waits=[("dv", hs["d_chain_prev"]), ("dv", ms["d_mt"])] + gS + gX)
            d_S = em.op("vector", lambda h, gi=gi, cp=cp, cur=cur, Sb=Sb:
                        h.tensor_add(St[1 - cur][:], B2[gi][:, cp * 256:cp * 256 + 256], Sb),
                        sem="dv", waits=[("pe", s1), ("ac", ms["a_b"])])
            hs["d_chain_prev"] = d_S
            if p == 0:
                hs["g_S0"] = [("dv", d_S)]
            else:
                hs["g_S1"] = [("dv", d_S)]

            if is_out:
                gU = hs["g_U0"] if p == 0 else hs["g_U1"]
                u2 = MM(Ub, X1T2[gi][:, cp * 128:cp * 128 + 128], St[cur][:], True, True,
                        waits=[("ac", ms["a_x1"])] + gU)
                d_U = em.op("vector", lambda h, gi=gi, cp=cp, Ub=Ub: h.tensor_add(
                    Ut[:], Uloc2[gi][:, cp * 256:cp * 256 + 256], Ub),
                    sem="dv", waits=[("pe", u2)])
                ob = j - OCH0
                mt = WtMt2[gi][:, cp * 256 + 128:cp * 256 + 256]
                MM(Ub[0:128, 0:128], St[cur][:, 0:128], q2[:, qloc:qloc + 128],
                   True, False, sem=None, waits=[("dv", d_U)])
                MM(Ub[0:128, 0:128], Ut[:, 0:128], mt, False, True, sem=None)
                MM(Ub[0:128, 128:256], St[cur][:, 128:256], q2[:, qloc:qloc + 128],
                   True, False, sem=None)
                oo = MM(Ub[0:128, 128:256], Ut[:, 128:256], mt, False, True)
                a_o2 = em.op("scalar", lambda h, ob=ob, Ub=Ub: h.activation(
                    outs_s.rearrange("p (v t) -> p v t", t=OWN)[:, :, ob * 128:(ob + 1) * 128],
                    Ub.rearrange("p (v t) -> p v t", t=128)[:, 0:2, :], AF.Copy),
                    sem="ac", waits=[("pe", oo)])
                if p == 0:
                    hs["g_U0"] = [("ac", a_o2)]
                else:
                    hs["g_U1"] = [("ac", a_o2)]

        NG = NCH // 2
        pend = {g: stages_of(g) for g in range(NG)}

        def pump(g, n):
            for _ in range(n):
                if pend.get(g):
                    pend[g].pop(0)()

        pump(0, 13)
        pump(1, 6)
        for g in range(NG):
            chain(2 * g)
            a = pend.get(g + 1, [])
            b = pend.get(g + 2, [])
            na, nb = len(a), min(6, len(b))
            for i in range(max(na, nb)):
                if i < na:
                    pump(g + 1, 1)
                if i < nb:
                    pump(g + 2, 1)
            chain(2 * g + 1)
        a_out_prev = em.val("ac")

        st1 = DMAS(outs_d[:, (2 * hd) * OWN:(2 * hd + 1) * OWN], outs_s[:, 0:OWN],
                   waits=[("ac", a_out_prev), ("dv", em.val("dv"))])
        st2 = DMAS(outs_d[:, (2 * hd + 1) * OWN:(2 * hd + 2) * OWN], outs_s[:, OWN:2 * OWN])
        prev_stores = [("dls", st2)]
        slab_war = [("pe", em.val("pe")), ("dv", em.val("dv")), ("ac", em.val("ac"))]

    em.wait("sync", "dls", em.val("dls"))
    em.barrier("mix")
    arena.reset()


def emit_wo(ctx):
    em = ctx["em"]; arena = ctx["arena"]
    pb = ctx["pb"]; pbE = ctx["pbE"]
    Wo = ctx["Wo"]; x_out = ctx["x_out"]
    outs_d, out1_d, xn2_d = ctx["outs_d"], ctx["out1_d"], ctx["xn2_d"]
    DMAS, DMAG, DMAA = ctx["DMAS"], ctx["DMAG"], ctx["DMAA"]
    colsA, colsB, epscol = ctx["colsA"], ctx["colsB"], ctx["epscol"]

    wo_s = arena.alloc("woW", [128, 32 * 1024], BF16)
    os_s = [arena.alloc("woO", [128, 32 * 512], BF16) for _ in range(2)]
    xo_s = [arena.alloc("woX", [128, 1024], F32) for _ in range(2)]
    o1 = [arena.alloc("woS", [128, 1024], F32) for _ in range(2)]
    outs3 = outs_d.rearrange("p (vc t) -> p vc t", t=OWN)

    os_war = [[], []]
    xo_war = [[], []]
    o1_war = [[], []]
    bank_g = [[], [], [], []]
    wo_war = []
    for dh in range(2):
        wlv = []
        for vc in range(32):
            wlv.append(DMAG(wo_s[:, vc * 1024:(vc + 1) * 1024],
                            Wo[vc * 128:(vc + 1) * 128, dh * 1024:(dh + 1) * 1024],
                            waits=wo_war))
        wo_war = []
        for tq in range(4):
            qp = tq % 2
            lo = DMAS(os_s[qp].rearrange("p (vc t) -> p vc t", t=512),
                      outs3[:, :, tq * 512:(tq + 1) * 512], waits=os_war[qp])
            os_war[qp] = []
            for ti in range(4):
                tch = tq * 4 + ti
                tp = tch % 2
                lx = DMAA(xo_s[tp][:], x_out[tch * 128:(tch + 1) * 128,
                                             dh * 1024:(dh + 1) * 1024],
                          waits=xo_war[tp])
                for dc in range(2):
                    bki = 2 * tp + dc
                    bk = pb[bki]
                    for vc in range(32):
                        w = [("dlg", wlv[vc])]
                        if vc == 0:
                            w += [("dls", lo)] + bank_g[bki]
                            bank_g[bki] = []
                        mmv = MM_g(em, bk[0:128, 0:512],
                                   os_s[qp][:, vc * 512 + ti * 128:vc * 512 + ti * 128 + 128],
                                   wo_s[:, vc * 1024 + dc * 512:vc * 1024 + dc * 512 + 512],
                                   vc == 0, vc == 31, w)
                    ev = em.op("vector", lambda h, o=o1[tp][:, dc * 512:(dc + 1) * 512],
                               a=bk[0:128, 0:512],
                               b=xo_s[tp][:, dc * 512:(dc + 1) * 512]:
                               h.tensor_add(o, a, b), sem="dv",
                               waits=[("pe", mmv), ("dla", lx)] + o1_war[tp])
                    o1_war[tp] = []
                    bank_g[bki] = [("dv", ev)]
                st = DMAS(out1_d[tch * 128:(tch + 1) * 128, dh * 1024:(dh + 1) * 1024],
                          o1[tp][:], waits=[("dv", em.val("dv"))])
                o1_war[tp] = [("dls", st)]
                xo_war[tp] = [("dv", em.val("dv"))]
            os_war[qp] = [("pe", em.val("pe"))]
        wo_war = [("pe", em.val("pe"))]
    em.wait("sync", "dls", em.val("dls"))
    em.barrier("wo_mm")
    arena.reset()

    xt = [arena.alloc("l2x", [128, D], F32) for _ in range(2)]
    sqs = [arena.alloc("l2q", [128, D], F32) for _ in range(2)]
    xnb = [arena.alloc("l2n", [128, D], BF16) for _ in range(2)]
    lastc = {}
    for i in range(OWN // 128):
        p = i % 2
        cols = colsA if p == 0 else colsB
        csum, cssq, cngm, cssd, cnvr, clnv, crsd = (cols[:, jj:jj+1] for jj in range(7))
        w = lastc.get(("xt", p), [])
        ld = DMAS(xt[p][:], out1_d[i*128:(i+1)*128, :], waits=w)
        em.op("scalar", lambda h, o=sqs[p][:], i_=xt[p][:], a=csum:
              h.activation(o, i_, AF.Copy, accum_out=a), sem="ac",
              waits=[("dls", ld)] + lastc.get(("cols", p), []))
        em.op("scalar", lambda h, o=sqs[p][:], i_=xt[p][:], a=cssq:
              h.activation(o, i_, AF.Square, accum_out=a), sem="ac")
        a_sq = em.val("ac")
        em.op("vector", lambda h, o=cngm, i_=csum: h.tensor_scalar_mul(o, i_, -1.0 / D),
              sem="dv", waits=[("ac", a_sq)])
        em.op("vector", lambda h, o=cssd, i_=cssq: h.tensor_scalar_mul(o, i_, 1.0 / D), sem="dv")
        em.op("vector", lambda h, o=cnvr, i0=cngm, s=cngm, i1=cssd:
              h.scalar_tensor_tensor(o, i0, s, i1, ALU.mult, ALU.subtract), sem="dv")
        d_var = em.val("dv")
        em.op("scalar", lambda h, o=clnv, i_=cnvr:
              h.activation(o, i_, AF.Ln, scale=-1.0, bias=epscol[:]),
              sem="ac", waits=[("dv", d_var)])
        em.op("scalar", lambda h, o=crsd, i_=clnv: h.activation(o, i_, AF.Exp, scale=-0.5),
              sem="ac")
        a_rs = em.val("ac")
        em.op("vector", lambda h, o=xnb[p][:], i_=xt[p][:], s1=cngm, s2=crsd:
              h.tensor_scalar(o, i_, s1, s2, ALU.add, ALU.mult), sem="dv",
              waits=[("ac", a_rs)] + lastc.get(("xnb", p), []))
        d_app = em.val("dv")
        st = DMAS(xn2_d[i*128:(i+1)*128, :], xnb[p][:], waits=[("dv", d_app)])
        lastc[("xt", p)] = [("ac", a_sq), ("dv", d_app)]
        lastc[("cols", p)] = [("dv", d_app)]
        lastc[("xnb", p)] = [("dls", st)]
    em.wait("sync", "dls", em.val("dls"))
    em.barrier("ln2")
    arena.reset()


def MM_g(em, out_ap, lhsT, rhs, st, sp, waits=()):
    return em.op("tensor", lambda h, o=out_ap, l=lhsT, r=rhs, s=st, e=sp:
                 h.matmul(o, l, r, start=s, stop=e),
                 sem="pe" if sp else None, waits=waits)


def emit_ffn(ctx):
    em = ctx["em"]; arena = ctx["arena"]
    pb = ctx["pb"]
    WgT, WuT, WdT = ctx["WgT"], ctx["WuT"], ctx["WdT"]
    xn2_d, gated_d, out1_d = ctx["xn2_d"], ctx["gated_d"], ctx["out1_d"]
    out = ctx["out"]
    DMAS, DMAG, DMAA = ctx["DMAS"], ctx["DMAG"], ctx["DMAA"]

    xf = arena.alloc("ffX", [128, 16 * OWN], BF16)
    wstr2 = [arena.alloc(f"ffW{i}", [128, 16 * 1024], BF16) for i in range(2)]
    gs = arena.alloc("ffG", [128, 8 * OWN], BF16)
    gmu = [arena.alloc("ffM", [128, OWN], BF16) for _ in range(2)]
    tl = 0
    for dc in range(16):
        tl = ctx["DMAT"](xf[:, dc * OWN:(dc + 1) * OWN],
                         xn2_d[:, dc * 128:(dc + 1) * 128])
    bank_g = [[] for _ in range(4)]
    gmu_war = [[], []]
    gs_war = []
    wldv = {}

    def wload(idx, guard):
        he, phase = idx // 2, idx % 2
        wstr = wstr2[idx % 2]
        h0 = he * 1024
        wl = 0
        for kc in range(16):
            wl = DMAG(wstr[:, kc * 1024:(kc + 1) * 1024],
                      (WgT if phase == 0 else WuT)[kc * 128:(kc + 1) * 128, h0:h0 + 1024],
                      waits=guard)
        wldv[idx] = wl

    wload(0, [])
    wload(1, [])
    for he in range(8):
        h0 = he * 1024
        for phase in range(2):
            idx = he * 2 + phase
            wstr = wstr2[idx % 2]
            wl = wldv[idx]
            for hof in range(8):
                for ts in range(4):
                    bki = (hof * 4 + ts) % 4
                    bk = pb[bki]
                    for kc in range(16):
                        w = []
                        if kc == 0:
                            w = [("dlg", wl), ("dls", tl)] + bank_g[bki]
                            bank_g[bki] = []
                        mmv = MM_g(em, bk[0:128, 0:512],
                                   wstr[:, kc * 1024 + hof * 128:kc * 1024 + hof * 128 + 128],
                                   xf[:, kc * OWN + ts * 512:kc * OWN + ts * 512 + 512],
                                   kc == 0, kc == 15, w)
                    if phase == 0:
                        a_ev = em.op("scalar", lambda h, o=gs[:, hof * OWN + ts * 512:hof * OWN + ts * 512 + 512],
                                     i_=bk[0:128, 0:512]:
                                     h.activation(o, i_, AF.Silu), sem="ac",
                                     waits=[("pe", mmv)] + gs_war)
                        gs_war = []
                        bank_g[bki] = [("ac", a_ev)]
                    else:
                        hp = hof % 2
                        ev = em.op("vector", lambda h, o=gmu[hp][:, ts * 512:ts * 512 + 512],
                                   a=bk[0:128, 0:512],
                                   b=gs[:, hof * OWN + ts * 512:hof * OWN + ts * 512 + 512]:
                                   h.tensor_mul(o, a, b), sem="dv",
                                   waits=[("pe", mmv), ("ac", a_gate)] + gmu_war[hp])
                        gmu_war[hp] = []
                        bank_g[bki] = [("dv", ev)]
                if phase == 1:
                    hp = hof % 2
                    st = DMAS(gated_d[h0 + hof * 128:h0 + hof * 128 + 128, :],
                              gmu[hp][:], waits=[("dv", em.val("dv"))])
                    gmu_war[hp] = [("dls", st)]
            if idx + 2 < 16:
                wload(idx + 2, [("pe", em.val("pe"))])
            if phase == 0:
                a_gate = em.val("ac")
            else:
                gs_war = [("dv", em.val("dv"))]
    em.wait("sync", "dls", em.val("dls"))
    em.barrier("gup")
    arena.reset()

    wd_s2 = [arena.alloc("fdW0", [128, 64 * 512], BF16)]
    gld = [arena.alloc("fdG", [128, 64 * 256], BF16) for _ in range(2)]
    o1t = [arena.alloc("fdO", [128, 512], F32) for _ in range(2)]
    ost = [arena.alloc("fdS", [128, 512], F32) for _ in range(2)]
    g3 = gated_d.rearrange("(hc p) t -> p hc t", p=128)
    bank_g = [[] for _ in range(4)]
    gld_war = [[], []]
    o1t_war = [[], []]
    ost_war = [[], []]
    wdldv = {}

    def wdload(dq, guard):
        wl = 0
        for hc in range(64):
            wl = DMAG(wd_s2[0][:, hc * 512:(hc + 1) * 512],
                      WdT[hc * 128:(hc + 1) * 128, dq * 512:dq * 512 + 512], waits=guard)
        wdldv[dq] = wl

    wdload(0, [])
    for dq in range(4):
        d0 = dq * 512
        wd_s = wd_s2[0]
        wl = wdldv[dq]
        for tp2 in range(8):
            pp = tp2 % 2
            lg = DMAS(gld[pp].rearrange("p (hc t) -> p hc t", t=256),
                      g3[:, :, tp2 * 256:(tp2 + 1) * 256], waits=gld_war[pp])
            gld_war[pp] = []
            for ti in range(2):
                tch = tp2 * 2 + ti
                bki = tch % 2 + 2 * (dq % 2)
                bk = pb[bki]
                lx = DMAA(o1t[pp][:], out1_d[tch * 128:(tch + 1) * 128, d0:d0 + 512],
                          waits=o1t_war[pp])
                for hc in range(64):
                    w = []
                    if hc == 0:
                        w = [("dlg", wl), ("dls", lg)] + bank_g[bki]
                        bank_g[bki] = []
                    mmv = MM_g(em, bk[0:128, 0:512],
                               gld[pp][:, hc * 256 + ti * 128:hc * 256 + ti * 128 + 128],
                               wd_s[:, hc * 512:(hc + 1) * 512],
                               hc == 0, hc == 63, w)
                ev = em.op("vector", lambda h, o=ost[pp][:], a=bk[0:128, 0:512],
                           b=o1t[pp][:]:
                           h.tensor_add(o, a, b), sem="dv",
                           waits=[("pe", mmv), ("dla", lx)] + ost_war[pp])
                bank_g[bki] = [("dv", ev)]
                o1t_war[pp] = [("dv", ev)]
                st = DMAS(out[tch * 128:(tch + 1) * 128, d0:d0 + 512], ost[pp][:],
                          waits=[("dv", ev)])
                ost_war[pp] = [("dls", st)]
            gld_war[pp] = [("pe", em.val("pe"))]
        if dq + 1 < 4:
            wdload(dq + 1, [("pe", em.val("pe"))])



def _consts():
    r = np.arange(128)
    cIbf = np.eye(128, dtype=ml_dtypes.bfloat16)
    cIf = np.eye(128, dtype=np.float32)
    cMaskW = np.where(r[None, :] >= r[:, None], np.float32(NEG), np.float32(0))
    cMaskM = np.where(r[None, :] > r[:, None], np.float32(NEG), np.float32(0))
    return {
        "cIbf": cIbf, "cIf": cIf,
        "cMaskW": cMaskW.astype(np.float32), "cMaskM": cMaskM.astype(np.float32),
        "cOneRow": np.ones((1, 128), np.float32),
        "cOneRowB": np.ones((1, 512), ml_dtypes.bfloat16),
        "cOneColB": np.ones((128, 1), ml_dtypes.bfloat16),
        "cTinyB": np.full((1, 1), 1e-24, ml_dtypes.bfloat16),
    }


def make_in_maps(inputs):
    x = np.ascontiguousarray(inputs["L0_attn_x"], np.float32)
    Wab = np.ascontiguousarray(
        np.concatenate([inputs["L0_attn_Wa"], inputs["L0_attn_Wb"]], axis=1), np.float32)
    shared = dict(
        Wk=np.ascontiguousarray(inputs["L0_attn_Wk"], np.float32),
        Wv=np.ascontiguousarray(inputs["L0_attn_Wv"], np.float32),
        Wq=np.ascontiguousarray(inputs["L0_attn_Wq"], np.float32),
        Wab=Wab,
        Wo=np.ascontiguousarray(inputs["L0_attn_Wo"], np.float32),
        qcw=np.ascontiguousarray(inputs["L0_attn_qcw"], np.float32),
        qcb=np.ascontiguousarray(inputs["L0_attn_qcb"], np.float32).reshape(D, 1),
        kcw=np.ascontiguousarray(inputs["L0_attn_kcw"], np.float32),
        kcb=np.ascontiguousarray(inputs["L0_attn_kcb"], np.float32).reshape(D, 1),
        vcw=np.ascontiguousarray(inputs["L0_attn_vcw"], np.float32),
        vcb=np.ascontiguousarray(inputs["L0_attn_vcb"], np.float32).reshape(NV, 1),
        WgT=np.ascontiguousarray(inputs["L0_ffn_w_gate"].T, np.float32),
        WuT=np.ascontiguousarray(inputs["L0_ffn_w_up"].T, np.float32),
        WdT=np.ascontiguousarray(inputs["L0_ffn_w_down"].T, np.float32),
        **_consts(),
    )
    ones_half = np.ones((HK, OWN), np.float32)
    zeros_half = np.zeros((HK, OWN), np.float32)
    cm_h0 = np.concatenate([zeros_half, ones_half], axis=1)
    cm_h1 = np.concatenate([ones_half, ones_half], axis=1)
    lncm_h0 = np.concatenate([np.full((HK, OWN), NEG, np.float32), zeros_half], axis=1)
    lncm_h1 = np.zeros((HK, T), np.float32)
    in_maps = []
    for c in range(8):
        b, hf = c // 2, c % 2
        if hf == 1:
            x_kvc = x[b]
            x_outc = x[b, OWN:]
            cm, lncm = cm_h1, lncm_h0 * 0
        else:
            x_kvc = np.concatenate([np.zeros((OWN, D), np.float32), x[b, :OWN]], axis=0)
            x_outc = x[b, :OWN]
            cm, lncm = cm_h0, lncm_h0
        in_maps.append(dict(shared,
                            x_kv=np.ascontiguousarray(x_kvc),
                            x_out=np.ascontiguousarray(x_outc),
                            cm16=cm, lncm16=lncm))
    return in_maps


_NC_CACHE = {}


def kernel(**inputs):
    key = (NPHASES, DEBUG_OUT)
    if key not in _NC_CACHE:
        _NC_CACHE[key] = build_program()
    nc = _NC_CACHE[key]
    in_maps = make_in_maps(inputs)
    res = run_bass_kernel_spmd(nc, in_maps, core_ids=list(range(8)))
    out = np.empty((B, T, D), np.float32)
    for c in range(8):
        b, hf = c // 2, c % 2
        out[b, hf * OWN:(hf + 1) * OWN] = res.results[c]["out"]
    kernel.last_results = res
    return out
